# version 01
# baseline (speedup 1.0000x reference)
import contextlib
import numpy as np
import concourse.bass as bass
import concourse.mybir as mybir
from concourse.bass_utils import run_bass_kernel_spmd

F32 = mybir.dt.float32
BF16 = mybir.dt.bfloat16
AF = mybir.ActivationFunctionType
ALU = mybir.AluOpType
AX = mybir.AxisListType.X

NCORE = 8
TOK = 1024
TCN = TOK // 128
DM = 4096
DCN = DM // 128
EPS = 1e-5

N_SUBLAYERS = 4
import os
DEBUG_STAGE = int(os.environ.get("KSTAGE", "99"))
KBSUB = int(os.environ.get("KBSUB", "3"))
KBGRP = int(os.environ.get("KBGRP", "8"))
KBK = int(os.environ.get("KBK", "8"))


class Sched:

    ENG = ("pe", "dve", "act", "pool", "sp")

    def __init__(self, nc, es):
        self.nc = nc
        self.es = es
        self.q = {e: [] for e in self.ENG}
        self.sems = {}
        self.cnt = {}
        self.last_wait = {e: {} for e in self.ENG}

    def sem(self, name):
        if name not in self.sems:
            self.sems[name] = self.es.enter_context(self.nc.semaphore(name))
            self.cnt[name] = 0
        return self.sems[name]

    def op(self, eng, fn, wait=(), inc=(), dma=False):
        wl = []
        for w in wait:
            name, val = w if isinstance(w, tuple) else (w, self.cnt[w])
            if val is None:
                val = self.cnt.get(name, 0)
            self.sem(name)
            if val > 0 and val > self.last_wait[eng].get(name, -1):
                self.last_wait[eng][name] = val
                wl.append((self.sems[name], val))
        il = []
        for i in inc:
            if isinstance(i, tuple):
                name, n = i
            else:
                name, n = i, (16 if dma else 1)
            self.sem(name)
            self.cnt[name] += n
            il.append((self.sems[name], n))

        def play(e, fn=fn, wl=wl, il=il):
            for s, v in wl:
                e.wait_ge(s, v)
            inst = fn(e) if fn is not None else None
            if il:
                assert inst is not None
                for s, n in il:
                    inst.then_inc(s, n)

        self.q[eng].append(play)

    def barrier(self):
        def dn(x):
            x.drain()
            return x.nop()
        for e in self.ENG:
            self.op(e, dn, inc=[("bar_" + e, 1)])
        snap = [(("bar_" + e), self.cnt["bar_" + e]) for e in self.ENG]
        for e in self.ENG:
            self.op(e, None, wait=snap)


def build_program():
    nc = bass.Bass("TRN2", target_bir_lowering=False, debug=False)

    def inp(name, shape, dt=F32):
        return nc.dram_tensor(name, shape, dt, kind="ExternalInput").ap()

    x_in = inp("x", [TOK, DM])
    w1 = inp("w1", [DM, DM])
    w2t = inp("w2t", [DM, 512])
    maskc = inp("maskc", [DM, 512])
    b2 = inp("b2", [1, DM])
    base_wt = inp("base_wt", [DM, DM])
    loraa_t = inp("loraa_t", [DM, 128])
    lorab_f = inp("lorab_f", [128, DM])
    routertp = inp("routertp", [DM, 128])
    wq = inp("wq", [DM, DM])
    wk = inp("wk", [DM, 1024])
    wv = inp("wv", [DM, 1024])
    wo = inp("wo", [DM, DM])
    identf = inp("identf", [128, 128])
    rotm = inp("rotm", [128, 128])
    cosq = inp("cosq", [128, TOK])
    sinq = inp("sinq", [128, TOK])
    cosk = inp("cosk", [128, TOK])
    sink = inp("sink", [128, TOK])
    trimask = inp("trimask", [128, 128])
    swmask = inp("swmask", [128, 256])
    swmask0 = inp("swmask0", [128, 256])
    gate = inp("gate", [128, 1])
    d0mask = inp("d0mask", [128, TOK])

    out = nc.dram_tensor("out", [TOK, DM], F32, kind="ExternalOutput").ap()
    dbg = nc.dram_tensor("dbg", [128, TCN * 128], F32, kind="ExternalOutput").ap()
    dbgb = nc.dram_tensor("dbgb", [128, 128], F32, kind="ExternalOutput").ap()

    attn_loc = nc.dram_tensor("attn_loc", [DM, 512], BF16)
    attn_all = nc.dram_tensor("attn_all", [NCORE * DM, 512], BF16)
    x_spill = nc.dram_tensor("x_spill", [128, TCN * DM], BF16)
    x2_dram = nc.dram_tensor("x2_dram", [128, TCN * DM], BF16)
    o_dram = nc.dram_tensor("o_dram", [8, 128, 4 * TOK], BF16)
    exch_loc = nc.dram_tensor("exch_loc", [8, 128, 256], F32)
    exch_pair = nc.dram_tensor("exch_pair", [8, 256, 256], F32)
    zdram = nc.dram_tensor("zdram", [8, 5, 1024], F32)

    es_root = contextlib.ExitStack()
    with es_root as es:
        s = Sched(nc, es)

        hT = es.enter_context(nc.sbuf_tensor("hT", [128, DCN * TOK], BF16, side="right"))
        id_bf = es.enter_context(nc.sbuf_tensor("id_bf", [128, 128], BF16, side="right"))
        ones_bf = es.enter_context(nc.sbuf_tensor("ones_bf", [128, 1], BF16, side="right"))
        ones_row = es.enter_context(nc.sbuf_tensor("ones_row", [1, 128], F32, side="right"))
        half_row = es.enter_context(nc.sbuf_tensor("half_row", [1, 128], F32, side="right"))
        stat = es.enter_context(nc.sbuf_tensor("stat", [128, 48], F32, side="right"))
        eps_sb = es.enter_context(nc.sbuf_tensor("eps_sb", [128, 1], F32, side="right"))

        s1 = contextlib.ExitStack()
        x_sb = s1.enter_context(nc.sbuf_tensor("x_sb", [128, TCN * DM], BF16))

        s.op("pool", lambda e: e.dma_start(
            out=x_sb.ap().rearrange("p (c f) -> p c f", c=TCN),
            in_=x_in.rearrange("(c p) f -> p c f", p=128)),
            inc=["ld_x"], dma=True)
        s.op("pool", lambda e: e.dma_start(out=id_bf[:], in_=identf),
             inc=["ld_const"], dma=True)
        s.op("dve", lambda e: e.memset(ones_bf[:], 1.0), inc=["const_c"])
        s.op("dve", lambda e: e.memset(eps_sb[:], EPS), inc=["const_c"])
        s.op("dve", lambda e: e.memset(ones_row[:], 1.0), inc=["const_c"])
        s.op("dve", lambda e: e.memset(half_row[:], 0.5), inc=["const_c"])

        ln_id = [0]

        def phase_ln(xt, hidden_scratch, ps_tr, sq_scratch):
            tag = "ln"
            ln_id[0] += 1
            b_rd = s.cnt.get("rd_" + tag, 0)
            b_act = s.cnt.get("act_" + tag, 0)
            b_pe = s.cnt.get("pe_" + tag, 0)
            for tc in range(TCN):
                xc = xt[:, tc * DM:(tc + 1) * DM]
                sq_i = sq_scratch[tc % 2]
                s.op("act", lambda e, xc=xc, sq=sq_i: e.activation(
                    sq[:], xc, AF.Square),
                    wait=[("rd_" + tag, b_rd + max(0, tc - 1))], inc=["act_" + tag])
                s.op("dve", lambda e, xc=xc, tc=tc: e.reduce_sum(
                    stat[:, 0 + tc:1 + tc], xc, axis=AX),
                    inc=["dve_" + tag])
                s.op("dve", lambda e, sq=sq_i, tc=tc: e.reduce_sum(
                    stat[:, 8 + tc:9 + tc], sq[:], axis=AX),
                    wait=[("act_" + tag, b_act + tc + 1)], inc=["rd_" + tag])
            s.op("dve", lambda e: e.tensor_scalar(
                stat[:, 32:40], stat[:, 8:16], 1.0 / DM, None, op0=ALU.mult),
                wait=[("rd_" + tag, b_rd + TCN)], inc=["dve_" + tag])
            s.op("dve", lambda e: e.tensor_scalar(
                stat[:, 16:24], stat[:, 0:8], 1.0 / DM, None, op0=ALU.mult),
                wait=[("dve_" + tag, None)],
                inc=["dve_" + tag])
            s.op("dve", lambda e: e.tensor_tensor(
                stat[:, 40:48], stat[:, 16:24], stat[:, 16:24], op=ALU.mult),
                wait=[("dve_" + tag, None)], inc=["dve_" + tag])
            s.op("dve", lambda e: e.tensor_tensor(
                stat[:, 32:40], stat[:, 32:40], stat[:, 40:48], op=ALU.subtract),
                wait=[("dve_" + tag, None)], inc=["dve_" + tag])
            s.op("act", lambda e: e.activation(
                stat[:, 40:48], stat[:, 32:40], AF.Sqrt, bias=eps_sb[:, 0:1]),
                wait=[("dve_" + tag, None), ("const_c", None)],
                inc=["act_" + tag])
            s.op("dve", lambda e: e.reciprocal(stat[:, 24:32], stat[:, 40:48]),
                 wait=[("act_" + tag, None)], inc=["dve_" + tag])
            recip_mark = s.cnt["dve_" + tag]
            for tc in range(TCN):
                xc = xt[:, tc * DM:(tc + 1) * DM]
                ht_i = hidden_scratch[tc % 2]
                s.op("dve", lambda e, xc=xc, ht=ht_i, tc=tc: e.tensor_scalar(
                    ht[:], xc, stat[:, 16 + tc:17 + tc], stat[:, 24 + tc:25 + tc],
                    op0=ALU.subtract, op1=ALU.mult),
                    wait=[("dve_" + tag, recip_mark),
                          ("pe_" + tag, b_pe + max(0, 8 * (tc - 1)))],
                    inc=["dve_" + tag])
                hm = s.cnt["dve_" + tag]
                for dg in range(DCN // 4):
                    ps = ps_tr[dg % 2]
                    for j in range(4):
                        dc = dg * 4 + j
                        s.op("pe", lambda e, ht=ht_i, dc=dc, j=j, ps=ps: e.transpose(
                            ps[:, j * 128:(j + 1) * 128],
                            ht[:, dc * 128:(dc + 1) * 128], id_bf[:]),
                            wait=[("ld_const", 16), ("dve_" + tag, hm),
                                  ("ev_" + tag, max(0, s.cnt.get("ev_" + tag, 0) - 1))],
                            inc=(["pe_" + tag] if j == 3 else []))
                    pm = s.cnt["pe_" + tag]
                    dst = hT.ap().rearrange("p (d t) -> p d t", d=DCN)[
                        :, dg * 4:(dg + 1) * 4, tc * 128:(tc + 1) * 128]
                    src = ps.ap().rearrange("p (a b) -> p a b", a=4)
                    eng = "dve" if (dg % 2 == 0) else "act"
                    if eng == "dve":
                        s.op("dve", lambda e, dst=dst, src=src: e.tensor_copy(dst, src),
                             wait=[("pe_" + tag, pm)], inc=["ev_" + tag])
                    else:
                        s.op("act", lambda e, dst=dst, src=src: e.activation(
                            dst, src, AF.Copy),
                            wait=[("pe_" + tag, pm)], inc=["ev_" + tag])
            s.barrier()

        if DEBUG_STAGE >= 1:
          with contextlib.ExitStack() as pes:
            hid = [pes.enter_context(nc.sbuf_tensor(f"hidA{i}", [128, DM], BF16))
                   for i in range(2)]
            sqs = [pes.enter_context(nc.sbuf_tensor(f"sqA{i}", [128, DM], F32))
                   for i in range(2)]
            ps_tr = [pes.enter_context(nc.psum_tensor(f"psTrA{i}", [128, 512], BF16))
                     for i in range(2)]
            s.op("dve", None, wait=[("ld_x", 16)])
            s.op("act", None, wait=[("ld_x", 16)])
            phase_ln(x_sb, hid, ps_tr, sqs)

        if DEBUG_STAGE >= 2:
          with contextlib.ExitStack() as pes:
            w2t_sb = pes.enter_context(nc.sbuf_tensor("w2t_sb", [128, DCN * 512], BF16))
            w1b = [pes.enter_context(nc.sbuf_tensor(f"w1b{i}", [128, DCN * 256], BF16))
                   for i in range(2)]
            mkb = [pes.enter_context(nc.sbuf_tensor(f"mkb{i}", [128, 512], F32))
                   for i in range(2)]
            atile = [pes.enter_context(nc.sbuf_tensor(f"atile{i}", [128, 512], BF16))
                     for i in range(2)]
            ps_a = [pes.enter_context(nc.psum_tensor(f"psA{i}", [128, 512], F32))
                    for i in range(2)]

            w2tv = w2t.rearrange("(c p) t -> p c t", p=128)
            w2tsv = w2t_sb.ap().rearrange("p (c t) -> p c t", c=DCN)
            for h in range(4):
                s.op("pool", lambda e, h=h: e.dma_start(
                    out=w2tsv[:, h * 8:(h + 1) * 8, :],
                    in_=w2tv[:, h * 8:(h + 1) * 8, :]),
                    inc=["ld_w2t"], dma=True)

            w1r = w1.rearrange("(c p) sx -> p c sx", p=128)
            alv = attn_loc.ap().rearrange("(c p) t -> c p t", p=128)
            for scb in range(16):
                buf = w1b[scb % 2]
                bv = buf.ap().rearrange("p (c t) -> p c t", c=DCN)
                for h in range(4):
                    s.op("pool", lambda e, bv=bv, scb=scb, h=h: e.dma_start(
                        out=bv[:, h * 8:(h + 1) * 8, :],
                        in_=w1r[:, h * 8:(h + 1) * 8, scb * 256:(scb + 1) * 256]),
                        wait=([("pe_aw", 2 * (scb - 1))] if (scb >= 2 and h == 0) else []),
                        inc=["ld_w1"], dma=True)
                ldm = s.cnt["ld_w1"]
                for sc2 in range(2):
                    sc = scb * 2 + sc2
                    ps = ps_a[sc % 2]
                    mb = mkb[sc % 2]
                    at = atile[sc % 2]
                    s.op("sp", lambda e, mb=mb, sc=sc: e.dma_start(
                        out=mb[:], in_=maskc[sc * 128:(sc + 1) * 128, :]),
                        wait=[("ev_attn", max(0, sc - 1))],
                        inc=["ld_mk"], dma=True)
                    mkm = s.cnt["ld_mk"]
                    bufv = buf.ap().rearrange("p (c t) -> p c t", c=DCN)
                    for dc in range(DCN):
                        s.op("pe", lambda e, bufv=bufv, dc=dc, sc2=sc2, ps=ps: e.matmul(
                            ps[:], bufv[:, dc, sc2 * 128:(sc2 + 1) * 128],
                            w2t_sb[:, dc * 512:(dc + 1) * 512],
                            start=(dc == 0), stop=(dc == DCN - 1)),
                            wait=([("ld_w1", ldm), ("ld_w2t", 64),
                                   ("ev_attn", max(0, s.cnt.get("ev_attn", 0) - 1))]
                                  if dc == 0 else []),
                            inc=(["pe_aw"] if dc == DCN - 1 else []))
                    pem = s.cnt["pe_aw"]
                    s.op("dve", lambda e, ps=ps, mb=mb, at=at: e.tensor_tensor(
                        at[:], ps[:], mb[:], op=ALU.mult),
                        wait=[("pe_aw", pem), ("ld_mk", mkm),
                              ("st_attn", max(0, 16 * (sc - 1)))],
                        inc=["ev_attn"])
                    evm = s.cnt["ev_attn"]
                    s.op("sp", lambda e, at=at, sc=sc: e.dma_start(
                        out=alv[sc], in_=at[:]),
                        wait=[("ev_attn", evm)], inc=["st_attn"], dma=True)
            s.op("pool", lambda e: e.collective_compute(
                "AllGather", ALU.bypass,
                replica_groups=[list(range(NCORE))],
                ins=[attn_loc.ap()], outs=[attn_all.ap()]),
                wait=[("st_attn", 16 * 32)], inc=[("cc", 1)])
            s.barrier()

        if DEBUG_STAGE >= 3:
          with contextlib.ExitStack() as pes:
            abuf = [pes.enter_context(nc.sbuf_tensor(f"ab{i}", [128, DCN * 512], BF16))
                    for i in range(2)]
            ps_y = [pes.enter_context(nc.psum_tensor(f"psY{i}", [128, 512], F32))
                    for i in range(2)]
            b2b = [pes.enter_context(nc.sbuf_tensor(f"b2b{i}", [128, 512], F32))
                   for i in range(2)]
            tmp1 = [pes.enter_context(nc.sbuf_tensor(f"tmpA{i}", [128, 512], F32))
                    for i in range(2)]
            tmp2 = [pes.enter_context(nc.sbuf_tensor(f"tmpB{i}", [128, 512], F32))
                    for i in range(2)]
            aav = attn_all.ap().rearrange("(r c p) t -> r p c t", r=NCORE, p=128)
            for r in range(NCORE):
                buf = abuf[r % 2]
                bufav = buf.ap().rearrange("p (c t) -> p c t", c=DCN)
                for h in range(4):
                    s.op("sp", lambda e, bufav=bufav, r=r, h=h: e.dma_start(
                        out=bufav[:, h * 8:(h + 1) * 8, :],
                        in_=aav[r][:, h * 8:(h + 1) * 8, :]),
                        wait=[("cc", 1)] + ([("pe_y", (r - 1) * TCN)]
                                            if (r >= 2 and h == 0) else []),
                        inc=["ld_at"], dma=True)
                ldm = s.cnt["ld_at"]
                bb = b2b[r % 2]
                s.op("sp", lambda e, bb=bb, r=r: e.dma_start(
                    out=bb[:],
                    in_=b2[0:1, r * 512:(r + 1) * 512].partition_broadcast(128)),
                    wait=[("ep_y", max(0, (r - 1) * TCN))],
                    inc=["b2r"], dma=True)
                b2m = s.cnt["b2r"]
                bufv = buf.ap().rearrange("p (c t) -> p c t", c=DCN)
                for tc in range(TCN):
                    ps = ps_y[tc % 2]
                    for dc in range(DCN):
                        s.op("pe", lambda e, bufv=bufv, dc=dc, tc=tc, ps=ps: e.matmul(
                            ps[:], hT[:, dc * TOK + tc * 128: dc * TOK + (tc + 1) * 128],
                            bufv[:, dc, :],
                            start=(dc == 0), stop=(dc == DCN - 1)),
                            wait=([("ld_at", ldm),
                                   ("ep_y", max(0, s.cnt.get("ep_y", 0) - 1))]
                                  if dc == 0 else []),
                            inc=(["pe_y"] if dc == DCN - 1 else []))
                    pem = s.cnt["pe_y"]
                    t1, t2 = tmp1[tc % 2], tmp2[tc % 2]
                    s.op("dve", lambda e, ps=ps, t1=t1, bb=bb: e.tensor_tensor(
                        t1[:], ps[:], bb[:], op=ALU.add),
                        wait=[("pe_y", pem), ("b2r", b2m),
                              ("ge_y", max(0, s.cnt.get("ge_y", 0) - 1))],
                        inc=["ep_y"])
                    epm = s.cnt["ep_y"]
                    s.op("act", lambda e, t1=t1, t2=t2: e.activation(
                        t2[:], t1[:], AF.Gelu),
                        wait=[("ep_y", epm),
                              ("rs_y", max(0, s.cnt.get("rs_y", 0) - 1))],
                        inc=["ge_y"])
                    gem = s.cnt["ge_y"]
                    xsl = x_sb[:, tc * DM + r * 512: tc * DM + (r + 1) * 512]
                    s.op("dve", lambda e, xsl=xsl, t2=t2: e.tensor_tensor(
                        xsl, xsl, t2[:], op=ALU.add),
                        wait=[("ge_y", gem)], inc=["rs_y"])
            s.barrier()

        def phase_ffn(fi, xt):
            t = "f_"
            tn = f"f{fi}_"
            b_rt = s.cnt.get(t + "rt", 0)
            b_la = s.cnt.get(t + "la", 0)
            b_lb = s.cnt.get(t + "lb", 0)
            b_lgev = s.cnt.get("lgev" + t, 0)
            b_evmt = s.cnt.get("ev_mt" + t, 0)
            b_peb = s.cnt.get("pe_b" + t, 0)
            with contextlib.ExitStack() as pes:
                hid = [pes.enter_context(
                    nc.sbuf_tensor(f"hid{tn}{i}", [128, DM], BF16)) for i in range(2)]
                sqs = [pes.enter_context(
                    nc.sbuf_tensor(f"sq{tn}{i}", [128, DM], F32)) for i in range(2)]
                ps_tr = [pes.enter_context(
                    nc.psum_tensor(f"psTr{tn}{i}", [128, 512], BF16)) for i in range(2)]
                phase_ln(xt, hid, ps_tr, sqs)
            if DEBUG_STAGE == 4 and fi == 0:
                s.op("pool", lambda e: e.dma_start(
                    out=dbg, in_=hT[:, 0:TCN * 128]),
                    inc=["st_dbg"], dma=True)
                s.op("pool", lambda e: e.dma_start(
                    out=dbgb[:, 0:48], in_=stat[:]),
                    inc=["st_dbg"], dma=True)
            with contextlib.ExitStack() as pes:
                la_sb = pes.enter_context(
                    nc.sbuf_tensor(tn + "la", [128, DCN * 128], BF16))
                lb_sb = pes.enter_context(
                    nc.sbuf_tensor(tn + "lb", [128, DM], BF16))
                rt_sb = pes.enter_context(
                    nc.sbuf_tensor(tn + "rt", [128, DCN * 128], BF16))
                midT = pes.enter_context(
                    nc.sbuf_tensor(tn + "midT", [128, TCN * 128], BF16))
                lg = pes.enter_context(nc.sbuf_tensor(tn + "lg", [128, TCN * 8], F32))
                gt = pes.enter_context(nc.sbuf_tensor(tn + "gt", [128, 8], F32))
                e1b = pes.enter_context(nc.sbuf_tensor(tn + "e1", [128, 8], F32))
                e2b = pes.enter_context(nc.sbuf_tensor(tn + "e2", [128, 8], F32))
                tmb = pes.enter_context(nc.sbuf_tensor(tn + "tm", [128, 8], F32))
                wde = pes.enter_context(nc.sbuf_tensor(tn + "wd", [128, TCN * 8], F32))
                mg = [pes.enter_context(
                    nc.sbuf_tensor(f"{tn}mg{i}", [128, 128], BF16)) for i in range(2)]
                bb = [pes.enter_context(
                    nc.sbuf_tensor(f"{tn}bb{i}", [128, DCN * 256], BF16))
                    for i in range(2)]
                ps_log = pes.enter_context(nc.psum_tensor(tn + "psL", [128, 64], F32))
                ps_mid = [pes.enter_context(
                    nc.psum_tensor(f"{tn}psM{i}", [128, 128], F32)) for i in range(2)]
                ps_mT = [pes.enter_context(
                    nc.psum_tensor(f"{tn}psMT{i}", [128, 128], BF16)) for i in range(2)]
                ps_b = [pes.enter_context(
                    nc.psum_tensor(f"{tn}psB{i}", [128, 256], F32)) for i in range(2)]

                LA, LB, RT, G = t + "la", t + "lb", t + "rt", t + "gg"
                lav = loraa_t.rearrange("(c p) r -> p c r", p=128)
                lasv = la_sb.ap().rearrange("p (c r) -> p c r", c=DCN)
                rtv = routertp.rearrange("(c p) r -> p c r", p=128)
                rtsv = rt_sb.ap().rearrange("p (c r) -> p c r", c=DCN)
                for h in range(4):
                    s.op("pool", lambda e, h=h: e.dma_start(
                        out=lasv[:, h * 8:(h + 1) * 8, :],
                        in_=lav[:, h * 8:(h + 1) * 8, :]), inc=[LA], dma=True)
                    s.op("pool", lambda e, h=h: e.dma_start(
                        out=rtsv[:, h * 8:(h + 1) * 8, :],
                        in_=rtv[:, h * 8:(h + 1) * 8, :]), inc=[RT], dma=True)
                s.op("pool", lambda e: e.dma_start(out=lb_sb[:], in_=lorab_f),
                     inc=[LB], dma=True)

                for tc in range(TCN if DEBUG_STAGE >= 5 else 0):
                    pslg = ps_log[:, tc * 8:(tc + 1) * 8]
                    for dc in range(DCN):
                        s.op("pe", lambda e, dc=dc, tc=tc, pslg=pslg: e.matmul(
                            pslg, hT[:, dc * TOK + tc * 128: dc * TOK + (tc + 1) * 128],
                            rt_sb[:, dc * 128: dc * 128 + 8],
                            start=(dc == 0), stop=(dc == DCN - 1)),
                            wait=([(RT, b_rt + 64), ("lgev" + t, b_lgev + tc)]
                                  if dc == 0 else []),
                            inc=(["pe_lg" + t] if dc == DCN - 1 else []))
                    lgm = s.cnt["pe_lg" + t]
                    psm = ps_mid[tc % 2]
                    for dc in range(DCN):
                        s.op("pe", lambda e, dc=dc, tc=tc, psm=psm: e.matmul(
                            psm[:], hT[:, dc * TOK + tc * 128: dc * TOK + (tc + 1) * 128],
                            la_sb[:, dc * 128:(dc + 1) * 128],
                            start=(dc == 0), stop=(dc == DCN - 1)),
                            wait=([(LA, b_la + 64),
                                   ("em_" + t, max(0, s.cnt.get("em_" + t, 0) - 1))]
                                  if dc == 0 else []),
                            inc=(["pe_md" + t] if dc == DCN - 1 else []))
                    mdm = s.cnt["pe_md" + t]

                    def g(eng, fn):
                        s.op(eng, fn, wait=[(G, s.cnt.get(G, 0))], inc=[G])
                    lgc = lg[:, tc * 8:(tc + 1) * 8]
                    s.op("dve", lambda e, pslg=pslg, lgc=lgc: e.tensor_copy(lgc, pslg),
                         wait=[("pe_lg" + t, lgm), (G, s.cnt.get(G, 0))],
                         inc=["lgev" + t])
                    s.op("dve", lambda e, lgc=lgc: e.reduce_max(
                        gt[:, 0:1], lgc, axis=AX),
                        wait=[("lgev" + t, None)], inc=[G])
                    g("dve", lambda e, lgc=lgc: e.tensor_scalar(
                        e1b[:], lgc, gt[:, 0:1], None, op0=ALU.is_equal))
                    g("dve", lambda e, lgc=lgc: e.scalar_tensor_tensor(
                        tmb[:], e1b[:], -1e30, lgc, op0=ALU.mult, op1=ALU.add))
                    g("dve", lambda e: e.reduce_max(gt[:, 1:2], tmb[:], axis=AX))
                    g("dve", lambda e: e.tensor_scalar(
                        e2b[:], tmb[:], gt[:, 1:2], None, op0=ALU.is_equal))
                    g("dve", lambda e: e.tensor_tensor(
                        gt[:, 4:5], gt[:, 1:2], gt[:, 0:1], op=ALU.subtract))
                    g("act", lambda e: e.activation(gt[:, 3:4], gt[:, 4:5], AF.Sigmoid))
                    g("dve", lambda e: e.tensor_scalar(
                        gt[:, 2:3], gt[:, 3:4], -2.0, 2.0, op0=ALU.mult, op1=ALU.add))
                    g("dve", lambda e: e.tensor_scalar(
                        gt[:, 3:4], gt[:, 3:4], 2.0, None, op0=ALU.mult))
                    g("dve", lambda e: e.tensor_scalar(
                        e1b[:], e1b[:], gt[:, 2:3], None, op0=ALU.mult))
                    g("dve", lambda e: e.tensor_scalar(
                        e2b[:], e2b[:], gt[:, 3:4], None, op0=ALU.mult))
                    wdc = wde[:, tc * 8:(tc + 1) * 8]
                    g("dve", lambda e: e.tensor_tensor(
                        e1b[:], e1b[:], e2b[:], op=ALU.add))
                    g("dve", lambda e, wdc=wdc: e.tensor_copy(wdc, e1b[:]))
                    gl = s.cnt[G]

                    mgt = mg[tc % 2]
                    wrep = wde.ap()[:, tc * 8:(tc + 1) * 8].unsqueeze(2) \
                        .broadcast_to([128, 8, 16])
                    s.op("dve", lambda e, psm=psm, mgt=mgt, wrep=wrep: e.tensor_tensor(
                        mgt.ap().rearrange("p (e r) -> p e r", e=8),
                        psm.ap().rearrange("p (e r) -> p e r", e=8),
                        wrep, op=ALU.mult),
                        wait=[("pe_md" + t, mdm), (G, gl),
                              ("pe_mt" + t, max(0, s.cnt.get("pe_mt" + t, 0) - 1))],
                        inc=["em_" + t])
                    emm = s.cnt["em_" + t]
                    psl = ps_mT[tc % 2][:]
                    s.op("pe", lambda e, mgt=mgt, psl=psl: e.transpose(
                        psl, mgt[:], id_bf[:]),
                        wait=[("em_" + t, emm),
                              ("ev_mt" + t, max(0, s.cnt.get("ev_mt" + t, 0) - 1))],
                        inc=["pe_mt" + t])
                    ptm = s.cnt["pe_mt" + t]
                    s.op("act", lambda e, tc=tc, psl=psl: e.activation(
                        midT[:, tc * 128:(tc + 1) * 128], psl, AF.Copy),
                        wait=[("pe_mt" + t, ptm)], inc=["ev_mt" + t])

                if DEBUG_STAGE == 5 and fi == 0:
                    s.op("pool", lambda e: e.dma_start(out=dbg, in_=midT[:]),
                         wait=[("ev_mt" + t, TCN)], inc=["st_dbg"], dma=True)
                    s.op("pool", lambda e: e.dma_start(
                        out=dbgb[:, 0:64], in_=lg[:]),
                        wait=[("lgev" + t, TCN)], inc=["st_dbg"], dma=True)
                    s.op("pool", lambda e: e.dma_start(
                        out=dbgb[:, 64:128], in_=wde[:]),
                        wait=[(G, None)], inc=["st_dbg"], dma=True)

                bwv = base_wt.rearrange("(c p) o -> p c o", p=128)
                for rb in range(16 if DEBUG_STAGE >= 6 else 0):
                    buf = bb[rb % 2]
                    bv = buf.ap().rearrange("p (c o) -> p c o", c=DCN)
                    for h in range(4):
                        s.op("pool", lambda e, bv=bv, rb=rb, h=h: e.dma_start(
                            out=bv[:, h * 8:(h + 1) * 8, :],
                            in_=bwv[:, h * 8:(h + 1) * 8, rb * 256:(rb + 1) * 256]),
                            wait=([("pe_b" + t, b_peb + 8 * (rb - 1))]
                                  if (rb >= 2 and h == 0) else []),
                            inc=["ld_b" + t], dma=True)
                    ldm = s.cnt["ld_b" + t]
                    use_inc = os.environ.get("KNOINC", "0") != "1"
                    for tc in range(TCN):
                        ps = ps_b[tc % 2]
                        for dc in range(DCN):
                            s.op("pe", lambda e, bv=bv, dc=dc, tc=tc, ps=ps: e.matmul(
                                ps[:], hT[:, dc * TOK + tc * 128: dc * TOK + (tc + 1) * 128],
                                bv[:, dc, :],
                                start=(dc == 0),
                                stop=(False if use_inc else (dc == DCN - 1))),
                                wait=([("ld_b" + t, ldm), (LB, b_lb + 16),
                                       ("ev_mt" + t,
                                        (b_evmt + TCN) if DEBUG_STAGE >= 5 else 0),
                                       ("rs_b" + t,
                                        max(0, s.cnt.get("rs_b" + t, 0) - 1))]
                                      if dc == 0 else []))
                        if use_inc:
                            s.op("pe", lambda e, tc=tc, rb=rb, ps=ps: e.matmul(
                                ps[:], midT[:, tc * 128:(tc + 1) * 128],
                                lb_sb[:, rb * 256:(rb + 1) * 256],
                                start=False, stop=True),
                                inc=["pe_b" + t])
                        else:
                            s.op("pe", lambda e: (e.drain(), e.nop())[1],
                                 inc=["pe_b" + t])
                        pbm = s.cnt["pe_b" + t]
                        xsl = xt[:, tc * DM + rb * 256: tc * DM + (rb + 1) * 256]
                        s.op("dve", lambda e, xsl=xsl, ps=ps: e.tensor_tensor(
                            xsl, xsl, ps[:], op=ALU.add),
                            wait=[("pe_b" + t, pbm)], inc=["rs_b" + t])
                s.barrier()

        if DEBUG_STAGE >= 4 and N_SUBLAYERS >= 2:
            phase_ffn(0, x_sb)

        def phase_attn_b_groups():
          with contextlib.ExitStack() as pes:
            T = pes.enter_context
            wqb = [T(nc.sbuf_tensor(f"b_wq{i}", [128, DCN * 128], BF16))
                   for i in range(2)]
            wkb = T(nc.sbuf_tensor("b_wk", [128, DCN * 128], BF16))
            wvb = T(nc.sbuf_tensor("b_wv", [128, DCN * 128], BF16))
            cq = T(nc.sbuf_tensor("b_cq", [128, TOK], BF16))
            sqt = T(nc.sbuf_tensor("b_sqt", [128, TOK], BF16))
            ck = T(nc.sbuf_tensor("b_ck", [128, TOK], BF16))
            skt = T(nc.sbuf_tensor("b_skt", [128, TOK], BF16))
            rot_bf = T(nc.sbuf_tensor("b_rot", [128, 128], BF16))
            swm_sb = T(nc.sbuf_tensor("b_swm", [128, 256], F32))
            swm0_sb = T(nc.sbuf_tensor("b_swm0", [128, 256], F32))
            tri_sb = T(nc.sbuf_tensor("b_tri", [128, 128], F32))
            gate_sb = T(nc.sbuf_tensor("b_gate", [128, 1], F32))
            ones128 = T(nc.sbuf_tensor("b_ones", [128, 128], F32))
            qx = T(nc.sbuf_tensor("b_qx", [128, 4 * TOK], BF16))
            qraw = T(nc.sbuf_tensor("b_qraw", [128, 4 * TOK], BF16))
            sq_sb = T(nc.sbuf_tensor("b_sq", [128, 4 * TOK], BF16))
            qtl = qx
            kraw = T(nc.sbuf_tensor("b_kraw", [128, TOK], BF16))
            kx = T(nc.sbuf_tensor("b_kx", [128, TOK], BF16))
            ktl = T(nc.sbuf_tensor("b_ktl", [128, TOK], BF16))
            khat = T(nc.sbuf_tensor("b_khat", [128, TOK], BF16))
            khtok = T(nc.sbuf_tensor("b_khtok", [128, TOK], BF16))
            sk_ext = T(nc.sbuf_tensor("b_skext", [128, 1152], BF16))
            v_ext = T(nc.sbuf_tensor("b_vext", [128, 1152], BF16))
            v_tok = T(nc.sbuf_tensor("b_vtok", [128, 8 * 128], BF16))
            v_win = T(nc.sbuf_tensor("b_vwin", [128, 9 * 128], BF16))
            g_sb = T(nc.sbuf_tensor("b_g", [128, TOK], F32))
            a_sb = T(nc.sbuf_tensor("b_a", [128, TOK], F32))
            eA = T(nc.sbuf_tensor("b_eA", [128, TOK], F32))
            eNA = T(nc.sbuf_tensor("b_eNA", [128, TOK], F32))
            tmpA = T(nc.sbuf_tensor("b_tmpA", [128, 512], F32))
            tmpB = T(nc.sbuf_tensor("b_tmpB", [128, 512], F32))
            tmpC = T(nc.sbuf_tensor("b_tmpC", [128, TOK], F32))
            zrow = T(nc.sbuf_tensor("b_zrow", [1, 1024], F32))
            zrow2 = T(nc.sbuf_tensor("b_zrow2", [1, 1024], F32))
            zbK = T(nc.sbuf_tensor("b_zbK", [128, 1024], F32))
            zbQ = zbK
            S_state = T(nc.sbuf_tensor("b_S", [128, 128], F32))
            S0st = T(nc.sbuf_tensor("b_S0", [128, 8 * 128], BF16))
            sprev = T(nc.sbuf_tensor("b_sprev", [128, 128], F32))
            exch_sb = T(nc.sbuf_tensor("b_exch", [128, 256], F32))
            aoff = T(nc.sbuf_tensor("b_aoff", [128, 16], F32))
            o_sb = T(nc.sbuf_tensor("b_o", [128, 4 * TOK], BF16))
            ePf = T(nc.sbuf_tensor("b_ePf", [128, 256], F32))
            ePm = T(nc.sbuf_tensor("b_ePm", [128, 256], F32))
            ePb = T(nc.sbuf_tensor("b_ePb", [128, 256], BF16))
            zsw = T(nc.sbuf_tensor("b_zsw", [128, 4], F32))
            PTb = [T(nc.sbuf_tensor(f"b_PT{i}", [128, 128], BF16)) for i in range(2)]
            ePT = [T(nc.sbuf_tensor(f"b_ePT{i}", [128, 256], BF16)) for i in range(2)]
            p0 = T(nc.psum_tensor("bp0", [128, 512], F32))
            p1 = T(nc.psum_tensor("bp1", [128, 512], F32))
            p2 = T(nc.psum_tensor("bp2", [128, 512], F32))
            p3 = T(nc.psum_tensor("bp3", [128, 512], F32))
            p4 = T(nc.psum_tensor("bp4", [128, 512], F32))
            p5 = T(nc.psum_tensor("bp5", [128, 256], F32))
            p6 = T(nc.psum_tensor("bp6", [128, 256], F32))
            p7 = T(nc.psum_tensor("bp7", [128, 512], BF16))

            for nm, dst, src in (("cq", cq, cosq), ("sqt", sqt, sinq),
                                 ("ck", ck, cosk), ("skt", skt, sink),
                                 ("rotb", rot_bf, rotm)):
                s.op("pool", lambda e, dst=dst, src=src: e.dma_start(
                    out=dst[:], in_=src), inc=["b_cst"], dma=True)
            for dst, src in ((swm_sb, swmask), (swm0_sb, swmask0),
                             (tri_sb, trimask), (gate_sb, gate)):
                s.op("sp", lambda e, dst=dst, src=src: e.dma_start(
                    out=dst[:], in_=src), inc=["b_cst"], dma=True)
            s.op("dve", lambda e: e.memset(ones128[:], 1.0), inc=["b_cst1"])
            CST = [("b_cst", 9 * 16), ("b_cst1", 1), ("const_c", None)]

            wqv = wq.rearrange("(c p) o -> p c o", p=128)
            wkv = wk.rearrange("(c p) o -> p c o", p=128)
            wvv = wv.rearrange("(c p) o -> p c o", p=128)

            for gidx in range(KBGRP):
                tg = "b"
                oev_pg = s.cnt.get("b_oev", 0)
                swe_pg = s.cnt.get("b_swe", 0)
                for h4 in range(4):
                    s.op("pool", lambda e, h4=h4, gidx=gidx: e.dma_start(
                        out=wkb.ap().rearrange("p (c o) -> p c o", c=DCN)[
                            :, h4 * 8:(h4 + 1) * 8, :],
                        in_=wkv[:, h4 * 8:(h4 + 1) * 8,
                                gidx * 128:(gidx + 1) * 128]),
                        wait=[("b_kp", max(0, s.cnt.get("b_kp", 0)))] if h4 == 0 else [],
                        inc=["b_ldk"], dma=True)
                    s.op("pool", lambda e, h4=h4, gidx=gidx: e.dma_start(
                        out=wvb.ap().rearrange("p (c o) -> p c o", c=DCN)[
                            :, h4 * 8:(h4 + 1) * 8, :],
                        in_=wvv[:, h4 * 8:(h4 + 1) * 8,
                                gidx * 128:(gidx + 1) * 128]),
                        wait=[("b_vp", max(0, s.cnt.get("b_vp", 0)))] if h4 == 0 else [],
                        inc=["b_ldv"], dma=True)
                ldk = s.cnt["b_ldk"]
                ldv = s.cnt["b_ldv"]
                s.op("dve", lambda e: e.memset(sk_ext[:, 1088:1152], 0.0),
                     wait=[("b_sc", s.cnt.get("b_sc", 0)),
                           ("b_tr", s.cnt.get("b_tr", 0)),
                           ("b_ex", s.cnt.get("b_ex", 0))], inc=["b_z0"])
                s.op("dve", lambda e: e.memset(v_ext[:, 1088:1152], 0.0),
                     inc=["b_z0"])
                for tt in range(2 if KBK >= 1 else 0):
                    ps = [p0, p1][tt]
                    for dc in range(DCN):
                        s.op("pe", lambda e, dc=dc, tt=tt, ps=ps: e.matmul(
                            ps[:], wkb[:, dc * 128:(dc + 1) * 128],
                            hT[:, dc * TOK + tt * 512: dc * TOK + tt * 512 + 512],
                            start=(dc == 0), stop=(dc == DCN - 1)),
                            wait=([("b_ldk", ldk), ("b_oev", oev_pg)]
                                  if dc == 0 else []),
                            inc=(["b_kp"] if dc == DCN - 1 else []))
                    kpm = s.cnt["b_kp"]
                    if os.environ.get("KNOEV", "0") == "1":
                        continue
                    s.op("dve", lambda e, tt=tt, ps=ps: e.tensor_copy(
                        kraw[:, tt * 512:(tt + 1) * 512], ps[:]),
                        wait=[("b_kp", kpm), ("b_g", s.cnt.get("b_g", 0))],
                        inc=["b_kev"])
                    s.op("act", lambda e, tt=tt: e.activation(
                        kx[:, tt * 512:(tt + 1) * 512],
                        kraw[:, tt * 512:(tt + 1) * 512], AF.Exp),
                        wait=[("b_kev", s.cnt.get("b_kev", 0)),
                              ("b_kt", s.cnt.get("b_kt", 0))],
                        inc=["b_kev"])
                for tt in range(2 if KBK >= 1 else 0):
                    ps = [p0, p1][tt]
                    for dc in range(DCN):
                        s.op("pe", lambda e, dc=dc, tt=tt, ps=ps: e.matmul(
                            ps[:], wvb[:, dc * 128:(dc + 1) * 128],
                            hT[:, dc * TOK + tt * 512: dc * TOK + tt * 512 + 512],
                            start=(dc == 0), stop=(dc == DCN - 1)),
                            wait=([("b_ldv", ldv), ("b_kev", s.cnt.get("b_kev", 0))]
                                  if dc == 0 else []),
                            inc=(["b_vp"] if dc == DCN - 1 else []))
                    vpm = s.cnt["b_vp"]
                    if os.environ.get("KNOEV", "0") == "1":
                        continue
                    s.op("dve", lambda e, tt=tt, ps=ps: e.tensor_copy(
                        v_ext[:, 64 + tt * 512: 64 + (tt + 1) * 512], ps[:]),
                        wait=[("b_vp", vpm), ("b_tr", s.cnt.get("b_tr", 0)),
                              ("b_ex", s.cnt.get("b_ex", 0))],
                        inc=["b_vev"])
                vev = s.cnt.get("b_vev", 0)
                for tt in range(2 if KBK >= 2 else 0):
                    s.op("pe", lambda e, tt=tt: e.matmul(
                        p2[:], rot_bf[:],
                        kraw[:, tt * 512:(tt + 1) * 512], start=True, stop=True),
                        wait=[("b_kev", s.cnt.get("b_kev", 0)),
                              ("b_rpe", s.cnt.get("b_rpe", 0))] + CST,
                        inc=["b_rot"])
                    rm = s.cnt["b_rot"]
                    s.op("dve", lambda e, tt=tt: e.tensor_tensor(
                        tmpA[:, 0:512], kraw[:, tt * 512:(tt + 1) * 512],
                        ck[:, tt * 512:(tt + 1) * 512], op=ALU.mult),
                        wait=[("b_rpe2", s.cnt.get("b_rpe2", 0))], inc=["b_rpe2"])
                    s.op("dve", lambda e, tt=tt: e.tensor_tensor(
                        tmpB[:, 0:512], p2[:, 0:512],
                        skt[:, tt * 512:(tt + 1) * 512], op=ALU.mult),
                        wait=[("b_rot", rm)], inc=["b_rpe"])
                    s.op("dve", lambda e, tt=tt: e.tensor_tensor(
                        sk_ext[:, 64 + tt * 512: 64 + (tt + 1) * 512],
                        tmpA[:, 0:512], tmpB[:, 0:512], op=ALU.add),
                        wait=[("b_rpe", None), ("b_rpe2", None),
                              ("b_sc", s.cnt.get("b_sc", 0)),
                              ("b_ex", s.cnt.get("b_ex", 0))], inc=["b_rpe2"])
                GS = "b_g"

                def gs(eng, fn):
                    if KBK >= 3:
                        s.op(eng, fn, wait=[(GS, s.cnt.get(GS, 0))], inc=[GS])
                if KBK >= 3:
                  s.op("act", lambda e: e.activation(tmpC[:], kraw[:], AF.Abs),
                     wait=[("b_kev", s.cnt.get("b_kev", 0)),
                           (GS, s.cnt.get(GS, 0)),
                           ("b_qt", s.cnt.get("b_qt", 0)),
                           ("b_chn", s.cnt.get("b_chn", 0)),
                           ("b_fold", s.cnt.get("b_fold", 0)),
                           ("b_kh", s.cnt.get("b_kh", 0))], inc=[GS])
                gs("act", lambda e: e.activation(a_sb[:], tmpC[:], AF.Exp, scale=-1.0))
                gs("act", lambda e: e.activation(tmpC[:], a_sb[:], AF.Ln, bias=1.0))
                gs("dve", lambda e: e.tensor_scalar(
                    a_sb[:], kraw[:], 0.0, None, op0=ALU.min))
                gs("dve", lambda e: e.tensor_tensor(
                    g_sb[:], a_sb[:], tmpC[:], op=ALU.subtract))
                for ch in range(8):
                    gs("dve", lambda e, ch=ch: e.tensor_tensor_scan(
                        a_sb[:, ch * 128:(ch + 1) * 128], ones128[:],
                        g_sb[:, ch * 128:(ch + 1) * 128], 0.0,
                        op0=ALU.mult, op1=ALU.add))
                gs("dve", lambda e: e.tensor_scalar(
                    a_sb[:], a_sb[:], -1280.0, None, op0=ALU.max))
                gs("act", lambda e: e.activation(eA[:], a_sb[:], AF.Exp,
                                                 scale=1.0 / 16.0))
                gs("act", lambda e: e.activation(eNA[:], a_sb[:], AF.Exp,
                                                 scale=-1.0 / 16.0))
                gs("dve", lambda e: e.memset(aoff[:, 0:1], 0.0))
                for ch in range(1, 8):
                    gs("dve", lambda e, ch=ch: e.tensor_tensor(
                        aoff[:, ch:ch + 1], aoff[:, ch - 1:ch],
                        a_sb[:, ch * 128 - 1:ch * 128], op=ALU.add))
                gs("act", lambda e: e.activation(aoff[:, 8:16], aoff[:, 0:8],
                                                 AF.Exp, scale=1.0 / 16.0))
                gsm = s.cnt.get(GS, 0)
                for tt in range(2 if KBK >= 4 else 0):
                    s.op("pe", lambda e, tt=tt: e.matmul(
                        p3[0:1, 0:512],
                        ones_bf[:], kx[:, tt * 512:(tt + 1) * 512],
                        start=True, stop=True),
                        wait=[("b_kev", s.cnt.get("b_kev", 0)),
                              ("b_zev", s.cnt.get("b_zev", 0)),
                              ("b_swa", s.cnt.get("b_swa", 0))],
                        inc=["b_zk"])
                    zkm = s.cnt["b_zk"]
                    s.op("dve", lambda e, tt=tt: e.tensor_copy(
                        zrow[0:1, tt * 512:(tt + 1) * 512], p3[0:1, 0:512]),
                        wait=[("b_zk", zkm)], inc=["b_zev"])
                if KBK >= 4:
                    s.op("dve", lambda e: e.reciprocal(zrow2[0:1, :], zrow[0:1, :]),
                         wait=[("b_zev", None)], inc=["b_zev"])
                    s.op("sp", lambda e, gidx=gidx: e.dma_start(
                        out=zdram[gidx, 0:1, :], in_=zrow2[0:1, :]),
                        wait=[("b_zev", None)], inc=["b_zst"], dma=True)
                    s.op("sp", lambda e, gidx=gidx: e.dma_start(
                        out=zbK[:], in_=zdram[gidx, 0:1, :].partition_broadcast(128)),
                        wait=[("b_zst", None),
                              ("b_kt", s.cnt.get("b_kt", 0))], inc=["b_zbk"], dma=True)
                zbkm = s.cnt.get("b_zbk", 0)
                if KBK >= 5:
                    s.op("dve", lambda e: e.tensor_tensor(
                        tmpC[:], kx[:], eNA[:], op=ALU.mult),
                        wait=[(GS, gsm)], inc=["b_kt"])
                    s.op("dve", lambda e: e.tensor_tensor(
                        ktl[:], tmpC[:], zbK[:], op=ALU.mult),
                        wait=[("b_zbk", zbkm), ("b_kt", None),
                              ("b_P", s.cnt.get("b_P", 0))], inc=["b_kt"])
                ktm = s.cnt.get("b_kt", 0)
                for ch in range(8 if KBK >= 5 else 0):
                    s.op("dve", lambda e, ch=ch: e.tensor_scalar(
                        khat[:, ch * 128:(ch + 1) * 128],
                        ktl[:, ch * 128:(ch + 1) * 128],
                        eA[:, ch * 128 + 127: ch * 128 + 128], None, op0=ALU.mult),
                        wait=[("b_kt", ktm), ("b_tr", s.cnt.get("b_tr", 0))]
                        if ch == 0 else [("b_kt", ktm)],
                        inc=["b_kh"])
                khm = s.cnt.get("b_kh", 0)
                trjobs = []
                for ch in range(8):
                    trjobs.append((khat.ap()[:, ch * 128:(ch + 1) * 128],
                                   khtok.ap()[:, ch * 128:(ch + 1) * 128],
                                   ("b_kh", khm)))
                for ch in range(8):
                    trjobs.append((v_ext.ap()[:, 64 + ch * 128: 64 + (ch + 1) * 128],
                                   v_tok.ap()[:, ch * 128:(ch + 1) * 128],
                                   ("b_vev", vev)))
                for j in range(1, 9):
                    trjobs.append((v_ext.ap()[:, j * 128:(j + 1) * 128],
                                   v_win.ap()[:, j * 128:(j + 1) * 128],
                                   ("b_vev", vev)))
                for j0 in range(0, len(trjobs) if KBK >= 6 else 0, 4):
                    grp = trjobs[j0:j0 + 4]
                    for ji, (src, dst, wt) in enumerate(grp):
                        s.op("pe", lambda e, src=src, ji=ji: e.transpose(
                            p7[:, ji * 128:(ji + 1) * 128], src, id_bf[:]),
                            wait=[wt, ("b_tev", s.cnt.get("b_tev", 0))]
                            if ji == 0 else [wt],
                            inc=(["b_tr"] if ji == len(grp) - 1 else []))
                    trm = s.cnt["b_tr"]
                    for ji, (src, dst, wt) in enumerate(grp):
                        s.op("dve", lambda e, dst=dst, ji=ji: e.tensor_copy(
                            dst, p7[:, ji * 128:(ji + 1) * 128]),
                            wait=[("b_tr", trm)],
                            inc=(["b_tev"] if ji == len(grp) - 1 else []))
                tevm = s.cnt.get("b_tev", 0)
                if KBK >= 7:
                    s.op("dve", lambda e: e.memset(S_state[:], 0.0),
                         wait=[("b_chn", s.cnt.get("b_chn", 0)),
                               ("b_oacc", s.cnt.get("b_oacc", 0))], inc=["b_chn"])
                for ch in range(8 if KBK >= 7 else 0):
                    s.op("dve", lambda e, ch=ch: e.tensor_copy(
                        S0st[:, ch * 128:(ch + 1) * 128], S_state[:]),
                        wait=[("b_chn", s.cnt.get("b_chn", 0)),
                              ("b_fold", s.cnt.get("b_fold", 0))], inc=["b_chn"])
                    s.op("pe", lambda e, ch=ch: e.matmul(
                        p5[:, 0:128], khtok[:, ch * 128:(ch + 1) * 128],
                        v_tok[:, ch * 128:(ch + 1) * 128],
                        start=True, stop=True),
                        wait=[("b_tev", tevm), ("b_chn", s.cnt.get("b_chn", 0))]
                        + ([("b_Pe", s.cnt.get("b_Pe", 0))] if ch == 0 else []),
                        inc=["b_dS"])
                    dsm = s.cnt["b_dS"]
                    s.op("dve", lambda e, ch=ch: e.scalar_tensor_tensor(
                        S_state[:], S_state[:],
                        eA[:, ch * 128 + 127: ch * 128 + 128],
                        p5[:, 0:128], op0=ALU.mult, op1=ALU.add),
                        wait=[("b_dS", dsm), ("b_chn", s.cnt.get("b_chn", 0))],
                        inc=["b_chn"])
                chm = s.cnt.get("b_chn", 0)
                if KBK >= 8:
                  s.op("pool", lambda e, gidx=gidx: e.dma_start(
                    out=exch_loc.ap()[gidx, :, 0:128], in_=S_state[:]),
                    wait=[("b_chn", chm)], inc=["b_ex"], dma=True)
                  s.op("pool", lambda e, gidx=gidx: e.dma_start(
                    out=exch_loc.ap()[gidx, :, 128:192],
                    in_=sk_ext[:, 1024:1088]),
                    wait=[("b_rpe2", None)], inc=["b_ex"], dma=True)
                  s.op("pool", lambda e, gidx=gidx: e.dma_start(
                    out=exch_loc.ap()[gidx, :, 192:256],
                    in_=v_ext[:, 1024:1088]),
                    wait=[("b_vev", vev)], inc=["b_ex"], dma=True)
                  s.op("pool", lambda e, gidx=gidx: e.collective_compute(
                    "AllGather", ALU.bypass,
                    replica_groups=[[0, 1], [2, 3], [4, 5], [6, 7]],
                    ins=[exch_loc.ap()[gidx]], outs=[exch_pair.ap()[gidx]]),
                    wait=[("b_ex", None)], inc=[("ccb", 1)])

                for h in range(4 if KBSUB >= 2 else 0):
                    wqi = wqb[h % 2]
                    for h4 in range(4):
                        s.op("pool", lambda e, h=h, h4=h4, gidx=gidx, wqi=wqi:
                             e.dma_start(
                            out=wqi.ap().rearrange("p (c o) -> p c o", c=DCN)[
                                :, h4 * 8:(h4 + 1) * 8, :],
                            in_=wqv[:, h4 * 8:(h4 + 1) * 8,
                                    (gidx * 4 + h) * 128:(gidx * 4 + h + 1) * 128]),
                            wait=[("b_qp", max(0, s.cnt.get("b_qp", 0) - 2))]
                            if h4 == 0 else [],
                            inc=["b_ldq"], dma=True)
                    ldq = s.cnt["b_ldq"]
                    for tt in range(2):
                        ps = [p0, p1][(h * 2 + tt) % 2]
                        for dc in range(DCN):
                            s.op("pe", lambda e, dc=dc, tt=tt, ps=ps, wqi=wqi:
                                 e.matmul(
                                ps[:], wqi[:, dc * 128:(dc + 1) * 128],
                                hT[:, dc * TOK + tt * 512: dc * TOK + tt * 512 + 512],
                                start=(dc == 0), stop=(dc == DCN - 1)),
                                wait=([("b_ldq", ldq),
                                       ("b_qt", s.cnt.get("b_qt", 0)),
                                       ("b_vev", vev),
                                       ("b_qev", max(0, s.cnt.get("b_qev", 0) - 2))]
                                      if dc == 0 else []),
                                inc=(["b_qp"] if dc == DCN - 1 else []))
                        qpm = s.cnt["b_qp"]
                        col = h * TOK + tt * 512
                        s.op("dve", lambda e, col=col, ps=ps: e.tensor_copy(
                            qraw[:, col:col + 512], ps[:]),
                            wait=[("b_qp", qpm)], inc=["b_qev"])
                        s.op("act", lambda e, col=col: e.activation(
                            qx[:, col:col + 512],
                            qraw[:, col:col + 512], AF.Exp),
                            wait=[("b_qev", s.cnt.get("b_qev", 0))],
                            inc=["b_qev"])
                    for tt in range(2):
                        col = h * TOK + tt * 512
                        s.op("pe", lambda e, col=col: e.matmul(
                            p2[:], rot_bf[:], qraw[:, col:col + 512],
                            start=True, stop=True),
                            wait=[("b_qev", s.cnt.get("b_qev", 0)),
                                  ("b_rpe", s.cnt.get("b_rpe", 0))],
                            inc=["b_rot"])
                        rm = s.cnt["b_rot"]
                        s.op("dve", lambda e, col=col, tt=tt: e.tensor_tensor(
                            tmpA[:, 0:512], qraw[:, col:col + 512],
                            cq[:, tt * 512:(tt + 1) * 512], op=ALU.mult),
                            wait=[("b_rpe2", s.cnt.get("b_rpe2", 0))],
                            inc=["b_rpe2"])
                        s.op("dve", lambda e, col=col, tt=tt: e.tensor_tensor(
                            tmpB[:, 0:512], p2[:, 0:512],
                            sqt[:, tt * 512:(tt + 1) * 512], op=ALU.mult),
                            wait=[("b_rot", rm)], inc=["b_rpe"])
                        s.op("dve", lambda e, col=col: e.tensor_tensor(
                            sq_sb[:, col:col + 512],
                            tmpA[:, 0:512], tmpB[:, 0:512], op=ALU.add),
                            wait=[("b_rpe", None), ("b_rpe2", None)],
                            inc=["b_rpe2"])
                    for tt in range(2):
                        col = h * TOK + tt * 512
                        s.op("pe", lambda e, col=col, tt=tt: e.matmul(
                            p3[0:1, 0:512],
                            ones_bf[:], qx[:, col:col + 512],
                            start=True, stop=True),
                            wait=[("b_qev", s.cnt.get("b_qev", 0)),
                                  ("b_zev", s.cnt.get("b_zev", 0)),
                                  ("b_swa", s.cnt.get("b_swa", 0))],
                            inc=["b_zk"])
                        zkm = s.cnt["b_zk"]
                        s.op("dve", lambda e, tt=tt: e.tensor_copy(
                            zrow[0:1, tt * 512:(tt + 1) * 512], p3[0:1, 0:512]),
                            wait=[("b_zk", zkm)], inc=["b_zev"])
                    s.op("dve", lambda e: e.reciprocal(zrow2[0:1, :], zrow[0:1, :]),
                         wait=[("b_zev", None)], inc=["b_zev"])
                    s.op("dve", lambda e: e.tensor_scalar(
                        zrow2[0:1, :], zrow2[0:1, :], 0.5, None, op0=ALU.mult),
                        wait=[("b_zev", None)], inc=["b_zev"])
                    s.op("sp", lambda e, gidx=gidx, h=h: e.dma_start(
                        out=zdram[gidx, 1 + h: 2 + h, :], in_=zrow2[0:1, :]),
                        wait=[("b_zev", None)], inc=["b_zst"], dma=True)
                    s.op("sp", lambda e, gidx=gidx, h=h: e.dma_start(
                        out=zbQ[:],
                        in_=zdram[gidx, 1 + h: 2 + h, :].partition_broadcast(128)),
                        wait=[("b_zst", None), ("b_qt", s.cnt.get("b_qt", 0)),
                              ("b_kt", ktm)],
                        inc=["b_zbq"], dma=True)
                    zbqm = s.cnt["b_zbq"]
                    s.op("dve", lambda e, h=h: e.tensor_tensor(
                        tmpC[:], qx[:, h * TOK:(h + 1) * TOK], eA[:], op=ALU.mult),
                        wait=[(GS, gsm), ("b_kt", ktm),
                              ("b_qt", s.cnt.get("b_qt", 0))],
                        inc=["b_qt"])
                    s.op("dve", lambda e, h=h: e.tensor_tensor(
                        qtl[:, h * TOK:(h + 1) * TOK], tmpC[:], zbQ[:],
                        op=ALU.mult),
                        wait=[("b_zbq", zbqm), ("b_qt", None),
                              ("b_zk", s.cnt.get("b_zk", 0)),
                              ("b_P", s.cnt.get("b_P", 0))], inc=["b_qt"])
                qtm = s.cnt.get("b_qt", 0)

                if KBK >= 8:
                  s.op("sp", lambda e, gidx=gidx: e.dma_start(
                    out=exch_sb[:], in_=exch_pair.ap()[gidx][0:128, :]),
                    wait=[("ccb", gidx + 1),
                          ("b_fold", s.cnt.get("b_fold", 0))],
                    inc=["b_exl"], dma=True)
                exm = s.cnt.get("b_exl", 0)
                s.op("dve", lambda e: e.tensor_scalar(
                    sprev[:], exch_sb[:, 0:128], gate_sb[:, 0:1], None,
                    op0=ALU.mult),
                    wait=[("b_exl", exm)] + CST, inc=["b_fold"])
                s.op("dve", lambda e: e.tensor_scalar(
                    sk_ext[:, 0:64], exch_sb[:, 128:192], gate_sb[:, 0:1], None,
                    op0=ALU.mult),
                    wait=[("b_exl", exm), ("b_sc", s.cnt.get("b_sc", 0))],
                    inc=["b_fold"])
                s.op("dve", lambda e: e.tensor_scalar(
                    v_ext[:, 0:64], exch_sb[:, 192:256], gate_sb[:, 0:1], None,
                    op0=ALU.mult),
                    wait=[("b_exl", exm)], inc=["b_fold"])
                fm = s.cnt["b_fold"]
                s.op("pe", lambda e: e.transpose(
                    p7[:, 0:128], v_ext[:, 0:128], id_bf[:]),
                    wait=[("b_fold", fm), ("b_tev", tevm)], inc=["b_tr"])
                s.op("dve", lambda e: e.tensor_copy(v_win[:, 0:128], p7[:, 0:128]),
                     wait=[("b_tr", s.cnt["b_tr"])], inc=["b_tev"])
                for ch in range(8):
                    s.op("dve", lambda e, ch=ch: e.scalar_tensor_tensor(
                        S0st[:, ch * 128:(ch + 1) * 128], sprev[:],
                        aoff[:, 8 + ch: 9 + ch],
                        S0st[:, ch * 128:(ch + 1) * 128],
                        op0=ALU.mult, op1=ALU.add),
                        wait=[("b_fold", fm), ("b_chn", chm),
                              (GS, gsm)], inc=["b_fold"])
                foldm = s.cnt["b_fold"]

                for u in range(32 if KBSUB >= 3 else 0):
                    h, ch = divmod(u, 8)
                    qcol = h * TOK + ch * 128
                    psP = [p5, p6][u % 2]
                    psS = [p4, p3][u % 2]
                    psO = [p0, p1][u % 2]
                    PT = PTb[u % 2]
                    eT = ePT[u % 2]
                    s.op("pe", lambda e, qcol=qcol, ch=ch, psP=psP: e.matmul(
                        psP[:, 0:128], ktl[:, ch * 128:(ch + 1) * 128],
                        qtl[:, qcol:qcol + 128], start=True, stop=True),
                        wait=[("b_qt", qtm), ("b_kt", ktm),
                              ("b_chn", chm),
                              ("b_Pe", max(0, s.cnt.get("b_Pe", 0) - 1))],
                        inc=["b_P"])
                    pm = s.cnt["b_P"]
                    s.op("pe", lambda e, qcol=qcol, ch=ch, psS=psS: e.matmul(
                        psS[:, 0:256], sq_sb[:, qcol:qcol + 128],
                        sk_ext[:, ch * 128: ch * 128 + 256],
                        start=True, stop=True),
                        wait=[("b_rpe2", None), ("b_fold", foldm),
                              ("b_zev", s.cnt.get("b_zev", 0)),
                              ("b_swa", max(0, s.cnt.get("b_swa", 0) - 1))],
                        inc=["b_sc"])
                    scm = s.cnt["b_sc"]
                    s.op("dve", lambda e, psP=psP, PT=PT: e.tensor_tensor(
                        PT[:], psP[:, 0:128], tri_sb[:], op=ALU.mult),
                        wait=[("b_P", pm),
                              ("b_oacc", max(0, s.cnt.get("b_oacc", 0) - 1))],
                        inc=["b_Pe"])
                    pem = s.cnt["b_Pe"]
                    s.op("act", lambda e, psS=psS: e.activation(
                        ePf[:, 0:256], psS[:, 0:256], AF.Exp),
                        wait=[("b_sc", scm),
                              ("b_swe", max(0, s.cnt.get("b_swe", 0) - 3))],
                        inc=["b_swa"])
                    swam = s.cnt["b_swa"]
                    msk = swm0_sb if ch == 0 else swm_sb
                    s.op("dve", lambda e, msk=msk: e.tensor_tensor(
                        ePm[:, 0:256], ePf[:, 0:256], msk[:], op=ALU.mult),
                        wait=[("b_swa", swam)] + CST, inc=["b_swe"])
                    s.op("dve", lambda e: e.reduce_sum(
                        zsw[:, 0:1], ePm[:, 0:256], axis=AX),
                        wait=[("b_swe", None)], inc=["b_swe"])
                    s.op("dve", lambda e: e.reciprocal(zsw[:, 1:2], zsw[:, 0:1]),
                         wait=[("b_swe", None)], inc=["b_swe"])
                    s.op("dve", lambda e: e.tensor_scalar(
                        ePb[:, 0:256], ePm[:, 0:256], zsw[:, 1:2], 0.5,
                        op0=ALU.mult, op1=ALU.mult),
                        wait=[("b_swe", None), ("b_tr", s.cnt.get("b_tr", 0))],
                        inc=["b_swe"])
                    swem = s.cnt["b_swe"]
                    for ji in range(2):
                        s.op("pe", lambda e, ji=ji: e.transpose(
                            p7[:, ji * 128:(ji + 1) * 128],
                            ePb[:, ji * 128:(ji + 1) * 128], id_bf[:]),
                            wait=[("b_swe", swem),
                                  ("b_tev", s.cnt.get("b_tev", 0))]
                            if ji == 0 else [],
                            inc=(["b_tr"] if ji == 1 else []))
                    s.op("dve", lambda e, eT=eT: e.tensor_copy(
                        eT[:, 0:256], p7[:, 0:256]),
                        wait=[("b_tr", s.cnt["b_tr"]),
                              ("b_oacc", max(0, s.cnt.get("b_oacc", 0) - 1))],
                        inc=["b_tev"])
                    tevu = s.cnt["b_tev"]
                    s.op("pe", lambda e, ch=ch, qcol=qcol, psO=psO: e.matmul(
                        psO[:, 0:128], S0st[:, ch * 128:(ch + 1) * 128],
                        qtl[:, qcol:qcol + 128], start=True, stop=False),
                        wait=[("b_fold", foldm),
                              ("b_qev", s.cnt.get("b_qev", 0)),
                              ("b_oev", max(0, s.cnt.get("b_oev", 0) - 1))])
                    s.op("pe", lambda e, ch=ch, psO=psO, PT=PT: e.matmul(
                        psO[:, 0:128], v_tok[:, ch * 128:(ch + 1) * 128],
                        PT[:], start=False, stop=False),
                        wait=[("b_Pe", pem)])
                    s.op("pe", lambda e, ch=ch, psO=psO, eT=eT: e.matmul(
                        psO[:, 0:128], v_win[:, ch * 128:(ch + 1) * 128],
                        eT[:, 0:128], start=False, stop=False),
                        wait=[("b_tev", tevu)])
                    s.op("pe", lambda e, ch=ch, psO=psO, eT=eT: e.matmul(
                        psO[:, 0:128], v_win[:, (ch + 1) * 128:(ch + 2) * 128],
                        eT[:, 128:256], start=False, stop=True),
                        inc=["b_oacc"])
                    oam = s.cnt["b_oacc"]
                    s.op("act", lambda e, qcol=qcol, psO=psO: e.activation(
                        o_sb[:, qcol:qcol + 128], psO[:, 0:128], AF.Copy),
                        wait=[("b_oacc", oam), ("b_ost", 16 * gidx)],
                        inc=["b_oev"])
                s.op("sp", lambda e, gidx=gidx: e.dma_start(
                    out=o_dram.ap()[gidx], in_=o_sb[:]),
                    wait=[("b_oev", None)], inc=["b_ost"], dma=True)
            s.barrier()

        def phase_wo():
          with contextlib.ExitStack() as pes:
            T = pes.enter_context
            o_all = T(nc.sbuf_tensor("w_oall", [128, DCN * TOK], BF16))
            wob = [T(nc.sbuf_tensor(f"w_wo{i}", [128, DCN * 256], BF16))
                   for i in range(2)]
            xb = [T(nc.sbuf_tensor(f"w_xb{i}", [128, TCN * 256], BF16))
                  for i in range(2)]
            xo = [T(nc.sbuf_tensor(f"w_xo{i}", [128, TCN * 256], BF16))
                  for i in range(2)]
            ps_w = [T(nc.psum_tensor(f"w_ps{i}", [128, 256], F32))
                    for i in range(2)]
            for gidx in range(8):
                s.op("pool", lambda e, gidx=gidx: e.dma_start(
                    out=o_all[:, gidx * 4096:(gidx + 1) * 4096],
                    in_=o_dram.ap()[gidx]),
                    wait=[("b_ost", 16 * min(gidx + 1, KBGRP))],
                    inc=["w_ldo"], dma=True)
            ldo = s.cnt["w_ldo"]
            wov = wo.rearrange("(c p) o -> p c o", p=128)
            xsv = x_spill.ap().rearrange("p (c f) -> p c f", c=TCN)
            x2v = x2_dram.ap().rearrange("p (c f) -> p c f", c=TCN)
            for rb in range(16):
                buf = wob[rb % 2]
                bv = buf.ap().rearrange("p (c o) -> p c o", c=DCN)
                for h4 in range(4):
                    s.op("pool", lambda e, bv=bv, rb=rb, h4=h4: e.dma_start(
                        out=bv[:, h4 * 8:(h4 + 1) * 8, :],
                        in_=wov[:, h4 * 8:(h4 + 1) * 8, rb * 256:(rb + 1) * 256]),
                        wait=([("w_pe", 8 * (rb - 1))] if (rb >= 2 and h4 == 0)
                              else []),
                        inc=["w_ldw"], dma=True)
                ldw = s.cnt["w_ldw"]
                xbi = xb[rb % 2]
                s.op("sp", lambda e, xbi=xbi, rb=rb: e.dma_start(
                    out=xbi.ap().rearrange("p (c f) -> p c f", c=TCN),
                    in_=xsv[:, :, rb * 256:(rb + 1) * 256]),
                    wait=[("st_sp", 16)] + ([("w_ev", 8 * (rb - 1))]
                                            if rb >= 2 else []),
                    inc=["w_ldx"], dma=True)
                ldx = s.cnt["w_ldx"]
                xoi = xo[rb % 2]
                for tc in range(TCN):
                    ps = ps_w[tc % 2]
                    for dc in range(DCN):
                        s.op("pe", lambda e, bv=bv, dc=dc, tc=tc, ps=ps: e.matmul(
                            ps[:], o_all[:, dc * TOK + tc * 128: dc * TOK + (tc + 1) * 128],
                            bv[:, dc, :],
                            start=(dc == 0), stop=(dc == DCN - 1)),
                            wait=([("w_ldw", ldw), ("w_ldo", ldo),
                                   ("w_ev", max(0, s.cnt.get("w_ev", 0) - 1))]
                                  if dc == 0 else []),
                            inc=(["w_pe"] if dc == DCN - 1 else []))
                    wpm = s.cnt["w_pe"]
                    s.op("dve", lambda e, xbi=xbi, xoi=xoi, tc=tc, ps=ps:
                         e.tensor_tensor(
                        xoi[:, tc * 256:(tc + 1) * 256],
                        xbi[:, tc * 256:(tc + 1) * 256], ps[:], op=ALU.add),
                        wait=[("w_pe", wpm), ("w_ldx", ldx),
                              ("st_x2", max(0, 16 * (rb - 1)))],
                        inc=["w_ev"])
                s.op("sp", lambda e, xoi=xoi, rb=rb: e.dma_start(
                    out=x2v[:, :, rb * 256:(rb + 1) * 256],
                    in_=xoi.ap().rearrange("p (c f) -> p c f", c=TCN)),
                    wait=[("w_ev", None)], inc=["st_x2"], dma=True)
            s.barrier()

        run_b = DEBUG_STAGE >= 7 and N_SUBLAYERS >= 3
        x_final = x_sb
        if run_b:
            with contextlib.ExitStack() as pes:
                hid = [pes.enter_context(
                    nc.sbuf_tensor(f"hidB{i}", [128, DM], BF16)) for i in range(2)]
                sqs = [pes.enter_context(
                    nc.sbuf_tensor(f"sqB{i}", [128, DM], F32)) for i in range(2)]
                ps_tr = [pes.enter_context(
                    nc.psum_tensor(f"psTrB{i}", [128, 512], BF16)) for i in range(2)]
                phase_ln(x_sb, hid, ps_tr, sqs)
            s.op("sp", lambda e: e.dma_start(out=x_spill[:, :], in_=x_sb[:]),
                 inc=["st_sp"], dma=True)
            for e_ in Sched.ENG:
                s.op(e_, None, wait=[("st_sp", 16)])
            s1.close()

            phase_attn_b_groups()
            phase_wo()

            with contextlib.ExitStack() as pes:
                x2_sb = pes.enter_context(
                    nc.sbuf_tensor("x2_sb", [128, TCN * DM], BF16))
                s.op("sp", lambda e: e.dma_start(out=x2_sb[:], in_=x2_dram[:, :]),
                     wait=[("st_x2", 16 * 16)], inc=["ld_x2"], dma=True)
                for e_ in Sched.ENG:
                    s.op(e_, None, wait=[("ld_x2", 16)])
                if DEBUG_STAGE >= 8 and N_SUBLAYERS >= 4:
                    phase_ffn(1, x2_sb)
                s.op("pool", lambda e: e.dma_start(
                    out=out.rearrange("(c p) f -> p c f", p=128),
                    in_=x2_sb.ap().rearrange("p (c f) -> p c f", c=TCN)),
                    inc=["st_out"], dma=True)
                if s.cnt.get("st_dbg", 0) == 0:
                    s.op("pool", lambda e: e.dma_start(
                        out=dbg, in_=x2_sb[:, 0:TCN * 128]),
                        inc=["st_dbg"], dma=True)
                    s.op("pool", lambda e: e.dma_start(
                        out=dbgb[:, :], in_=x2_sb[:, 0:128]),
                        inc=["st_dbg"], dma=True)
                s.op("sp", None, wait=[("st_out", None), ("st_dbg", None)])
        else:
            s.op("pool", lambda e: e.dma_start(
                out=out.rearrange("(c p) f -> p c f", p=128),
                in_=x_sb.ap().rearrange("p (c f) -> p c f", c=TCN)),
                inc=["st_out"], dma=True)
            if s.cnt.get("st_dbg", 0) == 0:
                s.op("pool", lambda e: e.dma_start(out=dbg, in_=x_sb[:, 0:TCN * 128]),
                     inc=["st_dbg"], dma=True)
                s.op("pool", lambda e: e.dma_start(
                    out=dbgb[:, :], in_=x_sb[:, 0:128]),
                    inc=["st_dbg"], dma=True)
            s.op("sp", None, wait=[("st_out", None), ("st_dbg", None)])
            s1.close()

        with nc.Block() as block:
            block.tensor(lambda e: [f(e) for f in s.q["pe"]])
            block.vector(lambda e: [f(e) for f in s.q["dve"]])
            block.scalar(lambda e: [f(e) for f in s.q["act"]])
            block.gpsimd(lambda e: [f(e) for f in s.q["pool"]])
            block.sync(lambda e: [f(e) for f in s.q["sp"]])

    return nc


def _host_consts(core):
    ident = np.eye(128, dtype=np.float32)
    rot = np.zeros((128, 128), dtype=np.float32)
    for dp in range(64):
        rot[dp + 64, dp] = -1.0
        rot[dp, dp + 64] = 1.0
    pos = (core % 2) * 1024 + np.arange(TOK, dtype=np.float64)
    inv = 1.0 / (10000.0 ** (2.0 * np.arange(64, dtype=np.float64) / 128.0))
    fr = pos[None, :] * inv[:, None]
    emb = np.concatenate([fr, fr], axis=0)
    cos = np.cos(emb).astype(np.float32)
    sin = np.sin(emb).astype(np.float32)
    scl = 1.0 / np.sqrt(128.0)
    tri = np.triu(np.ones((128, 128), dtype=np.float32))
    i = np.arange(128)[:, None]
    j = np.arange(256)[None, :]
    kpos = j - 64
    m = (kpos <= i) & (kpos >= i - 64)
    sw = m.astype(np.float32)
    sw0 = sw.copy()
    if core % 2 == 0:
        sw0[:64, :64] = 0.0
    gate = np.full((128, 1), 1.0 if core % 2 == 1 else 0.0, dtype=np.float32)
    d0 = np.ones((128, TOK), dtype=np.float32)
    d0[:, ::128] = 0.0
    return dict(
        identf=ident, rotm=rot,
        cosq=(cos * scl).astype(np.float32), sinq=(sin * scl).astype(np.float32),
        cosk=cos, sink=sin,
        trimask=tri, swmask=sw, swmask0=sw0, gate=gate, d0mask=d0,
    )


def kernel(L0_attn_a_x, L0_attn_a_w1, L0_attn_a_w2, L0_attn_a_b2,
           L0_attn_a_sparse_mask, L0_attn_b_Wq, L0_attn_b_Wk, L0_attn_b_Wv,
           L0_attn_b_Wo, L0_ffn_base_W, L0_ffn_router_W, L0_ffn_lora_A,
           L0_ffn_lora_B):
    x = np.ascontiguousarray(np.asarray(L0_attn_a_x).reshape(NCORE * TOK, DM))
    w1 = np.ascontiguousarray(L0_attn_a_w1)
    w2t = np.ascontiguousarray(np.asarray(L0_attn_a_w2).T)
    mask = np.ascontiguousarray(L0_attn_a_sparse_mask)
    b2 = np.ascontiguousarray(np.asarray(L0_attn_a_b2).reshape(1, DM))
    base_wt = np.ascontiguousarray(np.asarray(L0_ffn_base_W).T)
    loraa_t = np.ascontiguousarray(
        np.asarray(L0_ffn_lora_A).transpose(1, 0, 2).reshape(DM, 128))
    lorab_f = np.ascontiguousarray(np.asarray(L0_ffn_lora_B).reshape(128, DM))
    routertp = np.zeros((DM, 128), dtype=np.float32)
    routertp[:, :8] = np.asarray(L0_ffn_router_W).T
    wq = np.ascontiguousarray(L0_attn_b_Wq)
    wk = np.ascontiguousarray(L0_attn_b_Wk)
    wv = np.ascontiguousarray(L0_attn_b_Wv)
    wo = np.ascontiguousarray(L0_attn_b_Wo)

    nc = build_program()
    in_maps = []
    for c in range(NCORE):
        m = dict(
            x=np.ascontiguousarray(x[c * TOK:(c + 1) * TOK]),
            w1=w1,
            w2t=np.ascontiguousarray(w2t[:, c * 512:(c + 1) * 512]),
            maskc=np.ascontiguousarray(mask[:, c * 512:(c + 1) * 512]),
            b2=b2, base_wt=base_wt, loraa_t=loraa_t, lorab_f=lorab_f,
            routertp=routertp, wq=wq, wk=wk, wv=wv, wo=wo,
        )
        m.update(_host_consts(c))
        in_maps.append(m)
    res = run_bass_kernel_spmd(nc, in_maps, core_ids=list(range(NCORE)))
    global LAST_RES
    LAST_RES = res
    outs = [res.results[c]["out"] for c in range(NCORE)]
    y = np.concatenate(outs, axis=0).reshape(4, 2048, DM).astype(np.float32)
    return y


# version 03
# speedup vs baseline: 1.1166x; 1.1166x over previous
import contextlib
import numpy as np
import concourse.bass as bass
import concourse.mybir as mybir
from concourse.bass_utils import run_bass_kernel_spmd

F32 = mybir.dt.float32
BF16 = mybir.dt.bfloat16
AF = mybir.ActivationFunctionType
ALU = mybir.AluOpType
AX = mybir.AxisListType.X

NCORE = 8
TOK = 1024
TCN = TOK // 128
DM = 4096
DCN = DM // 128
EPS = 1e-5

N_SUBLAYERS = 4
import os
DEBUG_STAGE = int(os.environ.get("KSTAGE", "99"))
KBSUB = int(os.environ.get("KBSUB", "3"))
KBGRP = int(os.environ.get("KBGRP", "8"))
KBK = int(os.environ.get("KBK", "8"))


class Sched:

    ENG = ("pe", "dve", "act", "pool", "sp")

    def __init__(self, nc, es):
        self.nc = nc
        self.es = es
        self.q = {e: [] for e in self.ENG}
        self.sems = {}
        self.cnt = {}
        self.last_wait = {e: {} for e in self.ENG}

    def sem(self, name):
        if name not in self.sems:
            self.sems[name] = self.es.enter_context(self.nc.semaphore(name))
            self.cnt[name] = 0
        return self.sems[name]

    def op(self, eng, fn, wait=(), inc=(), dma=False):
        wl = []
        for w in wait:
            name, val = w if isinstance(w, tuple) else (w, self.cnt[w])
            if val is None:
                val = self.cnt.get(name, 0)
            self.sem(name)
            if val > 0 and val > self.last_wait[eng].get(name, -1):
                self.last_wait[eng][name] = val
                wl.append((self.sems[name], val))
        il = []
        for i in inc:
            if isinstance(i, tuple):
                name, n = i
            else:
                name, n = i, (16 if dma else 1)
            self.sem(name)
            self.cnt[name] += n
            il.append((self.sems[name], n))

        def play(e, fn=fn, wl=wl, il=il):
            for s, v in wl:
                e.wait_ge(s, v)
            inst = fn(e) if fn is not None else None
            if il:
                assert inst is not None
                for s, n in il:
                    inst.then_inc(s, n)

        self.q[eng].append(play)

    def barrier(self):
        def dn(x):
            x.drain()
            return x.nop()
        for e in self.ENG:
            self.op(e, dn, inc=[("bar_" + e, 1)])
        snap = [(("bar_" + e), self.cnt["bar_" + e]) for e in self.ENG]
        for e in self.ENG:
            self.op(e, None, wait=snap)


def build_program():
    nc = bass.Bass("TRN2", target_bir_lowering=False, debug=False)

    def inp(name, shape, dt=F32):
        return nc.dram_tensor(name, shape, dt, kind="ExternalInput").ap()

    x_in = inp("x", [TOK, DM])
    w1 = inp("w1", [16, 128, DCN * 256])
    w2t = inp("w2t", [128, DCN * 512])
    maskc = inp("maskc", [DM, 512])
    b2 = inp("b2", [1, DM])
    base_wt = inp("base_wt", [16, 128, DCN * 256])
    loraa_t = inp("loraa_t", [128, DCN * 128])
    lorab_f = inp("lorab_f", [128, DM])
    routertp = inp("routertp", [128, DCN * 128])
    wq = inp("wq", [32, 128, DCN * 128])
    wk = inp("wk", [8, 128, DCN * 128])
    wv = inp("wv", [8, 128, DCN * 128])
    wo = inp("wo", [16, 128, DCN * 256])
    identf = inp("identf", [128, 128])
    rotm = inp("rotm", [128, 128])
    cosq = inp("cosq", [128, TOK])
    sinq = inp("sinq", [128, TOK])
    cosk = inp("cosk", [128, TOK])
    sink = inp("sink", [128, TOK])
    trimask = inp("trimask", [128, 128])
    swmask = inp("swmask", [128, 256])
    swmask0 = inp("swmask0", [128, 256])
    gate = inp("gate", [128, 1])
    d0mask = inp("d0mask", [128, TOK])

    out = nc.dram_tensor("out", [TOK, DM], F32, kind="ExternalOutput").ap()
    dbg = nc.dram_tensor("dbg", [128, TCN * 128], F32, kind="ExternalOutput").ap()
    dbgb = nc.dram_tensor("dbgb", [128, 128], F32, kind="ExternalOutput").ap()

    attn_loc = nc.dram_tensor("attn_loc", [128, DCN * 512], BF16)
    attn_all = nc.dram_tensor("attn_all", [NCORE * 128, DCN * 512], BF16)
    x_spill = nc.dram_tensor("x_spill", [128, TCN * DM], BF16)
    x2_dram = nc.dram_tensor("x2_dram", [128, TCN * DM], BF16)
    o_dram = nc.dram_tensor("o_dram", [8, 128, 4 * TOK], BF16)
    exch_loc = nc.dram_tensor("exch_loc", [8, 128, 256], F32)
    exch_pair = nc.dram_tensor("exch_pair", [8, 256, 256], F32)
    zdram = nc.dram_tensor("zdram", [8, 5, 1024], F32)

    es_root = contextlib.ExitStack()
    with es_root as es:
        s = Sched(nc, es)

        hT = es.enter_context(nc.sbuf_tensor("hT", [128, DCN * TOK], BF16, side="right"))
        id_bf = es.enter_context(nc.sbuf_tensor("id_bf", [128, 128], BF16, side="right"))
        ones_bf = es.enter_context(nc.sbuf_tensor("ones_bf", [128, 1], BF16, side="right"))
        ones_row = es.enter_context(nc.sbuf_tensor("ones_row", [1, 128], F32, side="right"))
        half_row = es.enter_context(nc.sbuf_tensor("half_row", [1, 128], F32, side="right"))
        stat = es.enter_context(nc.sbuf_tensor("stat", [128, 48], F32, side="right"))
        eps_sb = es.enter_context(nc.sbuf_tensor("eps_sb", [128, 1], F32, side="right"))

        s1 = contextlib.ExitStack()
        x_sb = s1.enter_context(nc.sbuf_tensor("x_sb", [128, TCN * DM], BF16))

        s.op("pool", lambda e: e.dma_start(
            out=x_sb.ap().rearrange("p (c f) -> p c f", c=TCN),
            in_=x_in.rearrange("(c p) f -> p c f", p=128)),
            inc=["ld_x"], dma=True)
        s.op("pool", lambda e: e.dma_start(out=id_bf[:], in_=identf),
             inc=["ld_const"], dma=True)
        s.op("dve", lambda e: e.memset(ones_bf[:], 1.0), inc=["const_c"])
        s.op("dve", lambda e: e.memset(eps_sb[:], EPS), inc=["const_c"])
        s.op("dve", lambda e: e.memset(ones_row[:], 1.0), inc=["const_c"])
        s.op("dve", lambda e: e.memset(half_row[:], 0.5), inc=["const_c"])

        ln_id = [0]

        def phase_ln(xt, hidden_scratch, ps_tr, sq_scratch):
            tag = "ln"
            ln_id[0] += 1
            b_rd = s.cnt.get("rd_" + tag, 0)
            b_act = s.cnt.get("act_" + tag, 0)
            b_pe = s.cnt.get("pe_" + tag, 0)
            for tc in range(TCN):
                xc = xt[:, tc * DM:(tc + 1) * DM]
                sq_i = sq_scratch[tc % 2]
                s.op("act", lambda e, xc=xc, sq=sq_i: e.activation(
                    sq[:], xc, AF.Square),
                    wait=[("rd_" + tag, b_rd + max(0, tc - 1))], inc=["act_" + tag])
                s.op("dve", lambda e, xc=xc, tc=tc: e.reduce_sum(
                    stat[:, 0 + tc:1 + tc], xc, axis=AX),
                    inc=["dve_" + tag])
                s.op("dve", lambda e, sq=sq_i, tc=tc: e.reduce_sum(
                    stat[:, 8 + tc:9 + tc], sq[:], axis=AX),
                    wait=[("act_" + tag, b_act + tc + 1)], inc=["rd_" + tag])
            s.op("dve", lambda e: e.tensor_scalar(
                stat[:, 32:40], stat[:, 8:16], 1.0 / DM, None, op0=ALU.mult),
                wait=[("rd_" + tag, b_rd + TCN)], inc=["dve_" + tag])
            s.op("dve", lambda e: e.tensor_scalar(
                stat[:, 16:24], stat[:, 0:8], 1.0 / DM, None, op0=ALU.mult),
                wait=[("dve_" + tag, None)],
                inc=["dve_" + tag])
            s.op("dve", lambda e: e.tensor_tensor(
                stat[:, 40:48], stat[:, 16:24], stat[:, 16:24], op=ALU.mult),
                wait=[("dve_" + tag, None)], inc=["dve_" + tag])
            s.op("dve", lambda e: e.tensor_tensor(
                stat[:, 32:40], stat[:, 32:40], stat[:, 40:48], op=ALU.subtract),
                wait=[("dve_" + tag, None)], inc=["dve_" + tag])
            s.op("act", lambda e: e.activation(
                stat[:, 40:48], stat[:, 32:40], AF.Sqrt, bias=eps_sb[:, 0:1]),
                wait=[("dve_" + tag, None), ("const_c", None)],
                inc=["act_" + tag])
            s.op("dve", lambda e: e.reciprocal(stat[:, 24:32], stat[:, 40:48]),
                 wait=[("act_" + tag, None)], inc=["dve_" + tag])
            recip_mark = s.cnt["dve_" + tag]
            for tc in range(TCN):
                xc = xt[:, tc * DM:(tc + 1) * DM]
                ht_i = hidden_scratch[tc % 2]
                s.op("dve", lambda e, xc=xc, ht=ht_i, tc=tc: e.tensor_scalar(
                    ht[:], xc, stat[:, 16 + tc:17 + tc], stat[:, 24 + tc:25 + tc],
                    op0=ALU.subtract, op1=ALU.mult),
                    wait=[("dve_" + tag, recip_mark),
                          ("pe_" + tag, b_pe + max(0, 8 * (tc - 1)))],
                    inc=["dve_" + tag])
                hm = s.cnt["dve_" + tag]
                for dg in range(DCN // 4):
                    ps = ps_tr[dg % 2]
                    for j in range(4):
                        dc = dg * 4 + j
                        s.op("pe", lambda e, ht=ht_i, dc=dc, j=j, ps=ps: e.transpose(
                            ps[:, j * 128:(j + 1) * 128],
                            ht[:, dc * 128:(dc + 1) * 128], id_bf[:]),
                            wait=[("ld_const", 16), ("dve_" + tag, hm),
                                  ("ev_" + tag, max(0, s.cnt.get("ev_" + tag, 0) - 1))],
                            inc=(["pe_" + tag] if j == 3 else []))
                    pm = s.cnt["pe_" + tag]
                    dst = hT.ap().rearrange("p (d t) -> p d t", d=DCN)[
                        :, dg * 4:(dg + 1) * 4, tc * 128:(tc + 1) * 128]
                    src = ps.ap().rearrange("p (a b) -> p a b", a=4)
                    eng = "dve" if (dg % 2 == 0) else "act"
                    if eng == "dve":
                        s.op("dve", lambda e, dst=dst, src=src: e.tensor_copy(dst, src),
                             wait=[("pe_" + tag, pm)], inc=["ev_" + tag])
                    else:
                        s.op("act", lambda e, dst=dst, src=src: e.activation(
                            dst, src, AF.Copy),
                            wait=[("pe_" + tag, pm)], inc=["ev_" + tag])
            s.barrier()

        if DEBUG_STAGE >= 1:
          with contextlib.ExitStack() as pes:
            hid = [pes.enter_context(nc.sbuf_tensor(f"hidA{i}", [128, DM], BF16))
                   for i in range(2)]
            sqs = [pes.enter_context(nc.sbuf_tensor(f"sqA{i}", [128, DM], F32))
                   for i in range(2)]
            ps_tr = [pes.enter_context(nc.psum_tensor(f"psTrA{i}", [128, 512], BF16))
                     for i in range(2)]
            s.op("dve", None, wait=[("ld_x", 16)])
            s.op("act", None, wait=[("ld_x", 16)])
            phase_ln(x_sb, hid, ps_tr, sqs)

        if DEBUG_STAGE >= 2:
          with contextlib.ExitStack() as pes:
            w2t_sb = pes.enter_context(nc.sbuf_tensor("w2t_sb", [128, DCN * 512], BF16))
            w1b = [pes.enter_context(nc.sbuf_tensor(f"w1b{i}", [128, DCN * 256], BF16))
                   for i in range(2)]
            mkb = [pes.enter_context(nc.sbuf_tensor(f"mkb{i}", [128, 512], F32))
                   for i in range(2)]
            atile = [pes.enter_context(nc.sbuf_tensor(f"atile{i}", [128, 512], BF16))
                     for i in range(2)]
            ps_a = [pes.enter_context(nc.psum_tensor(f"psA{i}", [128, 512], F32))
                    for i in range(2)]

            s.op("pool", lambda e: e.dma_start(out=w2t_sb[:], in_=w2t),
                 inc=["ld_w2t"], dma=True)

            for scb in range(16):
                buf = w1b[scb % 2]
                s.op("pool", lambda e, buf=buf, scb=scb: e.dma_start(
                    out=buf[:], in_=w1[scb]),
                    wait=([("pe_aw", 2 * (scb - 1))] if scb >= 2 else []),
                    inc=["ld_w1"], dma=True)
                ldm = s.cnt["ld_w1"]
                for sc2 in range(2):
                    sc = scb * 2 + sc2
                    ps = ps_a[sc % 2]
                    mb = mkb[sc % 2]
                    at = atile[sc % 2]
                    s.op("sp", lambda e, mb=mb, sc=sc: e.dma_start(
                        out=mb[:], in_=maskc[sc * 128:(sc + 1) * 128, :]),
                        wait=[("ev_attn", max(0, sc - 1))],
                        inc=["ld_mk"], dma=True)
                    mkm = s.cnt["ld_mk"]
                    bufv = buf.ap().rearrange("p (c t) -> p c t", c=DCN)
                    for dc in range(DCN):
                        s.op("pe", lambda e, bufv=bufv, dc=dc, sc2=sc2, ps=ps: e.matmul(
                            ps[:], bufv[:, dc, sc2 * 128:(sc2 + 1) * 128],
                            w2t_sb[:, dc * 512:(dc + 1) * 512],
                            start=(dc == 0), stop=(dc == DCN - 1)),
                            wait=([("ld_w1", ldm), ("ld_w2t", 16),
                                   ("ev_attn", max(0, s.cnt.get("ev_attn", 0) - 1))]
                                  if dc == 0 else []),
                            inc=(["pe_aw"] if dc == DCN - 1 else []))
                    pem = s.cnt["pe_aw"]
                    s.op("dve", lambda e, ps=ps, mb=mb, at=at: e.tensor_tensor(
                        at[:], ps[:], mb[:], op=ALU.mult),
                        wait=[("pe_aw", pem), ("ld_mk", mkm),
                              ("st_attn", max(0, 16 * (sc - 1)))],
                        inc=["ev_attn"])
                    evm = s.cnt["ev_attn"]
                    s.op("sp", lambda e, at=at, sc=sc: e.dma_start(
                        out=attn_loc.ap()[:, sc * 512:(sc + 1) * 512], in_=at[:]),
                        wait=[("ev_attn", evm)], inc=["st_attn"], dma=True)
            s.op("pool", lambda e: e.collective_compute(
                "AllGather", ALU.bypass,
                replica_groups=[list(range(NCORE))],
                ins=[attn_loc.ap()], outs=[attn_all.ap()]),
                wait=[("st_attn", 16 * 32)], inc=[("cc", 1)])
            s.barrier()

        if DEBUG_STAGE >= 3:
          with contextlib.ExitStack() as pes:
            abuf = [pes.enter_context(nc.sbuf_tensor(f"ab{i}", [128, DCN * 512], BF16))
                    for i in range(2)]
            ps_y = [pes.enter_context(nc.psum_tensor(f"psY{i}", [128, 512], F32))
                    for i in range(2)]
            b2b = [pes.enter_context(nc.sbuf_tensor(f"b2b{i}", [128, 512], F32))
                   for i in range(2)]
            tmp1 = [pes.enter_context(nc.sbuf_tensor(f"tmpA{i}", [128, 512], F32))
                    for i in range(2)]
            tmp2 = [pes.enter_context(nc.sbuf_tensor(f"tmpB{i}", [128, 512], F32))
                    for i in range(2)]
            for r in range(NCORE):
                buf = abuf[r % 2]
                s.op("sp", lambda e, buf=buf, r=r: e.dma_start(
                    out=buf[:], in_=attn_all.ap()[r * 128:(r + 1) * 128, :]),
                    wait=[("cc", 1)] + ([("pe_y", (r - 1) * TCN)]
                                        if r >= 2 else []),
                    inc=["ld_at"], dma=True)
                ldm = s.cnt["ld_at"]
                bb = b2b[r % 2]
                s.op("sp", lambda e, bb=bb, r=r: e.dma_start(
                    out=bb[:],
                    in_=b2[0:1, r * 512:(r + 1) * 512].partition_broadcast(128)),
                    wait=[("ep_y", max(0, (r - 1) * TCN))],
                    inc=["b2r"], dma=True)
                b2m = s.cnt["b2r"]
                bufv = buf.ap().rearrange("p (c t) -> p c t", c=DCN)
                for tc in range(TCN):
                    ps = ps_y[tc % 2]
                    for dc in range(DCN):
                        s.op("pe", lambda e, bufv=bufv, dc=dc, tc=tc, ps=ps: e.matmul(
                            ps[:], hT[:, dc * TOK + tc * 128: dc * TOK + (tc + 1) * 128],
                            bufv[:, dc, :],
                            start=(dc == 0), stop=(dc == DCN - 1)),
                            wait=([("ld_at", ldm),
                                   ("ep_y", max(0, s.cnt.get("ep_y", 0) - 1))]
                                  if dc == 0 else []),
                            inc=(["pe_y"] if dc == DCN - 1 else []))
                    pem = s.cnt["pe_y"]
                    t1, t2 = tmp1[tc % 2], tmp2[tc % 2]
                    s.op("dve", lambda e, ps=ps, t1=t1, bb=bb: e.tensor_tensor(
                        t1[:], ps[:], bb[:], op=ALU.add),
                        wait=[("pe_y", pem), ("b2r", b2m),
                              ("ge_y", max(0, s.cnt.get("ge_y", 0) - 1))],
                        inc=["ep_y"])
                    epm = s.cnt["ep_y"]
                    s.op("act", lambda e, t1=t1, t2=t2: e.activation(
                        t2[:], t1[:], AF.Gelu),
                        wait=[("ep_y", epm),
                              ("rs_y", max(0, s.cnt.get("rs_y", 0) - 1))],
                        inc=["ge_y"])
                    gem = s.cnt["ge_y"]
                    xsl = x_sb[:, tc * DM + r * 512: tc * DM + (r + 1) * 512]
                    s.op("dve", lambda e, xsl=xsl, t2=t2: e.tensor_tensor(
                        xsl, xsl, t2[:], op=ALU.add),
                        wait=[("ge_y", gem)], inc=["rs_y"])
            s.barrier()

        def phase_ffn(fi, xt):
            t = "f_"
            tn = f"f{fi}_"
            b_rt = s.cnt.get(t + "rt", 0)
            b_la = s.cnt.get(t + "la", 0)
            b_lb = s.cnt.get(t + "lb", 0)
            b_lgev = s.cnt.get("lgev" + t, 0)
            b_evmt = s.cnt.get("ev_mt" + t, 0)
            b_peb = s.cnt.get("pe_b" + t, 0)
            with contextlib.ExitStack() as pes:
                hid = [pes.enter_context(
                    nc.sbuf_tensor(f"hid{tn}{i}", [128, DM], BF16)) for i in range(2)]
                sqs = [pes.enter_context(
                    nc.sbuf_tensor(f"sq{tn}{i}", [128, DM], F32)) for i in range(2)]
                ps_tr = [pes.enter_context(
                    nc.psum_tensor(f"psTr{tn}{i}", [128, 512], BF16)) for i in range(2)]
                phase_ln(xt, hid, ps_tr, sqs)
            if DEBUG_STAGE == 4 and fi == 0:
                s.op("pool", lambda e: e.dma_start(
                    out=dbg, in_=hT[:, 0:TCN * 128]),
                    inc=["st_dbg"], dma=True)
                s.op("pool", lambda e: e.dma_start(
                    out=dbgb[:, 0:48], in_=stat[:]),
                    inc=["st_dbg"], dma=True)
            with contextlib.ExitStack() as pes:
                la_sb = pes.enter_context(
                    nc.sbuf_tensor(tn + "la", [128, DCN * 128], BF16))
                lb_sb = pes.enter_context(
                    nc.sbuf_tensor(tn + "lb", [128, DM], BF16))
                rt_sb = pes.enter_context(
                    nc.sbuf_tensor(tn + "rt", [128, DCN * 128], BF16))
                midT = pes.enter_context(
                    nc.sbuf_tensor(tn + "midT", [128, TCN * 128], BF16))
                lg = pes.enter_context(nc.sbuf_tensor(tn + "lg", [128, TCN * 8], F32))
                gt = pes.enter_context(nc.sbuf_tensor(tn + "gt", [128, 8], F32))
                e1b = pes.enter_context(nc.sbuf_tensor(tn + "e1", [128, 8], F32))
                e2b = pes.enter_context(nc.sbuf_tensor(tn + "e2", [128, 8], F32))
                tmb = pes.enter_context(nc.sbuf_tensor(tn + "tm", [128, 8], F32))
                wde = pes.enter_context(nc.sbuf_tensor(tn + "wd", [128, TCN * 8], F32))
                mg = [pes.enter_context(
                    nc.sbuf_tensor(f"{tn}mg{i}", [128, 128], BF16)) for i in range(2)]
                bb = [pes.enter_context(
                    nc.sbuf_tensor(f"{tn}bb{i}", [128, DCN * 256], BF16))
                    for i in range(2)]
                ps_log = pes.enter_context(nc.psum_tensor(tn + "psL", [128, 64], F32))
                ps_mid = [pes.enter_context(
                    nc.psum_tensor(f"{tn}psM{i}", [128, 128], F32)) for i in range(2)]
                ps_mT = [pes.enter_context(
                    nc.psum_tensor(f"{tn}psMT{i}", [128, 128], BF16)) for i in range(2)]
                ps_b = [pes.enter_context(
                    nc.psum_tensor(f"{tn}psB{i}", [128, 256], F32)) for i in range(2)]

                LA, LB, RT, G = t + "la", t + "lb", t + "rt", t + "gg"
                s.op("pool", lambda e: e.dma_start(out=la_sb[:], in_=loraa_t),
                     inc=[LA], dma=True)
                s.op("pool", lambda e: e.dma_start(out=rt_sb[:], in_=routertp),
                     inc=[RT], dma=True)
                s.op("pool", lambda e: e.dma_start(out=lb_sb[:], in_=lorab_f),
                     inc=[LB], dma=True)

                for tc in range(TCN if DEBUG_STAGE >= 5 else 0):
                    pslg = ps_log[:, tc * 8:(tc + 1) * 8]
                    for dc in range(DCN):
                        s.op("pe", lambda e, dc=dc, tc=tc, pslg=pslg: e.matmul(
                            pslg, hT[:, dc * TOK + tc * 128: dc * TOK + (tc + 1) * 128],
                            rt_sb[:, dc * 128: dc * 128 + 8],
                            start=(dc == 0), stop=(dc == DCN - 1)),
                            wait=([(RT, b_rt + 16), ("lgev" + t, b_lgev + tc)]
                                  if dc == 0 else []),
                            inc=(["pe_lg" + t] if dc == DCN - 1 else []))
                    lgm = s.cnt["pe_lg" + t]
                    psm = ps_mid[tc % 2]
                    for dc in range(DCN):
                        s.op("pe", lambda e, dc=dc, tc=tc, psm=psm: e.matmul(
                            psm[:], hT[:, dc * TOK + tc * 128: dc * TOK + (tc + 1) * 128],
                            la_sb[:, dc * 128:(dc + 1) * 128],
                            start=(dc == 0), stop=(dc == DCN - 1)),
                            wait=([(LA, b_la + 16),
                                   ("em_" + t, max(0, s.cnt.get("em_" + t, 0) - 1))]
                                  if dc == 0 else []),
                            inc=(["pe_md" + t] if dc == DCN - 1 else []))
                    mdm = s.cnt["pe_md" + t]

                    def g(eng, fn):
                        s.op(eng, fn, wait=[(G, s.cnt.get(G, 0))], inc=[G])
                    lgc = lg[:, tc * 8:(tc + 1) * 8]
                    s.op("dve", lambda e, pslg=pslg, lgc=lgc: e.tensor_copy(lgc, pslg),
                         wait=[("pe_lg" + t, lgm), (G, s.cnt.get(G, 0))],
                         inc=["lgev" + t])
                    s.op("dve", lambda e, lgc=lgc: e.reduce_max(
                        gt[:, 0:1], lgc, axis=AX),
                        wait=[("lgev" + t, None)], inc=[G])
                    g("dve", lambda e, lgc=lgc: e.tensor_scalar(
                        e1b[:], lgc, gt[:, 0:1], None, op0=ALU.is_equal))
                    g("dve", lambda e, lgc=lgc: e.scalar_tensor_tensor(
                        tmb[:], e1b[:], -1e30, lgc, op0=ALU.mult, op1=ALU.add))
                    g("dve", lambda e: e.reduce_max(gt[:, 1:2], tmb[:], axis=AX))
                    g("dve", lambda e: e.tensor_scalar(
                        e2b[:], tmb[:], gt[:, 1:2], None, op0=ALU.is_equal))
                    g("dve", lambda e: e.tensor_tensor(
                        gt[:, 4:5], gt[:, 1:2], gt[:, 0:1], op=ALU.subtract))
                    g("act", lambda e: e.activation(gt[:, 3:4], gt[:, 4:5], AF.Sigmoid))
                    g("dve", lambda e: e.tensor_scalar(
                        gt[:, 2:3], gt[:, 3:4], -2.0, 2.0, op0=ALU.mult, op1=ALU.add))
                    g("dve", lambda e: e.tensor_scalar(
                        gt[:, 3:4], gt[:, 3:4], 2.0, None, op0=ALU.mult))
                    g("dve", lambda e: e.tensor_scalar(
                        e1b[:], e1b[:], gt[:, 2:3], None, op0=ALU.mult))
                    g("dve", lambda e: e.tensor_scalar(
                        e2b[:], e2b[:], gt[:, 3:4], None, op0=ALU.mult))
                    wdc = wde[:, tc * 8:(tc + 1) * 8]
                    g("dve", lambda e: e.tensor_tensor(
                        e1b[:], e1b[:], e2b[:], op=ALU.add))
                    g("dve", lambda e, wdc=wdc: e.tensor_copy(wdc, e1b[:]))
                    gl = s.cnt[G]

                    mgt = mg[tc % 2]
                    wrep = wde.ap()[:, tc * 8:(tc + 1) * 8].unsqueeze(2) \
                        .broadcast_to([128, 8, 16])
                    s.op("dve", lambda e, psm=psm, mgt=mgt, wrep=wrep: e.tensor_tensor(
                        mgt.ap().rearrange("p (e r) -> p e r", e=8),
                        psm.ap().rearrange("p (e r) -> p e r", e=8),
                        wrep, op=ALU.mult),
                        wait=[("pe_md" + t, mdm), (G, gl),
                              ("pe_mt" + t, max(0, s.cnt.get("pe_mt" + t, 0) - 1))],
                        inc=["em_" + t])
                    emm = s.cnt["em_" + t]
                    psl = ps_mT[tc % 2][:]
                    s.op("pe", lambda e, mgt=mgt, psl=psl: e.transpose(
                        psl, mgt[:], id_bf[:]),
                        wait=[("em_" + t, emm),
                              ("ev_mt" + t, max(0, s.cnt.get("ev_mt" + t, 0) - 1))],
                        inc=["pe_mt" + t])
                    ptm = s.cnt["pe_mt" + t]
                    s.op("act", lambda e, tc=tc, psl=psl: e.activation(
                        midT[:, tc * 128:(tc + 1) * 128], psl, AF.Copy),
                        wait=[("pe_mt" + t, ptm)], inc=["ev_mt" + t])

                if DEBUG_STAGE == 5 and fi == 0:
                    s.op("pool", lambda e: e.dma_start(out=dbg, in_=midT[:]),
                         wait=[("ev_mt" + t, TCN)], inc=["st_dbg"], dma=True)
                    s.op("pool", lambda e: e.dma_start(
                        out=dbgb[:, 0:64], in_=lg[:]),
                        wait=[("lgev" + t, TCN)], inc=["st_dbg"], dma=True)
                    s.op("pool", lambda e: e.dma_start(
                        out=dbgb[:, 64:128], in_=wde[:]),
                        wait=[(G, None)], inc=["st_dbg"], dma=True)

                for rb in range(16 if DEBUG_STAGE >= 6 else 0):
                    buf = bb[rb % 2]
                    bv = buf.ap().rearrange("p (c o) -> p c o", c=DCN)
                    s.op("pool", lambda e, buf=buf, rb=rb: e.dma_start(
                        out=buf[:], in_=base_wt[rb]),
                        wait=([("pe_b" + t, b_peb + 8 * (rb - 1))]
                              if rb >= 2 else []),
                        inc=["ld_b" + t], dma=True)
                    ldm = s.cnt["ld_b" + t]
                    use_inc = os.environ.get("KNOINC", "0") != "1"
                    for tc in range(TCN):
                        ps = ps_b[tc % 2]
                        for dc in range(DCN):
                            s.op("pe", lambda e, bv=bv, dc=dc, tc=tc, ps=ps: e.matmul(
                                ps[:], hT[:, dc * TOK + tc * 128: dc * TOK + (tc + 1) * 128],
                                bv[:, dc, :],
                                start=(dc == 0),
                                stop=(False if use_inc else (dc == DCN - 1))),
                                wait=([("ld_b" + t, ldm), (LB, b_lb + 16),
                                       ("ev_mt" + t,
                                        (b_evmt + TCN) if DEBUG_STAGE >= 5 else 0),
                                       ("rs_b" + t,
                                        max(0, s.cnt.get("rs_b" + t, 0) - 1))]
                                      if dc == 0 else []))
                        if use_inc:
                            s.op("pe", lambda e, tc=tc, rb=rb, ps=ps: e.matmul(
                                ps[:], midT[:, tc * 128:(tc + 1) * 128],
                                lb_sb[:, rb * 256:(rb + 1) * 256],
                                start=False, stop=True),
                                inc=["pe_b" + t])
                        else:
                            s.op("pe", lambda e: (e.drain(), e.nop())[1],
                                 inc=["pe_b" + t])
                        pbm = s.cnt["pe_b" + t]
                        xsl = xt[:, tc * DM + rb * 256: tc * DM + (rb + 1) * 256]
                        s.op("dve", lambda e, xsl=xsl, ps=ps: e.tensor_tensor(
                            xsl, xsl, ps[:], op=ALU.add),
                            wait=[("pe_b" + t, pbm)], inc=["rs_b" + t])
                s.barrier()

        if DEBUG_STAGE >= 4 and N_SUBLAYERS >= 2:
            phase_ffn(0, x_sb)

        def phase_attn_b_groups():
          with contextlib.ExitStack() as pes:
            T = pes.enter_context
            wqb = [T(nc.sbuf_tensor(f"b_wq{i}", [128, DCN * 128], BF16))
                   for i in range(2)]
            wkb = T(nc.sbuf_tensor("b_wk", [128, DCN * 128], BF16))
            wvb = T(nc.sbuf_tensor("b_wv", [128, DCN * 128], BF16))
            cq = T(nc.sbuf_tensor("b_cq", [128, TOK], BF16))
            sqt = T(nc.sbuf_tensor("b_sqt", [128, TOK], BF16))
            ck = T(nc.sbuf_tensor("b_ck", [128, TOK], BF16))
            skt = T(nc.sbuf_tensor("b_skt", [128, TOK], BF16))
            rot_bf = T(nc.sbuf_tensor("b_rot", [128, 128], BF16))
            swm_sb = T(nc.sbuf_tensor("b_swm", [128, 256], F32))
            swm0_sb = T(nc.sbuf_tensor("b_swm0", [128, 256], F32))
            tri_sb = T(nc.sbuf_tensor("b_tri", [128, 128], F32))
            gate_sb = T(nc.sbuf_tensor("b_gate", [128, 1], F32))
            ones128 = T(nc.sbuf_tensor("b_ones", [128, 128], F32))
            qx = T(nc.sbuf_tensor("b_qx", [128, 4 * TOK], BF16))
            qraw = T(nc.sbuf_tensor("b_qraw", [128, 4 * TOK], BF16))
            sq_sb = T(nc.sbuf_tensor("b_sq", [128, 4 * TOK], BF16))
            qtl = qx
            kraw = T(nc.sbuf_tensor("b_kraw", [128, TOK], BF16))
            kx = T(nc.sbuf_tensor("b_kx", [128, TOK], BF16))
            ktl = T(nc.sbuf_tensor("b_ktl", [128, TOK], BF16))
            khat = T(nc.sbuf_tensor("b_khat", [128, TOK], BF16))
            khtok = T(nc.sbuf_tensor("b_khtok", [128, TOK], BF16))
            sk_ext = T(nc.sbuf_tensor("b_skext", [128, 1152], BF16))
            v_ext = T(nc.sbuf_tensor("b_vext", [128, 1152], BF16))
            v_tok = T(nc.sbuf_tensor("b_vtok", [128, 8 * 128], BF16))
            v_win = T(nc.sbuf_tensor("b_vwin", [128, 9 * 128], BF16))
            g_sb = T(nc.sbuf_tensor("b_g", [128, TOK], F32))
            a_sb = T(nc.sbuf_tensor("b_a", [128, TOK], F32))
            eA = T(nc.sbuf_tensor("b_eA", [128, TOK], F32))
            eNA = T(nc.sbuf_tensor("b_eNA", [128, TOK], F32))
            tmpA = T(nc.sbuf_tensor("b_tmpA", [128, 512], F32))
            tmpB = T(nc.sbuf_tensor("b_tmpB", [128, 512], F32))
            tmpC = T(nc.sbuf_tensor("b_tmpC", [128, TOK], F32))
            zrow = T(nc.sbuf_tensor("b_zrow", [1, 1024], F32))
            zrow2 = T(nc.sbuf_tensor("b_zrow2", [1, 1024], F32))
            S_state = T(nc.sbuf_tensor("b_S", [128, 128], F32))
            S0st = T(nc.sbuf_tensor("b_S0", [128, 8 * 128], BF16))
            sprev = T(nc.sbuf_tensor("b_sprev", [128, 128], F32))
            exch_sb = T(nc.sbuf_tensor("b_exch", [128, 256], F32))
            aoff = T(nc.sbuf_tensor("b_aoff", [128, 16], F32))
            o_sb = T(nc.sbuf_tensor("b_o", [128, 4 * TOK], BF16))
            ePf = T(nc.sbuf_tensor("b_ePf", [128, 256], F32))
            ePm = T(nc.sbuf_tensor("b_ePm", [128, 256], F32))
            ePb = T(nc.sbuf_tensor("b_ePb", [128, 256], BF16))
            zsw = T(nc.sbuf_tensor("b_zsw", [128, 4], F32))
            PTb = [T(nc.sbuf_tensor(f"b_PT{i}", [128, 128], BF16)) for i in range(2)]
            ePT = [T(nc.sbuf_tensor(f"b_ePT{i}", [128, 256], BF16)) for i in range(2)]
            p0 = T(nc.psum_tensor("bp0", [128, 512], F32))
            p1 = T(nc.psum_tensor("bp1", [128, 512], F32))
            p2 = T(nc.psum_tensor("bp2", [128, 512], F32))
            p3 = T(nc.psum_tensor("bp3", [128, 512], F32))
            p4 = T(nc.psum_tensor("bp4", [128, 512], F32))
            p5 = T(nc.psum_tensor("bp5", [128, 512], F32))
            p6 = T(nc.psum_tensor("bp6", [128, 512], F32))
            p7 = T(nc.psum_tensor("bp7", [128, 512], BF16))

            for nm, dst, src in (("cq", cq, cosq), ("sqt", sqt, sinq),
                                 ("ck", ck, cosk), ("skt", skt, sink),
                                 ("rotb", rot_bf, rotm)):
                s.op("pool", lambda e, dst=dst, src=src: e.dma_start(
                    out=dst[:], in_=src), inc=["b_cst"], dma=True)
            for dst, src in ((swm_sb, swmask), (swm0_sb, swmask0),
                             (tri_sb, trimask), (gate_sb, gate)):
                s.op("sp", lambda e, dst=dst, src=src: e.dma_start(
                    out=dst[:], in_=src), inc=["b_cst"], dma=True)
            s.op("dve", lambda e: e.memset(ones128[:], 1.0), inc=["b_cst1"])
            CST = [("b_cst", 9 * 16), ("b_cst1", 1), ("const_c", None)]


            for gidx in range(KBGRP):
                tg = "b"
                oev_pg = s.cnt.get("b_oev", 0)
                swe_pg = s.cnt.get("b_swe", 0)
                s.op("pool", lambda e, gidx=gidx: e.dma_start(
                    out=wkb[:], in_=wk[gidx]),
                    wait=[("b_kp", s.cnt.get("b_kp", 0))],
                    inc=["b_ldk"], dma=True)
                s.op("pool", lambda e, gidx=gidx: e.dma_start(
                    out=wvb[:], in_=wv[gidx]),
                    wait=[("b_vp", s.cnt.get("b_vp", 0))],
                    inc=["b_ldv"], dma=True)
                ldk = s.cnt["b_ldk"]
                ldv = s.cnt["b_ldv"]
                s.op("dve", lambda e: e.memset(sk_ext[:, 1088:1152], 0.0),
                     wait=[("b_sc", s.cnt.get("b_sc", 0)),
                           ("b_tr", s.cnt.get("b_tr", 0)),
                           ("b_ex", s.cnt.get("b_ex", 0))], inc=["b_z0"])
                s.op("dve", lambda e: e.memset(v_ext[:, 1088:1152], 0.0),
                     inc=["b_z0"])
                for tt in range(2 if KBK >= 1 else 0):
                    ps = [p0, p1][tt]
                    for dc in range(DCN):
                        s.op("pe", lambda e, dc=dc, tt=tt, ps=ps: e.matmul(
                            ps[:], wkb[:, dc * 128:(dc + 1) * 128],
                            hT[:, dc * TOK + tt * 512: dc * TOK + tt * 512 + 512],
                            start=(dc == 0), stop=(dc == DCN - 1)),
                            wait=([("b_ldk", ldk), ("b_oev", oev_pg)]
                                  if dc == 0 else []),
                            inc=(["b_kp"] if dc == DCN - 1 else []))
                    kpm = s.cnt["b_kp"]
                    if os.environ.get("KNOEV", "0") == "1":
                        continue
                    s.op("dve", lambda e, tt=tt, ps=ps: e.tensor_copy(
                        kraw[:, tt * 512:(tt + 1) * 512], ps[:]),
                        wait=[("b_kp", kpm), ("b_g", s.cnt.get("b_g", 0))],
                        inc=["b_kev"])
                    s.op("act", lambda e, tt=tt: e.activation(
                        kx[:, tt * 512:(tt + 1) * 512],
                        kraw[:, tt * 512:(tt + 1) * 512], AF.Exp),
                        wait=[("b_kev", s.cnt.get("b_kev", 0)),
                              ("b_kt", s.cnt.get("b_kt", 0))],
                        inc=["b_kev"])
                for tt in range(2 if KBK >= 1 else 0):
                    ps = [p0, p1][tt]
                    for dc in range(DCN):
                        s.op("pe", lambda e, dc=dc, tt=tt, ps=ps: e.matmul(
                            ps[:], wvb[:, dc * 128:(dc + 1) * 128],
                            hT[:, dc * TOK + tt * 512: dc * TOK + tt * 512 + 512],
                            start=(dc == 0), stop=(dc == DCN - 1)),
                            wait=([("b_ldv", ldv), ("b_kev", s.cnt.get("b_kev", 0))]
                                  if dc == 0 else []),
                            inc=(["b_vp"] if dc == DCN - 1 else []))
                    vpm = s.cnt["b_vp"]
                    if os.environ.get("KNOEV", "0") == "1":
                        continue
                    s.op("dve", lambda e, tt=tt, ps=ps: e.tensor_copy(
                        v_ext[:, 64 + tt * 512: 64 + (tt + 1) * 512], ps[:]),
                        wait=[("b_vp", vpm), ("b_tr", s.cnt.get("b_tr", 0)),
                              ("b_ex", s.cnt.get("b_ex", 0))],
                        inc=["b_vev"])
                vev = s.cnt.get("b_vev", 0)
                for tt in range(2 if KBK >= 2 else 0):
                    s.op("pe", lambda e, tt=tt: e.matmul(
                        p2[:], rot_bf[:],
                        kraw[:, tt * 512:(tt + 1) * 512], start=True, stop=True),
                        wait=[("b_kev", s.cnt.get("b_kev", 0)),
                              ("b_rpe", s.cnt.get("b_rpe", 0))] + CST,
                        inc=["b_rot"])
                    rm = s.cnt["b_rot"]
                    s.op("dve", lambda e, tt=tt: e.tensor_tensor(
                        tmpA[:, 0:512], kraw[:, tt * 512:(tt + 1) * 512],
                        ck[:, tt * 512:(tt + 1) * 512], op=ALU.mult),
                        wait=[("b_rpe2", s.cnt.get("b_rpe2", 0))], inc=["b_rpe2"])
                    s.op("dve", lambda e, tt=tt: e.tensor_tensor(
                        tmpB[:, 0:512], p2[:, 0:512],
                        skt[:, tt * 512:(tt + 1) * 512], op=ALU.mult),
                        wait=[("b_rot", rm)], inc=["b_rpe"])
                    s.op("dve", lambda e, tt=tt: e.tensor_tensor(
                        sk_ext[:, 64 + tt * 512: 64 + (tt + 1) * 512],
                        tmpA[:, 0:512], tmpB[:, 0:512], op=ALU.add),
                        wait=[("b_rpe", None), ("b_rpe2", None),
                              ("b_sc", s.cnt.get("b_sc", 0)),
                              ("b_ex", s.cnt.get("b_ex", 0))], inc=["b_rpe2"])
                GS = "b_g"

                def gs(eng, fn):
                    if KBK >= 3:
                        s.op(eng, fn, wait=[(GS, s.cnt.get(GS, 0))], inc=[GS])
                if KBK >= 3:
                  s.op("act", lambda e: e.activation(tmpC[:], kraw[:], AF.Abs),
                     wait=[("b_kev", s.cnt.get("b_kev", 0)),
                           (GS, s.cnt.get(GS, 0)),
                           ("b_qt", s.cnt.get("b_qt", 0)),
                           ("b_chn", s.cnt.get("b_chn", 0)),
                           ("b_fold", s.cnt.get("b_fold", 0)),
                           ("b_kh", s.cnt.get("b_kh", 0))], inc=[GS])
                gs("act", lambda e: e.activation(a_sb[:], tmpC[:], AF.Exp, scale=-1.0))
                gs("act", lambda e: e.activation(tmpC[:], a_sb[:], AF.Ln, bias=1.0))
                gs("dve", lambda e: e.tensor_scalar(
                    a_sb[:], kraw[:], 0.0, None, op0=ALU.min))
                gs("dve", lambda e: e.tensor_tensor(
                    g_sb[:], a_sb[:], tmpC[:], op=ALU.subtract))
                for ch in range(8):
                    gs("dve", lambda e, ch=ch: e.tensor_tensor_scan(
                        a_sb[:, ch * 128:(ch + 1) * 128], ones128[:],
                        g_sb[:, ch * 128:(ch + 1) * 128], 0.0,
                        op0=ALU.mult, op1=ALU.add))
                gs("dve", lambda e: e.tensor_scalar(
                    a_sb[:], a_sb[:], -1280.0, None, op0=ALU.max))
                gs("act", lambda e: e.activation(eA[:], a_sb[:], AF.Exp,
                                                 scale=1.0 / 16.0))
                gs("act", lambda e: e.activation(eNA[:], a_sb[:], AF.Exp,
                                                 scale=-1.0 / 16.0))
                gs("dve", lambda e: e.memset(aoff[:, 0:1], 0.0))
                for ch in range(1, 8):
                    gs("dve", lambda e, ch=ch: e.tensor_tensor(
                        aoff[:, ch:ch + 1], aoff[:, ch - 1:ch],
                        a_sb[:, ch * 128 - 1:ch * 128], op=ALU.add))
                gs("act", lambda e: e.activation(aoff[:, 8:16], aoff[:, 0:8],
                                                 AF.Exp, scale=1.0 / 16.0))
                gsm = s.cnt.get(GS, 0)
                for tt in range(2 if KBK >= 4 else 0):
                    s.op("pe", lambda e, tt=tt: e.matmul(
                        p3[0:1, 0:512],
                        ones_bf[:], kx[:, tt * 512:(tt + 1) * 512],
                        start=True, stop=True),
                        wait=[("b_kev", s.cnt.get("b_kev", 0)),
                              ("b_zev", s.cnt.get("b_zev", 0)),
                              ("b_swa", s.cnt.get("b_swa", 0))],
                        inc=["b_zk"])
                    zkm = s.cnt["b_zk"]
                    s.op("dve", lambda e, tt=tt: e.tensor_copy(
                        zrow[0:1, tt * 512:(tt + 1) * 512], p3[0:1, 0:512]),
                        wait=[("b_zk", zkm)], inc=["b_zev"])
                if KBK >= 4:
                    s.op("dve", lambda e: e.reciprocal(zrow2[0:1, :], zrow[0:1, :]),
                         wait=[("b_zev", None)], inc=["b_zev"])
                if KBK >= 5:
                    s.op("dve", lambda e: e.tensor_tensor(
                        tmpC[:], kx[:], eNA[:], op=ALU.mult),
                        wait=[(GS, gsm)], inc=["b_kt"])
                    for tt in range(2):
                        s.op("pe", lambda e, tt=tt: e.matmul(
                            p2[:], ones_row[:],
                            zrow2[0:1, tt * 512:(tt + 1) * 512],
                            start=True, stop=True),
                            wait=[("b_zev", s.cnt.get("b_zev", 0)),
                                  ("b_rpe", s.cnt.get("b_rpe", 0)),
                                  ("b_kt", s.cnt.get("b_kt", 0))],
                            inc=["b_rot"])
                        s.op("dve", lambda e, tt=tt: e.tensor_tensor(
                            ktl[:, tt * 512:(tt + 1) * 512],
                            tmpC[:, tt * 512:(tt + 1) * 512], p2[:, 0:512],
                            op=ALU.mult),
                            wait=[("b_rot", s.cnt.get("b_rot", 0)),
                                  ("b_P", s.cnt.get("b_P", 0))], inc=["b_kt"])
                ktm = s.cnt.get("b_kt", 0)
                for ch in range(8 if KBK >= 5 else 0):
                    s.op("dve", lambda e, ch=ch: e.tensor_scalar(
                        khat[:, ch * 128:(ch + 1) * 128],
                        ktl[:, ch * 128:(ch + 1) * 128],
                        eA[:, ch * 128 + 127: ch * 128 + 128], None, op0=ALU.mult),
                        wait=[("b_kt", ktm), ("b_tr", s.cnt.get("b_tr", 0))]
                        if ch == 0 else [("b_kt", ktm)],
                        inc=["b_kh"])
                khm = s.cnt.get("b_kh", 0)
                trjobs = []
                for ch in range(8):
                    trjobs.append((khat.ap()[:, ch * 128:(ch + 1) * 128],
                                   khtok.ap()[:, ch * 128:(ch + 1) * 128],
                                   ("b_kh", khm)))
                for ch in range(8):
                    trjobs.append((v_ext.ap()[:, 64 + ch * 128: 64 + (ch + 1) * 128],
                                   v_tok.ap()[:, ch * 128:(ch + 1) * 128],
                                   ("b_vev", vev)))
                for j in range(1, 9):
                    trjobs.append((v_ext.ap()[:, j * 128:(j + 1) * 128],
                                   v_win.ap()[:, j * 128:(j + 1) * 128],
                                   ("b_vev", vev)))
                for j0 in range(0, len(trjobs) if KBK >= 6 else 0, 4):
                    grp = trjobs[j0:j0 + 4]
                    for ji, (src, dst, wt) in enumerate(grp):
                        s.op("pe", lambda e, src=src, ji=ji: e.transpose(
                            p7[:, ji * 128:(ji + 1) * 128], src, id_bf[:]),
                            wait=[wt, ("b_tev", s.cnt.get("b_tev", 0))]
                            if ji == 0 else [wt],
                            inc=(["b_tr"] if ji == len(grp) - 1 else []))
                    trm = s.cnt["b_tr"]
                    for ji, (src, dst, wt) in enumerate(grp):
                        s.op("dve", lambda e, dst=dst, ji=ji: e.tensor_copy(
                            dst, p7[:, ji * 128:(ji + 1) * 128]),
                            wait=[("b_tr", trm)],
                            inc=(["b_tev"] if ji == len(grp) - 1 else []))
                tevm = s.cnt.get("b_tev", 0)
                if KBK >= 7:
                    s.op("dve", lambda e: e.memset(S_state[:], 0.0),
                         wait=[("b_chn", s.cnt.get("b_chn", 0)),
                               ("b_oacc", s.cnt.get("b_oacc", 0))], inc=["b_chn"])
                for ch in range(8 if KBK >= 7 else 0):
                    psD = [p5, p6][ch // 4]
                    sl = (ch % 4) * 128
                    s.op("pe", lambda e, ch=ch, psD=psD, sl=sl: e.matmul(
                        psD[:, sl:sl + 128], khtok[:, ch * 128:(ch + 1) * 128],
                        v_tok[:, ch * 128:(ch + 1) * 128],
                        start=True, stop=True),
                        wait=([("b_tev", tevm), ("b_Pe", s.cnt.get("b_Pe", 0)),
                               ("b_chn", s.cnt.get("b_chn", 0))]
                              if ch == 0 else []),
                        inc=(["b_dS"] if ch == 7 else []))
                dsm = s.cnt.get("b_dS", 0)
                for ch in range(8 if KBK >= 7 else 0):
                    psD = [p5, p6][ch // 4]
                    sl = (ch % 4) * 128
                    s.op("dve", lambda e, ch=ch: e.tensor_copy(
                        S0st[:, ch * 128:(ch + 1) * 128], S_state[:]),
                        wait=[("b_dS", dsm), ("b_chn", s.cnt.get("b_chn", 0)),
                              ("b_fold", s.cnt.get("b_fold", 0))], inc=["b_chn"])
                    s.op("dve", lambda e, ch=ch, psD=psD, sl=sl:
                         e.scalar_tensor_tensor(
                        S_state[:], S_state[:],
                        eA[:, ch * 128 + 127: ch * 128 + 128],
                        psD[:, sl:sl + 128], op0=ALU.mult, op1=ALU.add),
                        wait=[("b_chn", s.cnt.get("b_chn", 0))],
                        inc=["b_chn"])
                chm = s.cnt.get("b_chn", 0)
                if KBK >= 8:
                  s.op("pool", lambda e, gidx=gidx: e.dma_start(
                    out=exch_loc.ap()[gidx, :, 0:128], in_=S_state[:]),
                    wait=[("b_chn", chm)], inc=["b_ex"], dma=True)
                  s.op("pool", lambda e, gidx=gidx: e.dma_start(
                    out=exch_loc.ap()[gidx, :, 128:192],
                    in_=sk_ext[:, 1024:1088]),
                    wait=[("b_rpe2", None)], inc=["b_ex"], dma=True)
                  s.op("pool", lambda e, gidx=gidx: e.dma_start(
                    out=exch_loc.ap()[gidx, :, 192:256],
                    in_=v_ext[:, 1024:1088]),
                    wait=[("b_vev", vev)], inc=["b_ex"], dma=True)
                  s.op("pool", lambda e, gidx=gidx: e.collective_compute(
                    "AllGather", ALU.bypass,
                    replica_groups=[[0, 1], [2, 3], [4, 5], [6, 7]],
                    ins=[exch_loc.ap()[gidx]], outs=[exch_pair.ap()[gidx]]),
                    wait=[("b_ex", None)], inc=[("ccb", 1)])

                qtms = [0, 0, 0, 0]
                rpems = [0, 0, 0, 0]
                for h in range(4 if KBSUB >= 2 else 0):
                    wqi = wqb[h % 2]
                    s.op("pool", lambda e, h=h, gidx=gidx, wqi=wqi: e.dma_start(
                        out=wqi[:], in_=wq[gidx * 4 + h]),
                        wait=[("b_qp", max(0, s.cnt.get("b_qp", 0) - 2))],
                        inc=["b_ldq"], dma=True)
                    ldq = s.cnt["b_ldq"]
                    for tt in range(2):
                        ps = [p0, p1][(h * 2 + tt) % 2]
                        for dc in range(DCN):
                            s.op("pe", lambda e, dc=dc, tt=tt, ps=ps, wqi=wqi:
                                 e.matmul(
                                ps[:], wqi[:, dc * 128:(dc + 1) * 128],
                                hT[:, dc * TOK + tt * 512: dc * TOK + tt * 512 + 512],
                                start=(dc == 0), stop=(dc == DCN - 1)),
                                wait=([("b_ldq", ldq),
                                       ("b_qt", s.cnt.get("b_qt", 0)),
                                       ("b_vev", vev),
                                       ("b_qev", max(0, s.cnt.get("b_qev", 0) - 2))]
                                      if dc == 0 else []),
                                inc=(["b_qp"] if dc == DCN - 1 else []))
                        qpm = s.cnt["b_qp"]
                        col = h * TOK + tt * 512
                        s.op("dve", lambda e, col=col, ps=ps: e.tensor_copy(
                            qraw[:, col:col + 512], ps[:]),
                            wait=[("b_qp", qpm)], inc=["b_qev"])
                        s.op("act", lambda e, col=col: e.activation(
                            qx[:, col:col + 512],
                            qraw[:, col:col + 512], AF.Exp),
                            wait=[("b_qev", s.cnt.get("b_qev", 0))],
                            inc=["b_qev"])
                    for tt in range(2):
                        col = h * TOK + tt * 512
                        s.op("pe", lambda e, col=col: e.matmul(
                            p2[:], rot_bf[:], qraw[:, col:col + 512],
                            start=True, stop=True),
                            wait=[("b_qev", s.cnt.get("b_qev", 0)),
                                  ("b_kt", s.cnt.get("b_kt", 0)),
                                  ("b_qt", s.cnt.get("b_qt", 0)),
                                  ("b_rpe", s.cnt.get("b_rpe", 0))],
                            inc=["b_rot"])
                        rm = s.cnt["b_rot"]
                        s.op("dve", lambda e, col=col, tt=tt: e.tensor_tensor(
                            tmpA[:, 0:512], qraw[:, col:col + 512],
                            cq[:, tt * 512:(tt + 1) * 512], op=ALU.mult),
                            wait=[("b_rpe2", s.cnt.get("b_rpe2", 0))],
                            inc=["b_rpe2"])
                        s.op("dve", lambda e, col=col, tt=tt: e.tensor_tensor(
                            tmpB[:, 0:512], p2[:, 0:512],
                            sqt[:, tt * 512:(tt + 1) * 512], op=ALU.mult),
                            wait=[("b_rot", rm)], inc=["b_rpe"])
                        s.op("dve", lambda e, col=col: e.tensor_tensor(
                            sq_sb[:, col:col + 512],
                            tmpA[:, 0:512], tmpB[:, 0:512], op=ALU.add),
                            wait=[("b_rpe", None), ("b_rpe2", None)],
                            inc=["b_rpe2"])
                    rpems[h] = s.cnt.get("b_rpe2", 0)
                    for tt in range(2):
                        col = h * TOK + tt * 512
                        s.op("pe", lambda e, col=col, tt=tt: e.matmul(
                            p3[0:1, 0:512],
                            ones_bf[:], qx[:, col:col + 512],
                            start=True, stop=True),
                            wait=[("b_qev", s.cnt.get("b_qev", 0)),
                                  ("b_zev", s.cnt.get("b_zev", 0)),
                                  ("b_swa", s.cnt.get("b_swa", 0))],
                            inc=["b_zk"])
                        zkm = s.cnt["b_zk"]
                        s.op("dve", lambda e, tt=tt: e.tensor_copy(
                            zrow[0:1, tt * 512:(tt + 1) * 512], p3[0:1, 0:512]),
                            wait=[("b_zk", zkm)], inc=["b_zev"])
                    s.op("dve", lambda e: e.reciprocal(zrow2[0:1, :], zrow[0:1, :]),
                         wait=[("b_zev", None)], inc=["b_zev"])
                    s.op("dve", lambda e: e.tensor_scalar(
                        zrow2[0:1, :], zrow2[0:1, :], 0.5, None, op0=ALU.mult),
                        wait=[("b_zev", None)], inc=["b_zev"])
                    s.op("dve", lambda e, h=h: e.tensor_tensor(
                        tmpC[:], qx[:, h * TOK:(h + 1) * TOK], eA[:], op=ALU.mult),
                        wait=[(GS, gsm), ("b_kt", ktm),
                              ("b_qt", s.cnt.get("b_qt", 0))],
                        inc=["b_qt"])
                    for tt in range(2):
                        s.op("pe", lambda e, tt=tt: e.matmul(
                            p2[:], ones_row[:],
                            zrow2[0:1, tt * 512:(tt + 1) * 512],
                            start=True, stop=True),
                            wait=[("b_zev", s.cnt.get("b_zev", 0)),
                                  ("b_rpe", s.cnt.get("b_rpe", 0)),
                                  ("b_qt", s.cnt.get("b_qt", 0))],
                            inc=["b_rot"])
                        s.op("dve", lambda e, h=h, tt=tt: e.tensor_tensor(
                            qtl[:, h * TOK + tt * 512: h * TOK + (tt + 1) * 512],
                            tmpC[:, tt * 512:(tt + 1) * 512], p2[:, 0:512],
                            op=ALU.mult),
                            wait=[("b_rot", s.cnt.get("b_rot", 0)),
                                  ("b_zk", s.cnt.get("b_zk", 0)),
                                  ("b_P", s.cnt.get("b_P", 0))], inc=["b_qt"])
                    qtms[h] = s.cnt.get("b_qt", 0)
                qtm = s.cnt.get("b_qt", 0)

                if KBK >= 8:
                  s.op("sp", lambda e, gidx=gidx: e.dma_start(
                    out=exch_sb[:], in_=exch_pair.ap()[gidx][0:128, :]),
                    wait=[("ccb", gidx + 1),
                          ("b_fold", s.cnt.get("b_fold", 0))],
                    inc=["b_exl"], dma=True)
                exm = s.cnt.get("b_exl", 0)
                s.op("dve", lambda e: e.tensor_scalar(
                    sprev[:], exch_sb[:, 0:128], gate_sb[:, 0:1], None,
                    op0=ALU.mult),
                    wait=[("b_exl", exm)] + CST, inc=["b_fold"])
                s.op("dve", lambda e: e.tensor_scalar(
                    sk_ext[:, 0:64], exch_sb[:, 128:192], gate_sb[:, 0:1], None,
                    op0=ALU.mult),
                    wait=[("b_exl", exm), ("b_sc", s.cnt.get("b_sc", 0))],
                    inc=["b_fold"])
                s.op("dve", lambda e: e.tensor_scalar(
                    v_ext[:, 0:64], exch_sb[:, 192:256], gate_sb[:, 0:1], None,
                    op0=ALU.mult),
                    wait=[("b_exl", exm)], inc=["b_fold"])
                fm = s.cnt["b_fold"]
                s.op("pe", lambda e: e.transpose(
                    p7[:, 0:128], v_ext[:, 0:128], id_bf[:]),
                    wait=[("b_fold", fm), ("b_tev", tevm)], inc=["b_tr"])
                s.op("dve", lambda e: e.tensor_copy(v_win[:, 0:128], p7[:, 0:128]),
                     wait=[("b_tr", s.cnt["b_tr"])], inc=["b_tev"])
                for ch in range(8):
                    s.op("dve", lambda e, ch=ch: e.scalar_tensor_tensor(
                        S0st[:, ch * 128:(ch + 1) * 128], sprev[:],
                        aoff[:, 8 + ch: 9 + ch],
                        S0st[:, ch * 128:(ch + 1) * 128],
                        op0=ALU.mult, op1=ALU.add),
                        wait=[("b_fold", fm), ("b_chn", chm),
                              (GS, gsm)], inc=["b_fold"])
                foldm = s.cnt["b_fold"]

                for u in range(32 if KBSUB >= 3 else 0):
                    h, ch = divmod(u, 8)
                    qcol = h * TOK + ch * 128
                    psP = [p5, p6][u % 2]
                    psS = [p4, p3][u % 2]
                    psO = [p0, p1][u % 2]
                    PT = PTb[u % 2]
                    eT = ePT[u % 2]
                    s.op("pe", lambda e, qcol=qcol, ch=ch, psP=psP: e.matmul(
                        psP[:, 0:128], ktl[:, ch * 128:(ch + 1) * 128],
                        qtl[:, qcol:qcol + 128], start=True, stop=True),
                        wait=[("b_qt", qtms[h]), ("b_kt", ktm),
                              ("b_chn", chm),
                              ("b_Pe", max(0, s.cnt.get("b_Pe", 0) - 1))],
                        inc=["b_P"])
                    pm = s.cnt["b_P"]
                    s.op("pe", lambda e, qcol=qcol, ch=ch, psS=psS: e.matmul(
                        psS[:, 0:256], sq_sb[:, qcol:qcol + 128],
                        sk_ext[:, ch * 128: ch * 128 + 256],
                        start=True, stop=True),
                        wait=[("b_rpe2", rpems[h]), ("b_fold", foldm),
                              ("b_zev", s.cnt.get("b_zev", 0)),
                              ("b_swa", max(0, s.cnt.get("b_swa", 0) - 1))],
                        inc=["b_sc"])
                    scm = s.cnt["b_sc"]
                    s.op("dve", lambda e, psP=psP, PT=PT: e.tensor_tensor(
                        PT[:], psP[:, 0:128], tri_sb[:], op=ALU.mult),
                        wait=[("b_P", pm),
                              ("b_oacc", max(0, s.cnt.get("b_oacc", 0) - 1))],
                        inc=["b_Pe"])
                    pem = s.cnt["b_Pe"]
                    s.op("act", lambda e, psS=psS: e.activation(
                        ePf[:, 0:256], psS[:, 0:256], AF.Exp),
                        wait=[("b_sc", scm),
                              ("b_swe", max(0, s.cnt.get("b_swe", 0) - 3))],
                        inc=["b_swa"])
                    swam = s.cnt["b_swa"]
                    msk = swm0_sb if ch == 0 else swm_sb
                    s.op("dve", lambda e, msk=msk: e.tensor_tensor(
                        ePm[:, 0:256], ePf[:, 0:256], msk[:], op=ALU.mult),
                        wait=[("b_swa", swam)] + CST, inc=["b_swe"])
                    s.op("dve", lambda e: e.reduce_sum(
                        zsw[:, 0:1], ePm[:, 0:256], axis=AX),
                        wait=[("b_swe", None)], inc=["b_swe"])
                    s.op("dve", lambda e: e.reciprocal(zsw[:, 1:2], zsw[:, 0:1]),
                         wait=[("b_swe", None)], inc=["b_swe"])
                    s.op("dve", lambda e: e.tensor_scalar(
                        ePb[:, 0:256], ePm[:, 0:256], zsw[:, 1:2], 0.5,
                        op0=ALU.mult, op1=ALU.mult),
                        wait=[("b_swe", None), ("b_tr", s.cnt.get("b_tr", 0))],
                        inc=["b_swe"])
                    swem = s.cnt["b_swe"]
                    for ji in range(2):
                        s.op("pe", lambda e, ji=ji: e.transpose(
                            p7[:, ji * 128:(ji + 1) * 128],
                            ePb[:, ji * 128:(ji + 1) * 128], id_bf[:]),
                            wait=[("b_swe", swem),
                                  ("b_tev", s.cnt.get("b_tev", 0))]
                            if ji == 0 else [],
                            inc=(["b_tr"] if ji == 1 else []))
                    s.op("dve", lambda e, eT=eT: e.tensor_copy(
                        eT[:, 0:256], p7[:, 0:256]),
                        wait=[("b_tr", s.cnt["b_tr"]),
                              ("b_oacc", max(0, s.cnt.get("b_oacc", 0) - 1))],
                        inc=["b_tev"])
                    tevu = s.cnt["b_tev"]
                    s.op("pe", lambda e, ch=ch, qcol=qcol, psO=psO: e.matmul(
                        psO[:, 0:128], S0st[:, ch * 128:(ch + 1) * 128],
                        qtl[:, qcol:qcol + 128], start=True, stop=False),
                        wait=[("b_fold", foldm), ("b_qt", qtms[h]),
                              ("b_qev", s.cnt.get("b_qev", 0)),
                              ("b_oev", max(0, s.cnt.get("b_oev", 0) - 1))])
                    s.op("pe", lambda e, ch=ch, psO=psO, PT=PT: e.matmul(
                        psO[:, 0:128], v_tok[:, ch * 128:(ch + 1) * 128],
                        PT[:], start=False, stop=False),
                        wait=[("b_Pe", pem)])
                    s.op("pe", lambda e, ch=ch, psO=psO, eT=eT: e.matmul(
                        psO[:, 0:128], v_win[:, ch * 128:(ch + 1) * 128],
                        eT[:, 0:128], start=False, stop=False),
                        wait=[("b_tev", tevu)])
                    s.op("pe", lambda e, ch=ch, psO=psO, eT=eT: e.matmul(
                        psO[:, 0:128], v_win[:, (ch + 1) * 128:(ch + 2) * 128],
                        eT[:, 128:256], start=False, stop=True),
                        inc=["b_oacc"])
                    oam = s.cnt["b_oacc"]
                    s.op("act", lambda e, qcol=qcol, psO=psO: e.activation(
                        o_sb[:, qcol:qcol + 128], psO[:, 0:128], AF.Copy),
                        wait=[("b_oacc", oam), ("b_ost", 16 * gidx)],
                        inc=["b_oev"])
                s.op("sp", lambda e, gidx=gidx: e.dma_start(
                    out=o_dram.ap()[gidx], in_=o_sb[:]),
                    wait=[("b_oev", None)], inc=["b_ost"], dma=True)
            s.barrier()

        def phase_wo():
          with contextlib.ExitStack() as pes:
            T = pes.enter_context
            o_all = T(nc.sbuf_tensor("w_oall", [128, DCN * TOK], BF16))
            wob = [T(nc.sbuf_tensor(f"w_wo{i}", [128, DCN * 256], BF16))
                   for i in range(2)]
            xb = [T(nc.sbuf_tensor(f"w_xb{i}", [128, TCN * 256], BF16))
                  for i in range(2)]
            xo = [T(nc.sbuf_tensor(f"w_xo{i}", [128, TCN * 256], BF16))
                  for i in range(2)]
            ps_w = [T(nc.psum_tensor(f"w_ps{i}", [128, 256], F32))
                    for i in range(2)]
            for gidx in range(8):
                s.op("pool", lambda e, gidx=gidx: e.dma_start(
                    out=o_all[:, gidx * 4096:(gidx + 1) * 4096],
                    in_=o_dram.ap()[gidx]),
                    wait=[("b_ost", 16 * min(gidx + 1, KBGRP))],
                    inc=["w_ldo"], dma=True)
            ldo = s.cnt["w_ldo"]
            xsv = x_spill.ap().rearrange("p (c f) -> p c f", c=TCN)
            x2v = x2_dram.ap().rearrange("p (c f) -> p c f", c=TCN)
            for rb in range(16):
                buf = wob[rb % 2]
                bv = buf.ap().rearrange("p (c o) -> p c o", c=DCN)
                s.op("pool", lambda e, buf=buf, rb=rb: e.dma_start(
                    out=buf[:], in_=wo[rb]),
                    wait=([("w_pe", 8 * (rb - 1))] if rb >= 2 else []),
                    inc=["w_ldw"], dma=True)
                ldw = s.cnt["w_ldw"]
                xbi = xb[rb % 2]
                s.op("sp", lambda e, xbi=xbi, rb=rb: e.dma_start(
                    out=xbi.ap().rearrange("p (c f) -> p c f", c=TCN),
                    in_=xsv[:, :, rb * 256:(rb + 1) * 256]),
                    wait=[("st_sp", 16)] + ([("w_ev", 8 * (rb - 1))]
                                            if rb >= 2 else []),
                    inc=["w_ldx"], dma=True)
                ldx = s.cnt["w_ldx"]
                xoi = xo[rb % 2]
                for tc in range(TCN):
                    ps = ps_w[tc % 2]
                    for dc in range(DCN):
                        s.op("pe", lambda e, bv=bv, dc=dc, tc=tc, ps=ps: e.matmul(
                            ps[:], o_all[:, dc * TOK + tc * 128: dc * TOK + (tc + 1) * 128],
                            bv[:, dc, :],
                            start=(dc == 0), stop=(dc == DCN - 1)),
                            wait=([("w_ldw", ldw), ("w_ldo", ldo),
                                   ("w_ev", max(0, s.cnt.get("w_ev", 0) - 1))]
                                  if dc == 0 else []),
                            inc=(["w_pe"] if dc == DCN - 1 else []))
                    wpm = s.cnt["w_pe"]
                    s.op("dve", lambda e, xbi=xbi, xoi=xoi, tc=tc, ps=ps:
                         e.tensor_tensor(
                        xoi[:, tc * 256:(tc + 1) * 256],
                        xbi[:, tc * 256:(tc + 1) * 256], ps[:], op=ALU.add),
                        wait=[("w_pe", wpm), ("w_ldx", ldx),
                              ("st_x2", max(0, 16 * (rb - 1)))],
                        inc=["w_ev"])
                s.op("sp", lambda e, xoi=xoi, rb=rb: e.dma_start(
                    out=x2v[:, :, rb * 256:(rb + 1) * 256],
                    in_=xoi.ap().rearrange("p (c f) -> p c f", c=TCN)),
                    wait=[("w_ev", None)], inc=["st_x2"], dma=True)
            s.barrier()

        run_b = DEBUG_STAGE >= 7 and N_SUBLAYERS >= 3
        x_final = x_sb
        if run_b:
            with contextlib.ExitStack() as pes:
                hid = [pes.enter_context(
                    nc.sbuf_tensor(f"hidB{i}", [128, DM], BF16)) for i in range(2)]
                sqs = [pes.enter_context(
                    nc.sbuf_tensor(f"sqB{i}", [128, DM], F32)) for i in range(2)]
                ps_tr = [pes.enter_context(
                    nc.psum_tensor(f"psTrB{i}", [128, 512], BF16)) for i in range(2)]
                phase_ln(x_sb, hid, ps_tr, sqs)
            s.op("sp", lambda e: e.dma_start(out=x_spill[:, :], in_=x_sb[:]),
                 inc=["st_sp"], dma=True)
            for e_ in Sched.ENG:
                s.op(e_, None, wait=[("st_sp", 16)])
            s1.close()

            phase_attn_b_groups()
            phase_wo()

            with contextlib.ExitStack() as pes:
                x2_sb = pes.enter_context(
                    nc.sbuf_tensor("x2_sb", [128, TCN * DM], BF16))
                s.op("sp", lambda e: e.dma_start(out=x2_sb[:], in_=x2_dram[:, :]),
                     wait=[("st_x2", 16 * 16)], inc=["ld_x2"], dma=True)
                for e_ in Sched.ENG:
                    s.op(e_, None, wait=[("ld_x2", 16)])
                if DEBUG_STAGE >= 8 and N_SUBLAYERS >= 4:
                    phase_ffn(1, x2_sb)
                s.op("pool", lambda e: e.dma_start(
                    out=out.rearrange("(c p) f -> p c f", p=128),
                    in_=x2_sb.ap().rearrange("p (c f) -> p c f", c=TCN)),
                    inc=["st_out"], dma=True)
                if s.cnt.get("st_dbg", 0) == 0:
                    s.op("pool", lambda e: e.dma_start(
                        out=dbg, in_=x2_sb[:, 0:TCN * 128]),
                        inc=["st_dbg"], dma=True)
                    s.op("pool", lambda e: e.dma_start(
                        out=dbgb[:, :], in_=x2_sb[:, 0:128]),
                        inc=["st_dbg"], dma=True)
                s.op("sp", None, wait=[("st_out", None), ("st_dbg", None)])
        else:
            s.op("pool", lambda e: e.dma_start(
                out=out.rearrange("(c p) f -> p c f", p=128),
                in_=x_sb.ap().rearrange("p (c f) -> p c f", c=TCN)),
                inc=["st_out"], dma=True)
            if s.cnt.get("st_dbg", 0) == 0:
                s.op("pool", lambda e: e.dma_start(out=dbg, in_=x_sb[:, 0:TCN * 128]),
                     inc=["st_dbg"], dma=True)
                s.op("pool", lambda e: e.dma_start(
                    out=dbgb[:, :], in_=x_sb[:, 0:128]),
                    inc=["st_dbg"], dma=True)
            s.op("sp", None, wait=[("st_out", None), ("st_dbg", None)])
            s1.close()

        with nc.Block() as block:
            block.tensor(lambda e: [f(e) for f in s.q["pe"]])
            block.vector(lambda e: [f(e) for f in s.q["dve"]])
            block.scalar(lambda e: [f(e) for f in s.q["act"]])
            block.gpsimd(lambda e: [f(e) for f in s.q["pool"]])
            block.sync(lambda e: [f(e) for f in s.q["sp"]])

    return nc


def _host_consts(core):
    ident = np.eye(128, dtype=np.float32)
    rot = np.zeros((128, 128), dtype=np.float32)
    for dp in range(64):
        rot[dp + 64, dp] = -1.0
        rot[dp, dp + 64] = 1.0
    pos = (core % 2) * 1024 + np.arange(TOK, dtype=np.float64)
    inv = 1.0 / (10000.0 ** (2.0 * np.arange(64, dtype=np.float64) / 128.0))
    fr = pos[None, :] * inv[:, None]
    emb = np.concatenate([fr, fr], axis=0)
    cos = np.cos(emb).astype(np.float32)
    sin = np.sin(emb).astype(np.float32)
    scl = 1.0 / np.sqrt(128.0)
    tri = np.triu(np.ones((128, 128), dtype=np.float32))
    i = np.arange(128)[:, None]
    j = np.arange(256)[None, :]
    kpos = j - 64
    m = (kpos <= i) & (kpos >= i - 64)
    sw = m.astype(np.float32)
    sw0 = sw.copy()
    if core % 2 == 0:
        sw0[:64, :64] = 0.0
    gate = np.full((128, 1), 1.0 if core % 2 == 1 else 0.0, dtype=np.float32)
    d0 = np.ones((128, TOK), dtype=np.float32)
    d0[:, ::128] = 0.0
    return dict(
        identf=ident, rotm=rot,
        cosq=(cos * scl).astype(np.float32), sinq=(sin * scl).astype(np.float32),
        cosk=cos, sink=sin,
        trimask=tri, swmask=sw, swmask0=sw0, gate=gate, d0mask=d0,
    )


def _colblocks(W, width):
    IN, OUT = W.shape
    nb = OUT // width
    return np.ascontiguousarray(
        W.reshape(IN // 128, 128, nb, width).transpose(2, 1, 0, 3)
        .reshape(nb, 128, (IN // 128) * width))


def kernel(L0_attn_a_x, L0_attn_a_w1, L0_attn_a_w2, L0_attn_a_b2,
           L0_attn_a_sparse_mask, L0_attn_b_Wq, L0_attn_b_Wk, L0_attn_b_Wv,
           L0_attn_b_Wo, L0_ffn_base_W, L0_ffn_router_W, L0_ffn_lora_A,
           L0_ffn_lora_B):
    x = np.ascontiguousarray(np.asarray(L0_attn_a_x).reshape(NCORE * TOK, DM))
    w1 = _colblocks(np.asarray(L0_attn_a_w1), 256)
    w2t_full = np.asarray(L0_attn_a_w2).T
    mask = np.ascontiguousarray(L0_attn_a_sparse_mask)
    b2 = np.ascontiguousarray(np.asarray(L0_attn_a_b2).reshape(1, DM))
    base_wt = _colblocks(np.asarray(L0_ffn_base_W).T, 256)
    loraa_t = _colblocks(
        np.asarray(L0_ffn_lora_A).transpose(1, 0, 2).reshape(DM, 128), 128)[0]
    lorab_f = np.ascontiguousarray(np.asarray(L0_ffn_lora_B).reshape(128, DM))
    rtp = np.zeros((DM, 128), dtype=np.float32)
    rtp[:, :8] = np.asarray(L0_ffn_router_W).T
    routertp = _colblocks(rtp, 128)[0]
    wq = _colblocks(np.asarray(L0_attn_b_Wq), 128)
    wk = _colblocks(np.asarray(L0_attn_b_Wk), 128)
    wv = _colblocks(np.asarray(L0_attn_b_Wv), 128)
    wo = _colblocks(np.asarray(L0_attn_b_Wo), 256)

    nc = build_program()
    in_maps = []
    for c in range(NCORE):
        m = dict(
            x=np.ascontiguousarray(x[c * TOK:(c + 1) * TOK]),
            w1=w1,
            w2t=_colblocks(
                np.ascontiguousarray(w2t_full[:, c * 512:(c + 1) * 512]), 512)[0],
            maskc=np.ascontiguousarray(mask[:, c * 512:(c + 1) * 512]),
            b2=b2, base_wt=base_wt, loraa_t=loraa_t, lorab_f=lorab_f,
            routertp=routertp, wq=wq, wk=wk, wv=wv, wo=wo,
        )
        m.update(_host_consts(c))
        in_maps.append(m)
    res = run_bass_kernel_spmd(nc, in_maps, core_ids=list(range(NCORE)))
    global LAST_RES
    LAST_RES = res
    outs = [res.results[c]["out"] for c in range(NCORE)]
    y = np.concatenate(outs, axis=0).reshape(4, 2048, DM).astype(np.float32)
    return y


# version 04
# speedup vs baseline: 1.1421x; 1.0228x over previous
import contextlib
import numpy as np
import concourse.bass as bass
import concourse.mybir as mybir
from concourse.bass_utils import run_bass_kernel_spmd

F32 = mybir.dt.float32
BF16 = mybir.dt.bfloat16
AF = mybir.ActivationFunctionType
ALU = mybir.AluOpType
AX = mybir.AxisListType.X

NCORE = 8
TOK = 1024
TCN = TOK // 128
DM = 4096
DCN = DM // 128
EPS = 1e-5

N_SUBLAYERS = 4
import os
DEBUG_STAGE = int(os.environ.get("KSTAGE", "99"))
KBSUB = int(os.environ.get("KBSUB", "3"))
KBGRP = int(os.environ.get("KBGRP", "8"))
KBK = int(os.environ.get("KBK", "8"))


class Sched:

    ENG = ("pe", "dve", "act", "pool", "sp")

    def __init__(self, nc, es):
        self.nc = nc
        self.es = es
        self.q = {e: [] for e in self.ENG}
        self.sems = {}
        self.cnt = {}
        self.last_wait = {e: {} for e in self.ENG}

    def sem(self, name):
        if name not in self.sems:
            self.sems[name] = self.es.enter_context(self.nc.semaphore(name))
            self.cnt[name] = 0
        return self.sems[name]

    def op(self, eng, fn, wait=(), inc=(), dma=False):
        wl = []
        for w in wait:
            name, val = w if isinstance(w, tuple) else (w, self.cnt[w])
            if val is None:
                val = self.cnt.get(name, 0)
            self.sem(name)
            if val > 0 and val > self.last_wait[eng].get(name, -1):
                self.last_wait[eng][name] = val
                wl.append((self.sems[name], val))
        il = []
        for i in inc:
            if isinstance(i, tuple):
                name, n = i
            else:
                name, n = i, (16 if dma else 1)
            self.sem(name)
            self.cnt[name] += n
            il.append((self.sems[name], n))

        def play(e, fn=fn, wl=wl, il=il):
            for s, v in wl:
                e.wait_ge(s, v)
            inst = fn(e) if fn is not None else None
            if il:
                assert inst is not None
                for s, n in il:
                    inst.then_inc(s, n)

        self.q[eng].append(play)

    def barrier(self):
        def dn(x):
            x.drain()
            return x.nop()
        for e in self.ENG:
            self.op(e, dn, inc=[("bar_" + e, 1)])
        snap = [(("bar_" + e), self.cnt["bar_" + e]) for e in self.ENG]
        for e in self.ENG:
            self.op(e, None, wait=snap)


def build_program():
    nc = bass.Bass("TRN2", target_bir_lowering=False, debug=False)

    def inp(name, shape, dt=F32):
        return nc.dram_tensor(name, shape, dt, kind="ExternalInput").ap()

    x_in = inp("x", [TOK, DM])
    w1 = inp("w1", [16, 128, DCN * 256])
    w2t = inp("w2t", [128, DCN * 512])
    maskc = inp("maskc", [DM, 512])
    b2 = inp("b2", [1, DM])
    base_wt = inp("base_wt", [16, 128, DCN * 256])
    loraa_t = inp("loraa_t", [128, DCN * 128])
    lorab_f = inp("lorab_f", [128, DM])
    routertp = inp("routertp", [128, DCN * 128])
    wq = inp("wq", [32, 128, DCN * 128])
    wk = inp("wk", [8, 128, DCN * 128])
    wv = inp("wv", [8, 128, DCN * 128])
    wo = inp("wo", [16, 128, DCN * 256])
    identf = inp("identf", [128, 128])
    rotm = inp("rotm", [128, 128])
    cosq = inp("cosq", [128, TOK])
    sinq = inp("sinq", [128, TOK])
    cosk = inp("cosk", [128, TOK])
    sink = inp("sink", [128, TOK])
    trimask = inp("trimask", [128, 128])
    swmask = inp("swmask", [128, 256])
    swmask0 = inp("swmask0", [128, 256])
    gate = inp("gate", [128, 1])
    d0mask = inp("d0mask", [128, TOK])

    out = nc.dram_tensor("out", [TOK, DM], F32, kind="ExternalOutput").ap()
    dbg = nc.dram_tensor("dbg", [128, TCN * 128], F32, kind="ExternalOutput").ap()
    dbgb = nc.dram_tensor("dbgb", [128, 128], F32, kind="ExternalOutput").ap()

    attn_loc = nc.dram_tensor("attn_loc", [128, DCN * 512], BF16)
    attn_all = nc.dram_tensor("attn_all", [NCORE * 128, DCN * 512], BF16)
    x_spill = nc.dram_tensor("x_spill", [128, TCN * DM], BF16)
    x2_dram = nc.dram_tensor("x2_dram", [128, TCN * DM], BF16)
    o_dram = nc.dram_tensor("o_dram", [8, 128, 4 * TOK], BF16)
    exch_loc = nc.dram_tensor("exch_loc", [8, 128, 256], F32)
    exch_pair = nc.dram_tensor("exch_pair", [8, 256, 256], F32)
    zdram = nc.dram_tensor("zdram", [8, 5, 1024], F32)

    es_root = contextlib.ExitStack()
    with es_root as es:
        s = Sched(nc, es)

        hT = es.enter_context(nc.sbuf_tensor("hT", [128, DCN * TOK], BF16, side="right"))
        id_bf = es.enter_context(nc.sbuf_tensor("id_bf", [128, 128], BF16, side="right"))
        ones_bf = es.enter_context(nc.sbuf_tensor("ones_bf", [128, 1], BF16, side="right"))
        ones_row = es.enter_context(nc.sbuf_tensor("ones_row", [1, 128], F32, side="right"))
        half_row = es.enter_context(nc.sbuf_tensor("half_row", [1, 128], F32, side="right"))
        stat = es.enter_context(nc.sbuf_tensor("stat", [128, 48], F32, side="right"))
        eps_sb = es.enter_context(nc.sbuf_tensor("eps_sb", [128, 1], F32, side="right"))

        s1 = contextlib.ExitStack()
        x_sb = s1.enter_context(nc.sbuf_tensor("x_sb", [128, TCN * DM], BF16))

        s.op("pool", lambda e: e.dma_start(
            out=x_sb.ap().rearrange("p (c f) -> p c f", c=TCN),
            in_=x_in.rearrange("(c p) f -> p c f", p=128)),
            inc=["ld_x"], dma=True)
        s.op("pool", lambda e: e.dma_start(out=id_bf[:], in_=identf),
             inc=["ld_const"], dma=True)
        s.op("dve", lambda e: e.memset(ones_bf[:], 1.0), inc=["const_c"])
        s.op("dve", lambda e: e.memset(eps_sb[:], EPS), inc=["const_c"])
        s.op("dve", lambda e: e.memset(ones_row[:], 1.0), inc=["const_c"])
        s.op("dve", lambda e: e.memset(half_row[:], 0.5), inc=["const_c"])

        ln_id = [0]

        def phase_ln(xt, hidden_scratch, ps_tr, sq_scratch):
            tag = "ln"
            ln_id[0] += 1
            b_rd = s.cnt.get("rd_" + tag, 0)
            b_act = s.cnt.get("act_" + tag, 0)
            b_pe = s.cnt.get("pe_" + tag, 0)
            for tc in range(TCN):
                xc = xt[:, tc * DM:(tc + 1) * DM]
                sq_i = sq_scratch[tc % 2]
                ht_i = hidden_scratch[tc % 2]
                c0 = 8 * tc
                sc0 = (tc % 2) * 6
                s.op("act", lambda e, xc=xc, sq=sq_i: e.activation(
                    sq[:], xc, AF.Square),
                    wait=[("rd_" + tag, b_rd + max(0, tc - 1))],
                    inc=["act_" + tag])
                s.op("dve", lambda e, xc=xc, sc0=sc0: e.reduce_sum(
                    stat[:, sc0:sc0 + 1], xc, axis=AX),
                    wait=[("dve_" + tag, max(0, s.cnt.get("dve_" + tag, 0) - 6))],
                    inc=["dve_" + tag])
                s.op("dve", lambda e, sq=sq_i, sc0=sc0: e.reduce_sum(
                    stat[:, sc0 + 1:sc0 + 2], sq[:], axis=AX),
                    wait=[("act_" + tag, b_act + 2 * tc + 1)], inc=["rd_" + tag])
                s.op("dve", lambda e, sc0=sc0: e.tensor_scalar(
                    stat[:, sc0 + 2:sc0 + 3], stat[:, sc0:sc0 + 1],
                    1.0 / DM, None, op0=ALU.mult),
                    wait=[("rd_" + tag, None)], inc=["dve_" + tag])
                s.op("dve", lambda e, sc0=sc0: e.tensor_scalar(
                    stat[:, sc0 + 4:sc0 + 5], stat[:, sc0 + 1:sc0 + 2],
                    1.0 / DM, None, op0=ALU.mult), inc=["dve_" + tag])
                s.op("dve", lambda e, sc0=sc0: e.tensor_tensor(
                    stat[:, sc0 + 5:sc0 + 6], stat[:, sc0 + 2:sc0 + 3],
                    stat[:, sc0 + 2:sc0 + 3], op=ALU.mult),
                    wait=[("dve_" + tag, None)], inc=["dve_" + tag])
                s.op("dve", lambda e, sc0=sc0: e.tensor_tensor(
                    stat[:, sc0 + 4:sc0 + 5], stat[:, sc0 + 4:sc0 + 5],
                    stat[:, sc0 + 5:sc0 + 6], op=ALU.subtract),
                    wait=[("dve_" + tag, None)], inc=["dve_" + tag])
                s.op("act", lambda e, sc0=sc0: e.activation(
                    stat[:, sc0 + 5:sc0 + 6], stat[:, sc0 + 4:sc0 + 5],
                    AF.Sqrt, bias=eps_sb[:, 0:1]),
                    wait=[("dve_" + tag, None), ("const_c", None)],
                    inc=["act_" + tag])
                s.op("dve", lambda e, sc0=sc0: e.reciprocal(
                    stat[:, sc0 + 3:sc0 + 4], stat[:, sc0 + 5:sc0 + 6]),
                    wait=[("act_" + tag, None)], inc=["dve_" + tag])
                s.op("dve", lambda e, xc=xc, ht=ht_i, sc0=sc0: e.tensor_scalar(
                    ht[:], xc, stat[:, sc0 + 2:sc0 + 3], stat[:, sc0 + 3:sc0 + 4],
                    op0=ALU.subtract, op1=ALU.mult),
                    wait=[("dve_" + tag, None),
                          ("pe_" + tag, b_pe + max(0, 8 * (tc - 1)))],
                    inc=["dve_" + tag])
                hm = s.cnt["dve_" + tag]
                for dg in range(DCN // 4):
                    ps = ps_tr[dg % 2]
                    for j in range(4):
                        dc = dg * 4 + j
                        s.op("pe", lambda e, ht=ht_i, dc=dc, j=j, ps=ps: e.transpose(
                            ps[:, j * 128:(j + 1) * 128],
                            ht[:, dc * 128:(dc + 1) * 128], id_bf[:]),
                            wait=[("ld_const", 16), ("dve_" + tag, hm),
                                  ("ev_" + tag, max(0, s.cnt.get("ev_" + tag, 0) - 1))],
                            inc=(["pe_" + tag] if j == 3 else []))
                    pm = s.cnt["pe_" + tag]
                    dst = hT.ap().rearrange("p (d t) -> p d t", d=DCN)[
                        :, dg * 4:(dg + 1) * 4, tc * 128:(tc + 1) * 128]
                    src_ = ps.ap().rearrange("p (a b) -> p a b", a=4)
                    eng = "dve" if (dg % 2 == 0) else "act"
                    if eng == "dve":
                        s.op("dve", lambda e, dst=dst, src_=src_: e.tensor_copy(dst, src_),
                             wait=[("pe_" + tag, pm)], inc=["ev_" + tag])
                    else:
                        s.op("act", lambda e, dst=dst, src_=src_: e.activation(
                            dst, src_, AF.Copy),
                            wait=[("pe_" + tag, pm)], inc=["ev_" + tag])
            s.barrier()

        if DEBUG_STAGE >= 1:
          with contextlib.ExitStack() as pes:
            hid = [pes.enter_context(nc.sbuf_tensor(f"hidA{i}", [128, DM], BF16))
                   for i in range(2)]
            sqs = [pes.enter_context(nc.sbuf_tensor(f"sqA{i}", [128, DM], F32))
                   for i in range(2)]
            ps_tr = [pes.enter_context(nc.psum_tensor(f"psTrA{i}", [128, 512], BF16))
                     for i in range(2)]
            s.op("dve", None, wait=[("ld_x", 16)])
            s.op("act", None, wait=[("ld_x", 16)])
            phase_ln(x_sb, hid, ps_tr, sqs)

        if DEBUG_STAGE >= 2:
          with contextlib.ExitStack() as pes:
            w2t_sb = pes.enter_context(nc.sbuf_tensor("w2t_sb", [128, DCN * 512], BF16))
            w1b = [pes.enter_context(nc.sbuf_tensor(f"w1b{i}", [128, DCN * 256], BF16))
                   for i in range(2)]
            mkb = [pes.enter_context(nc.sbuf_tensor(f"mkb{i}", [128, 512], F32))
                   for i in range(2)]
            atile = [pes.enter_context(nc.sbuf_tensor(f"atile{i}", [128, 512], BF16))
                     for i in range(2)]
            ps_a = [pes.enter_context(nc.psum_tensor(f"psA{i}", [128, 512], F32))
                    for i in range(2)]

            s.op("pool", lambda e: e.dma_start(out=w2t_sb[:], in_=w2t),
                 inc=["ld_w2t"], dma=True)

            for scb in range(16):
                buf = w1b[scb % 2]
                s.op("pool", lambda e, buf=buf, scb=scb: e.dma_start(
                    out=buf[:], in_=w1[scb]),
                    wait=([("pe_aw", 2 * (scb - 1))] if scb >= 2 else []),
                    inc=["ld_w1"], dma=True)
                ldm = s.cnt["ld_w1"]
                for sc2 in range(2):
                    sc = scb * 2 + sc2
                    ps = ps_a[sc % 2]
                    mb = mkb[sc % 2]
                    at = atile[sc % 2]
                    s.op("sp", lambda e, mb=mb, sc=sc: e.dma_start(
                        out=mb[:], in_=maskc[sc * 128:(sc + 1) * 128, :]),
                        wait=[("ev_attn", max(0, sc - 1))],
                        inc=["ld_mk"], dma=True)
                    mkm = s.cnt["ld_mk"]
                    bufv = buf.ap().rearrange("p (c t) -> p c t", c=DCN)
                    for dc in range(DCN):
                        s.op("pe", lambda e, bufv=bufv, dc=dc, sc2=sc2, ps=ps: e.matmul(
                            ps[:], bufv[:, dc, sc2 * 128:(sc2 + 1) * 128],
                            w2t_sb[:, dc * 512:(dc + 1) * 512],
                            start=(dc == 0), stop=(dc == DCN - 1)),
                            wait=([("ld_w1", ldm), ("ld_w2t", 16),
                                   ("ev_attn", max(0, s.cnt.get("ev_attn", 0) - 1))]
                                  if dc == 0 else []),
                            inc=(["pe_aw"] if dc == DCN - 1 else []))
                    pem = s.cnt["pe_aw"]
                    s.op("dve", lambda e, ps=ps, mb=mb, at=at: e.tensor_tensor(
                        at[:], ps[:], mb[:], op=ALU.mult),
                        wait=[("pe_aw", pem), ("ld_mk", mkm),
                              ("st_attn", max(0, 16 * (sc - 1)))],
                        inc=["ev_attn"])
                    evm = s.cnt["ev_attn"]
                    s.op("sp", lambda e, at=at, sc=sc: e.dma_start(
                        out=attn_loc.ap()[:, sc * 512:(sc + 1) * 512], in_=at[:]),
                        wait=[("ev_attn", evm)], inc=["st_attn"], dma=True)
            s.op("pool", lambda e: e.collective_compute(
                "AllGather", ALU.bypass,
                replica_groups=[list(range(NCORE))],
                ins=[attn_loc.ap()], outs=[attn_all.ap()]),
                wait=[("st_attn", 16 * 32)], inc=[("cc", 1)])
            s.barrier()

        if DEBUG_STAGE >= 3:
          with contextlib.ExitStack() as pes:
            abuf = [pes.enter_context(nc.sbuf_tensor(f"ab{i}", [128, DCN * 512], BF16))
                    for i in range(2)]
            ps_y = [pes.enter_context(nc.psum_tensor(f"psY{i}", [128, 512], F32))
                    for i in range(2)]
            b2b = [pes.enter_context(nc.sbuf_tensor(f"b2b{i}", [128, 512], F32))
                   for i in range(2)]
            tmp1 = [pes.enter_context(nc.sbuf_tensor(f"tmpA{i}", [128, 512], F32))
                    for i in range(2)]
            tmp2 = [pes.enter_context(nc.sbuf_tensor(f"tmpB{i}", [128, 512], F32))
                    for i in range(2)]
            for r in range(NCORE):
                buf = abuf[r % 2]
                s.op("sp", lambda e, buf=buf, r=r: e.dma_start(
                    out=buf[:], in_=attn_all.ap()[r * 128:(r + 1) * 128, :]),
                    wait=[("cc", 1)] + ([("pe_y", (r - 1) * TCN)]
                                        if r >= 2 else []),
                    inc=["ld_at"], dma=True)
                ldm = s.cnt["ld_at"]
                bb = b2b[r % 2]
                s.op("sp", lambda e, bb=bb, r=r: e.dma_start(
                    out=bb[:],
                    in_=b2[0:1, r * 512:(r + 1) * 512].partition_broadcast(128)),
                    wait=[("ep_y", max(0, (r - 1) * TCN))],
                    inc=["b2r"], dma=True)
                b2m = s.cnt["b2r"]
                bufv = buf.ap().rearrange("p (c t) -> p c t", c=DCN)
                for tc in range(TCN):
                    ps = ps_y[tc % 2]
                    for dc in range(DCN):
                        s.op("pe", lambda e, bufv=bufv, dc=dc, tc=tc, ps=ps: e.matmul(
                            ps[:], hT[:, dc * TOK + tc * 128: dc * TOK + (tc + 1) * 128],
                            bufv[:, dc, :],
                            start=(dc == 0), stop=(dc == DCN - 1)),
                            wait=([("ld_at", ldm),
                                   ("ep_y", max(0, s.cnt.get("ep_y", 0) - 1))]
                                  if dc == 0 else []),
                            inc=(["pe_y"] if dc == DCN - 1 else []))
                    pem = s.cnt["pe_y"]
                    t1, t2 = tmp1[tc % 2], tmp2[tc % 2]
                    s.op("dve", lambda e, ps=ps, t1=t1, bb=bb: e.tensor_tensor(
                        t1[:], ps[:], bb[:], op=ALU.add),
                        wait=[("pe_y", pem), ("b2r", b2m),
                              ("ge_y", max(0, s.cnt.get("ge_y", 0) - 1))],
                        inc=["ep_y"])
                    epm = s.cnt["ep_y"]
                    s.op("act", lambda e, t1=t1, t2=t2: e.activation(
                        t2[:], t1[:], AF.Gelu),
                        wait=[("ep_y", epm),
                              ("rs_y", max(0, s.cnt.get("rs_y", 0) - 1))],
                        inc=["ge_y"])
                    gem = s.cnt["ge_y"]
                    xsl = x_sb[:, tc * DM + r * 512: tc * DM + (r + 1) * 512]
                    s.op("dve", lambda e, xsl=xsl, t2=t2: e.tensor_tensor(
                        xsl, xsl, t2[:], op=ALU.add),
                        wait=[("ge_y", gem)], inc=["rs_y"])
            s.barrier()

        def phase_ffn(fi, xt):
            t = "f_"
            tn = f"f{fi}_"
            b_rt = s.cnt.get(t + "rt", 0)
            b_la = s.cnt.get(t + "la", 0)
            b_lb = s.cnt.get(t + "lb", 0)
            b_lgev = s.cnt.get("lgev" + t, 0)
            b_evmt = s.cnt.get("ev_mt" + t, 0)
            b_peb = s.cnt.get("pe_b" + t, 0)
            with contextlib.ExitStack() as pes:
                hid = [pes.enter_context(
                    nc.sbuf_tensor(f"hid{tn}{i}", [128, DM], BF16)) for i in range(2)]
                sqs = [pes.enter_context(
                    nc.sbuf_tensor(f"sq{tn}{i}", [128, DM], F32)) for i in range(2)]
                ps_tr = [pes.enter_context(
                    nc.psum_tensor(f"psTr{tn}{i}", [128, 512], BF16)) for i in range(2)]
                phase_ln(xt, hid, ps_tr, sqs)
            if DEBUG_STAGE == 4 and fi == 0:
                s.op("pool", lambda e: e.dma_start(
                    out=dbg, in_=hT[:, 0:TCN * 128]),
                    inc=["st_dbg"], dma=True)
                s.op("pool", lambda e: e.dma_start(
                    out=dbgb[:, 0:48], in_=stat[:]),
                    inc=["st_dbg"], dma=True)
            with contextlib.ExitStack() as pes:
                la_sb = pes.enter_context(
                    nc.sbuf_tensor(tn + "la", [128, DCN * 128], BF16))
                lb_sb = pes.enter_context(
                    nc.sbuf_tensor(tn + "lb", [128, DM], BF16))
                rt_sb = pes.enter_context(
                    nc.sbuf_tensor(tn + "rt", [128, DCN * 128], BF16))
                midT = pes.enter_context(
                    nc.sbuf_tensor(tn + "midT", [128, TCN * 128], BF16))
                lg = pes.enter_context(nc.sbuf_tensor(tn + "lg", [128, TCN * 8], F32))
                gt = pes.enter_context(nc.sbuf_tensor(tn + "gt", [128, 8], F32))
                e1b = pes.enter_context(nc.sbuf_tensor(tn + "e1", [128, 8], F32))
                e2b = pes.enter_context(nc.sbuf_tensor(tn + "e2", [128, 8], F32))
                tmb = pes.enter_context(nc.sbuf_tensor(tn + "tm", [128, 8], F32))
                wde = pes.enter_context(nc.sbuf_tensor(tn + "wd", [128, TCN * 8], F32))
                mg = [pes.enter_context(
                    nc.sbuf_tensor(f"{tn}mg{i}", [128, 128], BF16)) for i in range(2)]
                bb = [pes.enter_context(
                    nc.sbuf_tensor(f"{tn}bb{i}", [128, DCN * 256], BF16))
                    for i in range(2)]
                ps_log = pes.enter_context(nc.psum_tensor(tn + "psL", [128, 64], F32))
                ps_mid = [pes.enter_context(
                    nc.psum_tensor(f"{tn}psM{i}", [128, 128], F32)) for i in range(2)]
                ps_mT = [pes.enter_context(
                    nc.psum_tensor(f"{tn}psMT{i}", [128, 128], BF16)) for i in range(2)]
                ps_b = [pes.enter_context(
                    nc.psum_tensor(f"{tn}psB{i}", [128, 256], F32)) for i in range(2)]

                LA, LB, RT, G = t + "la", t + "lb", t + "rt", t + "gg"
                s.op("pool", lambda e: e.dma_start(out=la_sb[:], in_=loraa_t),
                     inc=[LA], dma=True)
                s.op("pool", lambda e: e.dma_start(out=rt_sb[:], in_=routertp),
                     inc=[RT], dma=True)
                s.op("pool", lambda e: e.dma_start(out=lb_sb[:], in_=lorab_f),
                     inc=[LB], dma=True)

                for tc in range(TCN if DEBUG_STAGE >= 5 else 0):
                    pslg = ps_log[:, tc * 8:(tc + 1) * 8]
                    for dc in range(DCN):
                        s.op("pe", lambda e, dc=dc, tc=tc, pslg=pslg: e.matmul(
                            pslg, hT[:, dc * TOK + tc * 128: dc * TOK + (tc + 1) * 128],
                            rt_sb[:, dc * 128: dc * 128 + 8],
                            start=(dc == 0), stop=(dc == DCN - 1)),
                            wait=([(RT, b_rt + 16), ("lgev" + t, b_lgev + tc)]
                                  if dc == 0 else []),
                            inc=(["pe_lg" + t] if dc == DCN - 1 else []))
                    lgm = s.cnt["pe_lg" + t]
                    psm = ps_mid[tc % 2]
                    for dc in range(DCN):
                        s.op("pe", lambda e, dc=dc, tc=tc, psm=psm: e.matmul(
                            psm[:], hT[:, dc * TOK + tc * 128: dc * TOK + (tc + 1) * 128],
                            la_sb[:, dc * 128:(dc + 1) * 128],
                            start=(dc == 0), stop=(dc == DCN - 1)),
                            wait=([(LA, b_la + 16),
                                   ("em_" + t, max(0, s.cnt.get("em_" + t, 0) - 1))]
                                  if dc == 0 else []),
                            inc=(["pe_md" + t] if dc == DCN - 1 else []))
                    mdm = s.cnt["pe_md" + t]

                    def g(eng, fn):
                        s.op(eng, fn, wait=[(G, s.cnt.get(G, 0))], inc=[G])
                    lgc = lg[:, tc * 8:(tc + 1) * 8]
                    s.op("dve", lambda e, pslg=pslg, lgc=lgc: e.tensor_copy(lgc, pslg),
                         wait=[("pe_lg" + t, lgm), (G, s.cnt.get(G, 0))],
                         inc=["lgev" + t])
                    s.op("dve", lambda e, lgc=lgc: e.reduce_max(
                        gt[:, 0:1], lgc, axis=AX),
                        wait=[("lgev" + t, None)], inc=[G])
                    g("dve", lambda e, lgc=lgc: e.tensor_scalar(
                        e1b[:], lgc, gt[:, 0:1], None, op0=ALU.is_equal))
                    g("dve", lambda e, lgc=lgc: e.scalar_tensor_tensor(
                        tmb[:], e1b[:], -1e30, lgc, op0=ALU.mult, op1=ALU.add))
                    g("dve", lambda e: e.reduce_max(gt[:, 1:2], tmb[:], axis=AX))
                    g("dve", lambda e: e.tensor_scalar(
                        e2b[:], tmb[:], gt[:, 1:2], None, op0=ALU.is_equal))
                    g("dve", lambda e: e.tensor_tensor(
                        gt[:, 4:5], gt[:, 1:2], gt[:, 0:1], op=ALU.subtract))
                    g("act", lambda e: e.activation(gt[:, 3:4], gt[:, 4:5], AF.Sigmoid))
                    g("dve", lambda e: e.tensor_scalar(
                        gt[:, 2:3], gt[:, 3:4], -2.0, 2.0, op0=ALU.mult, op1=ALU.add))
                    g("dve", lambda e: e.tensor_scalar(
                        gt[:, 3:4], gt[:, 3:4], 2.0, None, op0=ALU.mult))
                    g("dve", lambda e: e.tensor_scalar(
                        e1b[:], e1b[:], gt[:, 2:3], None, op0=ALU.mult))
                    g("dve", lambda e: e.tensor_scalar(
                        e2b[:], e2b[:], gt[:, 3:4], None, op0=ALU.mult))
                    wdc = wde[:, tc * 8:(tc + 1) * 8]
                    g("dve", lambda e: e.tensor_tensor(
                        e1b[:], e1b[:], e2b[:], op=ALU.add))
                    g("dve", lambda e, wdc=wdc: e.tensor_copy(wdc, e1b[:]))
                    gl = s.cnt[G]

                    mgt = mg[tc % 2]
                    wrep = wde.ap()[:, tc * 8:(tc + 1) * 8].unsqueeze(2) \
                        .broadcast_to([128, 8, 16])
                    s.op("dve", lambda e, psm=psm, mgt=mgt, wrep=wrep: e.tensor_tensor(
                        mgt.ap().rearrange("p (e r) -> p e r", e=8),
                        psm.ap().rearrange("p (e r) -> p e r", e=8),
                        wrep, op=ALU.mult),
                        wait=[("pe_md" + t, mdm), (G, gl),
                              ("pe_mt" + t, max(0, s.cnt.get("pe_mt" + t, 0) - 1))],
                        inc=["em_" + t])
                    emm = s.cnt["em_" + t]
                    psl = ps_mT[tc % 2][:]
                    s.op("pe", lambda e, mgt=mgt, psl=psl: e.transpose(
                        psl, mgt[:], id_bf[:]),
                        wait=[("em_" + t, emm),
                              ("ev_mt" + t, max(0, s.cnt.get("ev_mt" + t, 0) - 1))],
                        inc=["pe_mt" + t])
                    ptm = s.cnt["pe_mt" + t]
                    s.op("act", lambda e, tc=tc, psl=psl: e.activation(
                        midT[:, tc * 128:(tc + 1) * 128], psl, AF.Copy),
                        wait=[("pe_mt" + t, ptm)], inc=["ev_mt" + t])

                if DEBUG_STAGE == 5 and fi == 0:
                    s.op("pool", lambda e: e.dma_start(out=dbg, in_=midT[:]),
                         wait=[("ev_mt" + t, TCN)], inc=["st_dbg"], dma=True)
                    s.op("pool", lambda e: e.dma_start(
                        out=dbgb[:, 0:64], in_=lg[:]),
                        wait=[("lgev" + t, TCN)], inc=["st_dbg"], dma=True)
                    s.op("pool", lambda e: e.dma_start(
                        out=dbgb[:, 64:128], in_=wde[:]),
                        wait=[(G, None)], inc=["st_dbg"], dma=True)

                for rb in range(16 if DEBUG_STAGE >= 6 else 0):
                    buf = bb[rb % 2]
                    bv = buf.ap().rearrange("p (c o) -> p c o", c=DCN)
                    s.op("pool", lambda e, buf=buf, rb=rb: e.dma_start(
                        out=buf[:], in_=base_wt[rb]),
                        wait=([("pe_b" + t, b_peb + 8 * (rb - 1))]
                              if rb >= 2 else []),
                        inc=["ld_b" + t], dma=True)
                    ldm = s.cnt["ld_b" + t]
                    use_inc = os.environ.get("KNOINC", "0") != "1"
                    for tc in range(TCN):
                        ps = ps_b[tc % 2]
                        for dc in range(DCN):
                            s.op("pe", lambda e, bv=bv, dc=dc, tc=tc, ps=ps: e.matmul(
                                ps[:], hT[:, dc * TOK + tc * 128: dc * TOK + (tc + 1) * 128],
                                bv[:, dc, :],
                                start=(dc == 0),
                                stop=(False if use_inc else (dc == DCN - 1))),
                                wait=([("ld_b" + t, ldm), (LB, b_lb + 16),
                                       ("ev_mt" + t,
                                        (b_evmt + TCN) if DEBUG_STAGE >= 5 else 0),
                                       ("rs_b" + t,
                                        max(0, s.cnt.get("rs_b" + t, 0) - 1))]
                                      if dc == 0 else []))
                        if use_inc:
                            s.op("pe", lambda e, tc=tc, rb=rb, ps=ps: e.matmul(
                                ps[:], midT[:, tc * 128:(tc + 1) * 128],
                                lb_sb[:, rb * 256:(rb + 1) * 256],
                                start=False, stop=True),
                                inc=["pe_b" + t])
                        else:
                            s.op("pe", lambda e: (e.drain(), e.nop())[1],
                                 inc=["pe_b" + t])
                        pbm = s.cnt["pe_b" + t]
                        xsl = xt[:, tc * DM + rb * 256: tc * DM + (rb + 1) * 256]
                        s.op("dve", lambda e, xsl=xsl, ps=ps: e.tensor_tensor(
                            xsl, xsl, ps[:], op=ALU.add),
                            wait=[("pe_b" + t, pbm)], inc=["rs_b" + t])
                s.barrier()

        if DEBUG_STAGE >= 4 and N_SUBLAYERS >= 2:
            phase_ffn(0, x_sb)

        def phase_attn_b_groups():
          with contextlib.ExitStack() as pes:
            T = pes.enter_context
            wqb = [T(nc.sbuf_tensor(f"b_wq{i}", [128, DCN * 128], BF16))
                   for i in range(2)]
            wkb = T(nc.sbuf_tensor("b_wk", [128, DCN * 128], BF16))
            wvb = T(nc.sbuf_tensor("b_wv", [128, DCN * 128], BF16))
            cq = T(nc.sbuf_tensor("b_cq", [128, TOK], BF16))
            sqt = T(nc.sbuf_tensor("b_sqt", [128, TOK], BF16))
            ck = T(nc.sbuf_tensor("b_ck", [128, TOK], BF16))
            skt = T(nc.sbuf_tensor("b_skt", [128, TOK], BF16))
            rot_bf = T(nc.sbuf_tensor("b_rot", [128, 128], BF16))
            swm_sb = T(nc.sbuf_tensor("b_swm", [128, 256], F32))
            swm0_sb = T(nc.sbuf_tensor("b_swm0", [128, 256], F32))
            tri_sb = T(nc.sbuf_tensor("b_tri", [128, 128], F32))
            gate_sb = T(nc.sbuf_tensor("b_gate", [128, 1], F32))
            ones128 = T(nc.sbuf_tensor("b_ones", [128, 128], F32))
            qx = T(nc.sbuf_tensor("b_qx", [128, 4 * TOK], BF16))
            qraw = T(nc.sbuf_tensor("b_qraw", [128, 4 * TOK], BF16))
            sq_sb = T(nc.sbuf_tensor("b_sq", [128, 4 * TOK], BF16))
            qtl = qx
            kraw = T(nc.sbuf_tensor("b_kraw", [128, TOK], BF16))
            kx = T(nc.sbuf_tensor("b_kx", [128, TOK], BF16))
            ktl = T(nc.sbuf_tensor("b_ktl", [128, TOK], BF16))
            khat = T(nc.sbuf_tensor("b_khat", [128, TOK], BF16))
            khtok = T(nc.sbuf_tensor("b_khtok", [128, TOK], BF16))
            sk_ext = T(nc.sbuf_tensor("b_skext", [128, 1152], BF16))
            v_ext = T(nc.sbuf_tensor("b_vext", [128, 1152], BF16))
            v_tok = T(nc.sbuf_tensor("b_vtok", [128, 8 * 128], BF16))
            v_win = T(nc.sbuf_tensor("b_vwin", [128, 9 * 128], BF16))
            g_sb = T(nc.sbuf_tensor("b_g", [128, TOK], F32))
            a_sb = T(nc.sbuf_tensor("b_a", [128, TOK], F32))
            eA = T(nc.sbuf_tensor("b_eA", [128, TOK], F32))
            eNA = T(nc.sbuf_tensor("b_eNA", [128, TOK], F32))
            tmpA = T(nc.sbuf_tensor("b_tmpA", [128, 512], F32))
            tmpB = T(nc.sbuf_tensor("b_tmpB", [128, 512], F32))
            tmpC = T(nc.sbuf_tensor("b_tmpC", [128, TOK], F32))
            zrow = T(nc.sbuf_tensor("b_zrow", [1, 1024], F32))
            zrow2 = T(nc.sbuf_tensor("b_zrow2", [1, 1024], F32))
            S_state = T(nc.sbuf_tensor("b_S", [128, 128], F32))
            S0st = T(nc.sbuf_tensor("b_S0", [128, 8 * 128], BF16))
            sprev = T(nc.sbuf_tensor("b_sprev", [128, 128], F32))
            exch_sb = T(nc.sbuf_tensor("b_exch", [128, 256], F32))
            aoff = T(nc.sbuf_tensor("b_aoff", [128, 16], F32))
            o_sb = T(nc.sbuf_tensor("b_o", [128, 4 * TOK], BF16))
            ePf = T(nc.sbuf_tensor("b_ePf", [128, 256], F32))
            ePm = T(nc.sbuf_tensor("b_ePm", [128, 256], F32))
            ePb = T(nc.sbuf_tensor("b_ePb", [128, 256], BF16))
            zsw = T(nc.sbuf_tensor("b_zsw", [128, 4], F32))
            PTb = [T(nc.sbuf_tensor(f"b_PT{i}", [128, 128], BF16)) for i in range(2)]
            ePT = [T(nc.sbuf_tensor(f"b_ePT{i}", [128, 256], BF16)) for i in range(2)]
            p0 = T(nc.psum_tensor("bp0", [128, 512], F32))
            p1 = T(nc.psum_tensor("bp1", [128, 512], F32))
            p2 = T(nc.psum_tensor("bp2", [128, 512], F32))
            p3 = T(nc.psum_tensor("bp3", [128, 512], F32))
            p4 = T(nc.psum_tensor("bp4", [128, 512], F32))
            p5 = T(nc.psum_tensor("bp5", [128, 512], F32))
            p6 = T(nc.psum_tensor("bp6", [128, 512], F32))
            p7 = T(nc.psum_tensor("bp7", [128, 512], BF16))

            for nm, dst, src in (("cq", cq, cosq), ("sqt", sqt, sinq),
                                 ("ck", ck, cosk), ("skt", skt, sink),
                                 ("rotb", rot_bf, rotm)):
                s.op("pool", lambda e, dst=dst, src=src: e.dma_start(
                    out=dst[:], in_=src), inc=["b_cst"], dma=True)
            for dst, src in ((swm_sb, swmask), (swm0_sb, swmask0),
                             (tri_sb, trimask), (gate_sb, gate)):
                s.op("sp", lambda e, dst=dst, src=src: e.dma_start(
                    out=dst[:], in_=src), inc=["b_cst"], dma=True)
            s.op("dve", lambda e: e.memset(ones128[:], 1.0), inc=["b_cst1"])
            CST = [("b_cst", 9 * 16), ("b_cst1", 1), ("const_c", None)]


            for gidx in range(KBGRP):
                tg = "b"
                oev_pg = s.cnt.get("b_oev", 0)
                swe_pg = s.cnt.get("b_swe", 0)
                s.op("pool", lambda e, gidx=gidx: e.dma_start(
                    out=wkb[:], in_=wk[gidx]),
                    wait=[("b_kp", s.cnt.get("b_kp", 0))],
                    inc=["b_ldk"], dma=True)
                s.op("pool", lambda e, gidx=gidx: e.dma_start(
                    out=wvb[:], in_=wv[gidx]),
                    wait=[("b_vp", s.cnt.get("b_vp", 0))],
                    inc=["b_ldv"], dma=True)
                ldk = s.cnt["b_ldk"]
                ldv = s.cnt["b_ldv"]
                s.op("dve", lambda e: e.memset(sk_ext[:, 1088:1152], 0.0),
                     wait=[("b_sc", s.cnt.get("b_sc", 0)),
                           ("b_tr", s.cnt.get("b_tr", 0)),
                           ("b_ex", s.cnt.get("b_ex", 0))], inc=["b_z0"])
                s.op("dve", lambda e: e.memset(v_ext[:, 1088:1152], 0.0),
                     inc=["b_z0"])
                for tt in range(2 if KBK >= 1 else 0):
                    ps = [p0, p1][tt]
                    for dc in range(DCN):
                        s.op("pe", lambda e, dc=dc, tt=tt, ps=ps: e.matmul(
                            ps[:], wkb[:, dc * 128:(dc + 1) * 128],
                            hT[:, dc * TOK + tt * 512: dc * TOK + tt * 512 + 512],
                            start=(dc == 0), stop=(dc == DCN - 1)),
                            wait=([("b_ldk", ldk), ("b_oev", oev_pg)]
                                  if dc == 0 else []),
                            inc=(["b_kp"] if dc == DCN - 1 else []))
                    kpm = s.cnt["b_kp"]
                    if os.environ.get("KNOEV", "0") == "1":
                        continue
                    s.op("dve", lambda e, tt=tt, ps=ps: e.tensor_copy(
                        kraw[:, tt * 512:(tt + 1) * 512], ps[:]),
                        wait=[("b_kp", kpm), ("b_g", s.cnt.get("b_g", 0))],
                        inc=["b_kev"])
                    s.op("act", lambda e, tt=tt: e.activation(
                        kx[:, tt * 512:(tt + 1) * 512],
                        kraw[:, tt * 512:(tt + 1) * 512], AF.Exp),
                        wait=[("b_kev", s.cnt.get("b_kev", 0)),
                              ("b_kt", s.cnt.get("b_kt", 0))],
                        inc=["b_kev"])
                for tt in range(2 if KBK >= 1 else 0):
                    ps = [p0, p1][tt]
                    for dc in range(DCN):
                        s.op("pe", lambda e, dc=dc, tt=tt, ps=ps: e.matmul(
                            ps[:], wvb[:, dc * 128:(dc + 1) * 128],
                            hT[:, dc * TOK + tt * 512: dc * TOK + tt * 512 + 512],
                            start=(dc == 0), stop=(dc == DCN - 1)),
                            wait=([("b_ldv", ldv), ("b_kev", s.cnt.get("b_kev", 0))]
                                  if dc == 0 else []),
                            inc=(["b_vp"] if dc == DCN - 1 else []))
                    vpm = s.cnt["b_vp"]
                    if os.environ.get("KNOEV", "0") == "1":
                        continue
                    s.op("dve", lambda e, tt=tt, ps=ps: e.tensor_copy(
                        v_ext[:, 64 + tt * 512: 64 + (tt + 1) * 512], ps[:]),
                        wait=[("b_vp", vpm), ("b_tr", s.cnt.get("b_tr", 0)),
                              ("b_ex", s.cnt.get("b_ex", 0))],
                        inc=["b_vev"])
                vev = s.cnt.get("b_vev", 0)
                qtms = [0, 0, 0, 0]
                rpems = [0, 0, 0, 0]
                for h in range(4 if KBSUB >= 2 else 0):
                    wqi = wqb[h % 2]
                    s.op("pool", lambda e, h=h, gidx=gidx, wqi=wqi: e.dma_start(
                        out=wqi[:], in_=wq[gidx * 4 + h]),
                        wait=[("b_qp", max(0, s.cnt.get("b_qp", 0) - 2))],
                        inc=["b_ldq"], dma=True)
                    ldq = s.cnt["b_ldq"]
                    for tt in range(2):
                        ps = [p0, p1][(h * 2 + tt) % 2]
                        for dc in range(DCN):
                            s.op("pe", lambda e, dc=dc, tt=tt, ps=ps, wqi=wqi:
                                 e.matmul(
                                ps[:], wqi[:, dc * 128:(dc + 1) * 128],
                                hT[:, dc * TOK + tt * 512: dc * TOK + tt * 512 + 512],
                                start=(dc == 0), stop=(dc == DCN - 1)),
                                wait=([("b_ldq", ldq),
                                       ("b_qt", s.cnt.get("b_qt", 0)),
                                       ("b_vev", vev),
                                       ("b_qev", max(0, s.cnt.get("b_qev", 0) - 2))]
                                      if dc == 0 else []),
                                inc=(["b_qp"] if dc == DCN - 1 else []))
                        qpm = s.cnt["b_qp"]
                        col = h * TOK + tt * 512
                        s.op("dve", lambda e, col=col, ps=ps: e.tensor_copy(
                            qraw[:, col:col + 512], ps[:]),
                            wait=[("b_qp", qpm)], inc=["b_qev"])
                        s.op("act", lambda e, col=col: e.activation(
                            qx[:, col:col + 512],
                            qraw[:, col:col + 512], AF.Exp),
                            wait=[("b_qev", s.cnt.get("b_qev", 0))],
                            inc=["b_qev"])
                    for tt in range(2):
                        col = h * TOK + tt * 512
                        s.op("pe", lambda e, col=col: e.matmul(
                            p2[:], rot_bf[:], qraw[:, col:col + 512],
                            start=True, stop=True),
                            wait=[("b_qev", s.cnt.get("b_qev", 0)),
                                  ("b_kt", s.cnt.get("b_kt", 0)),
                                  ("b_qt", s.cnt.get("b_qt", 0)),
                                  ("b_rpe", s.cnt.get("b_rpe", 0))],
                            inc=["b_rot"])
                        rm = s.cnt["b_rot"]
                        s.op("dve", lambda e, col=col, tt=tt: e.tensor_tensor(
                            tmpA[:, 0:512], qraw[:, col:col + 512],
                            cq[:, tt * 512:(tt + 1) * 512], op=ALU.mult),
                            wait=[("b_rpe2", s.cnt.get("b_rpe2", 0))],
                            inc=["b_rpe2"])
                        s.op("dve", lambda e, col=col, tt=tt: e.tensor_tensor(
                            tmpB[:, 0:512], p2[:, 0:512],
                            sqt[:, tt * 512:(tt + 1) * 512], op=ALU.mult),
                            wait=[("b_rot", rm)], inc=["b_rpe"])
                        s.op("dve", lambda e, col=col: e.tensor_tensor(
                            sq_sb[:, col:col + 512],
                            tmpA[:, 0:512], tmpB[:, 0:512], op=ALU.add),
                            wait=[("b_rpe", None), ("b_rpe2", None)],
                            inc=["b_rpe2"])
                    rpems[h] = s.cnt.get("b_rpe2", 0)

                for tt in range(2 if KBK >= 2 else 0):
                    s.op("pe", lambda e, tt=tt: e.matmul(
                        p2[:], rot_bf[:],
                        kraw[:, tt * 512:(tt + 1) * 512], start=True, stop=True),
                        wait=[("b_kev", s.cnt.get("b_kev", 0)),
                              ("b_rpe", s.cnt.get("b_rpe", 0))] + CST,
                        inc=["b_rot"])
                    rm = s.cnt["b_rot"]
                    s.op("dve", lambda e, tt=tt: e.tensor_tensor(
                        tmpA[:, 0:512], kraw[:, tt * 512:(tt + 1) * 512],
                        ck[:, tt * 512:(tt + 1) * 512], op=ALU.mult),
                        wait=[("b_rpe2", s.cnt.get("b_rpe2", 0))], inc=["b_rpe2"])
                    s.op("dve", lambda e, tt=tt: e.tensor_tensor(
                        tmpB[:, 0:512], p2[:, 0:512],
                        skt[:, tt * 512:(tt + 1) * 512], op=ALU.mult),
                        wait=[("b_rot", rm)], inc=["b_rpe"])
                    s.op("dve", lambda e, tt=tt: e.tensor_tensor(
                        sk_ext[:, 64 + tt * 512: 64 + (tt + 1) * 512],
                        tmpA[:, 0:512], tmpB[:, 0:512], op=ALU.add),
                        wait=[("b_rpe", None), ("b_rpe2", None),
                              ("b_sc", s.cnt.get("b_sc", 0)),
                              ("b_ex", s.cnt.get("b_ex", 0))], inc=["b_rpe2"])
                krm = s.cnt.get("b_rpe2", 0)
                GS = "b_g"

                def gs(eng, fn):
                    if KBK >= 3:
                        s.op(eng, fn, wait=[(GS, s.cnt.get(GS, 0))], inc=[GS])
                if KBK >= 3:
                  s.op("act", lambda e: e.activation(tmpC[:], kraw[:], AF.Abs),
                     wait=[("b_kev", s.cnt.get("b_kev", 0)),
                           (GS, s.cnt.get(GS, 0)),
                           ("b_qt", s.cnt.get("b_qt", 0)),
                           ("b_chn", s.cnt.get("b_chn", 0)),
                           ("b_fold", s.cnt.get("b_fold", 0)),
                           ("b_kh", s.cnt.get("b_kh", 0))], inc=[GS])
                gs("act", lambda e: e.activation(a_sb[:], tmpC[:], AF.Exp, scale=-1.0))
                gs("act", lambda e: e.activation(tmpC[:], a_sb[:], AF.Ln, bias=1.0))
                gs("dve", lambda e: e.tensor_scalar(
                    a_sb[:], kraw[:], 0.0, None, op0=ALU.min))
                gs("dve", lambda e: e.tensor_tensor(
                    g_sb[:], a_sb[:], tmpC[:], op=ALU.subtract))
                for ch in range(8):
                    gs("dve", lambda e, ch=ch: e.tensor_tensor_scan(
                        a_sb[:, ch * 128:(ch + 1) * 128], ones128[:],
                        g_sb[:, ch * 128:(ch + 1) * 128], 0.0,
                        op0=ALU.mult, op1=ALU.add))
                gs("dve", lambda e: e.tensor_scalar(
                    a_sb[:], a_sb[:], -1280.0, None, op0=ALU.max))
                gs("act", lambda e: e.activation(eA[:], a_sb[:], AF.Exp,
                                                 scale=1.0 / 16.0))
                gs("act", lambda e: e.activation(eNA[:], a_sb[:], AF.Exp,
                                                 scale=-1.0 / 16.0))
                gs("dve", lambda e: e.memset(aoff[:, 0:1], 0.0))
                for ch in range(1, 8):
                    gs("dve", lambda e, ch=ch: e.tensor_tensor(
                        aoff[:, ch:ch + 1], aoff[:, ch - 1:ch],
                        a_sb[:, ch * 128 - 1:ch * 128], op=ALU.add))
                gs("act", lambda e: e.activation(aoff[:, 8:16], aoff[:, 0:8],
                                                 AF.Exp, scale=1.0 / 16.0))
                gsm = s.cnt.get(GS, 0)
                for tt in range(2 if KBK >= 4 else 0):
                    s.op("pe", lambda e, tt=tt: e.matmul(
                        p3[0:1, 0:512],
                        ones_bf[:], kx[:, tt * 512:(tt + 1) * 512],
                        start=True, stop=True),
                        wait=[("b_kev", s.cnt.get("b_kev", 0)),
                              ("b_zev", s.cnt.get("b_zev", 0)),
                              ("b_swa", s.cnt.get("b_swa", 0))],
                        inc=["b_zk"])
                    zkm = s.cnt["b_zk"]
                    s.op("dve", lambda e, tt=tt: e.tensor_copy(
                        zrow[0:1, tt * 512:(tt + 1) * 512], p3[0:1, 0:512]),
                        wait=[("b_zk", zkm)], inc=["b_zev"])
                if KBK >= 4:
                    s.op("dve", lambda e: e.reciprocal(zrow2[0:1, :], zrow[0:1, :]),
                         wait=[("b_zev", None)], inc=["b_zev"])
                if KBK >= 5:
                    s.op("dve", lambda e: e.tensor_tensor(
                        tmpC[:], kx[:], eNA[:], op=ALU.mult),
                        wait=[(GS, gsm)], inc=["b_kt"])
                    for tt in range(2):
                        s.op("pe", lambda e, tt=tt: e.matmul(
                            p2[:], ones_row[:],
                            zrow2[0:1, tt * 512:(tt + 1) * 512],
                            start=True, stop=True),
                            wait=[("b_zev", s.cnt.get("b_zev", 0)),
                                  ("b_rpe", s.cnt.get("b_rpe", 0)),
                                  ("b_kt", s.cnt.get("b_kt", 0))],
                            inc=["b_rot"])
                        s.op("dve", lambda e, tt=tt: e.tensor_tensor(
                            ktl[:, tt * 512:(tt + 1) * 512],
                            tmpC[:, tt * 512:(tt + 1) * 512], p2[:, 0:512],
                            op=ALU.mult),
                            wait=[("b_rot", s.cnt.get("b_rot", 0)),
                                  ("b_P", s.cnt.get("b_P", 0))], inc=["b_kt"])
                ktm = s.cnt.get("b_kt", 0)
                for ch in range(8 if KBK >= 5 else 0):
                    s.op("dve", lambda e, ch=ch: e.tensor_scalar(
                        khat[:, ch * 128:(ch + 1) * 128],
                        ktl[:, ch * 128:(ch + 1) * 128],
                        eA[:, ch * 128 + 127: ch * 128 + 128], None, op0=ALU.mult),
                        wait=[("b_kt", ktm), ("b_tr", s.cnt.get("b_tr", 0))]
                        if ch == 0 else [("b_kt", ktm)],
                        inc=["b_kh"])
                khm = s.cnt.get("b_kh", 0)
                trjobs = []
                for ch in range(8):
                    trjobs.append((khat.ap()[:, ch * 128:(ch + 1) * 128],
                                   khtok.ap()[:, ch * 128:(ch + 1) * 128],
                                   ("b_kh", khm)))
                for ch in range(8):
                    trjobs.append((v_ext.ap()[:, 64 + ch * 128: 64 + (ch + 1) * 128],
                                   v_tok.ap()[:, ch * 128:(ch + 1) * 128],
                                   ("b_vev", vev)))
                for j in range(1, 9):
                    trjobs.append((v_ext.ap()[:, j * 128:(j + 1) * 128],
                                   v_win.ap()[:, j * 128:(j + 1) * 128],
                                   ("b_vev", vev)))
                for j0 in range(0, len(trjobs) if KBK >= 6 else 0, 4):
                    grp = trjobs[j0:j0 + 4]
                    for ji, (src, dst, wt) in enumerate(grp):
                        s.op("pe", lambda e, src=src, ji=ji: e.transpose(
                            p7[:, ji * 128:(ji + 1) * 128], src, id_bf[:]),
                            wait=[wt, ("b_tev", s.cnt.get("b_tev", 0))]
                            if ji == 0 else [wt],
                            inc=(["b_tr"] if ji == len(grp) - 1 else []))
                    trm = s.cnt["b_tr"]
                    for ji, (src, dst, wt) in enumerate(grp):
                        s.op("dve", lambda e, dst=dst, ji=ji: e.tensor_copy(
                            dst, p7[:, ji * 128:(ji + 1) * 128]),
                            wait=[("b_tr", trm)],
                            inc=(["b_tev"] if ji == len(grp) - 1 else []))
                tevm = s.cnt.get("b_tev", 0)
                if KBK >= 7:
                    s.op("dve", lambda e: e.memset(S_state[:], 0.0),
                         wait=[("b_chn", s.cnt.get("b_chn", 0)),
                               ("b_oacc", s.cnt.get("b_oacc", 0))], inc=["b_chn"])
                for ch in range(8 if KBK >= 7 else 0):
                    psD = [p5, p6][ch // 4]
                    sl = (ch % 4) * 128
                    s.op("pe", lambda e, ch=ch, psD=psD, sl=sl: e.matmul(
                        psD[:, sl:sl + 128], khtok[:, ch * 128:(ch + 1) * 128],
                        v_tok[:, ch * 128:(ch + 1) * 128],
                        start=True, stop=True),
                        wait=([("b_tev", tevm), ("b_Pe", s.cnt.get("b_Pe", 0)),
                               ("b_chn", s.cnt.get("b_chn", 0))]
                              if ch == 0 else []),
                        inc=(["b_dS"] if ch == 7 else []))
                dsm = s.cnt.get("b_dS", 0)
                for ch in range(8 if KBK >= 7 else 0):
                    psD = [p5, p6][ch // 4]
                    sl = (ch % 4) * 128
                    s.op("dve", lambda e, ch=ch: e.tensor_copy(
                        S0st[:, ch * 128:(ch + 1) * 128], S_state[:]),
                        wait=[("b_dS", dsm), ("b_chn", s.cnt.get("b_chn", 0)),
                              ("b_fold", s.cnt.get("b_fold", 0))], inc=["b_chn"])
                    s.op("dve", lambda e, ch=ch, psD=psD, sl=sl:
                         e.scalar_tensor_tensor(
                        S_state[:], S_state[:],
                        eA[:, ch * 128 + 127: ch * 128 + 128],
                        psD[:, sl:sl + 128], op0=ALU.mult, op1=ALU.add),
                        wait=[("b_chn", s.cnt.get("b_chn", 0))],
                        inc=["b_chn"])
                chm = s.cnt.get("b_chn", 0)
                if KBK >= 8:
                  s.op("pool", lambda e, gidx=gidx: e.dma_start(
                    out=exch_loc.ap()[gidx, :, 0:128], in_=S_state[:]),
                    wait=[("b_chn", chm)], inc=["b_ex"], dma=True)
                  s.op("pool", lambda e, gidx=gidx: e.dma_start(
                    out=exch_loc.ap()[gidx, :, 128:192],
                    in_=sk_ext[:, 1024:1088]),
                    wait=[("b_rpe2", None)], inc=["b_ex"], dma=True)
                  s.op("pool", lambda e, gidx=gidx: e.dma_start(
                    out=exch_loc.ap()[gidx, :, 192:256],
                    in_=v_ext[:, 1024:1088]),
                    wait=[("b_vev", vev)], inc=["b_ex"], dma=True)
                  s.op("pool", lambda e, gidx=gidx: e.collective_compute(
                    "AllGather", ALU.bypass,
                    replica_groups=[[0, 1], [2, 3], [4, 5], [6, 7]],
                    ins=[exch_loc.ap()[gidx]], outs=[exch_pair.ap()[gidx]]),
                    wait=[("b_ex", None)], inc=[("ccb", 1)])

                for h in range(4 if KBSUB >= 2 else 0):
                    for tt in range(2):
                        col = h * TOK + tt * 512
                        s.op("pe", lambda e, col=col, tt=tt: e.matmul(
                            p3[0:1, 0:512],
                            ones_bf[:], qx[:, col:col + 512],
                            start=True, stop=True),
                            wait=[("b_qev", s.cnt.get("b_qev", 0)),
                                  ("b_zev", s.cnt.get("b_zev", 0)),
                                  ("b_swa", s.cnt.get("b_swa", 0))],
                            inc=["b_zk"])
                        zkm = s.cnt["b_zk"]
                        s.op("dve", lambda e, tt=tt: e.tensor_copy(
                            zrow[0:1, tt * 512:(tt + 1) * 512], p3[0:1, 0:512]),
                            wait=[("b_zk", zkm)], inc=["b_zev"])
                    s.op("dve", lambda e: e.reciprocal(zrow2[0:1, :], zrow[0:1, :]),
                         wait=[("b_zev", None)], inc=["b_zev"])
                    s.op("dve", lambda e: e.tensor_scalar(
                        zrow2[0:1, :], zrow2[0:1, :], 0.5, None, op0=ALU.mult),
                        wait=[("b_zev", None)], inc=["b_zev"])
                    s.op("dve", lambda e, h=h: e.tensor_tensor(
                        tmpC[:], qx[:, h * TOK:(h + 1) * TOK], eA[:], op=ALU.mult),
                        wait=[(GS, gsm), ("b_kt", ktm),
                              ("b_qt", s.cnt.get("b_qt", 0))],
                        inc=["b_qt"])
                    for tt in range(2):
                        s.op("pe", lambda e, tt=tt: e.matmul(
                            p2[:], ones_row[:],
                            zrow2[0:1, tt * 512:(tt + 1) * 512],
                            start=True, stop=True),
                            wait=[("b_zev", s.cnt.get("b_zev", 0)),
                                  ("b_rpe", s.cnt.get("b_rpe", 0)),
                                  ("b_qt", s.cnt.get("b_qt", 0))],
                            inc=["b_rot"])
                        s.op("dve", lambda e, h=h, tt=tt: e.tensor_tensor(
                            qtl[:, h * TOK + tt * 512: h * TOK + (tt + 1) * 512],
                            tmpC[:, tt * 512:(tt + 1) * 512], p2[:, 0:512],
                            op=ALU.mult),
                            wait=[("b_rot", s.cnt.get("b_rot", 0)),
                                  ("b_zk", s.cnt.get("b_zk", 0)),
                                  ("b_P", s.cnt.get("b_P", 0))], inc=["b_qt"])
                    qtms[h] = s.cnt.get("b_qt", 0)
                qtm = s.cnt.get("b_qt", 0)

                if KBK >= 8:
                  s.op("sp", lambda e, gidx=gidx: e.dma_start(
                    out=exch_sb[:], in_=exch_pair.ap()[gidx][0:128, :]),
                    wait=[("ccb", gidx + 1),
                          ("b_fold", s.cnt.get("b_fold", 0))],
                    inc=["b_exl"], dma=True)
                exm = s.cnt.get("b_exl", 0)
                s.op("dve", lambda e: e.tensor_scalar(
                    sprev[:], exch_sb[:, 0:128], gate_sb[:, 0:1], None,
                    op0=ALU.mult),
                    wait=[("b_exl", exm)] + CST, inc=["b_fold"])
                s.op("dve", lambda e: e.tensor_scalar(
                    sk_ext[:, 0:64], exch_sb[:, 128:192], gate_sb[:, 0:1], None,
                    op0=ALU.mult),
                    wait=[("b_exl", exm), ("b_sc", s.cnt.get("b_sc", 0))],
                    inc=["b_fold"])
                s.op("dve", lambda e: e.tensor_scalar(
                    v_ext[:, 0:64], exch_sb[:, 192:256], gate_sb[:, 0:1], None,
                    op0=ALU.mult),
                    wait=[("b_exl", exm)], inc=["b_fold"])
                fm = s.cnt["b_fold"]
                s.op("pe", lambda e: e.transpose(
                    p7[:, 0:128], v_ext[:, 0:128], id_bf[:]),
                    wait=[("b_fold", fm), ("b_tev", tevm)], inc=["b_tr"])
                s.op("dve", lambda e: e.tensor_copy(v_win[:, 0:128], p7[:, 0:128]),
                     wait=[("b_tr", s.cnt["b_tr"])], inc=["b_tev"])
                for ch in range(8):
                    s.op("dve", lambda e, ch=ch: e.scalar_tensor_tensor(
                        S0st[:, ch * 128:(ch + 1) * 128], sprev[:],
                        aoff[:, 8 + ch: 9 + ch],
                        S0st[:, ch * 128:(ch + 1) * 128],
                        op0=ALU.mult, op1=ALU.add),
                        wait=[("b_fold", fm), ("b_chn", chm),
                              (GS, gsm)], inc=["b_fold"])
                foldm = s.cnt["b_fold"]

                NU = 32 if KBSUB >= 3 else 0
                pms = {}
                scms = {}

                def emit_A(u):
                    h, ch = divmod(u, 8)
                    qcol = h * TOK + ch * 128
                    psP = [p5, p6][u % 2]
                    psS = [p4, p3][u % 2]
                    s.op("pe", lambda e, qcol=qcol, ch=ch, psP=psP: e.matmul(
                        psP[:, 0:128], ktl[:, ch * 128:(ch + 1) * 128],
                        qtl[:, qcol:qcol + 128], start=True, stop=True),
                        wait=[("b_qt", qtms[h]), ("b_kt", ktm),
                              ("b_chn", chm),
                              ("b_Pe", s.cnt.get("b_Pe", 0))],
                        inc=["b_P"])
                    pms[u] = s.cnt["b_P"]
                    s.op("pe", lambda e, qcol=qcol, ch=ch, psS=psS: e.matmul(
                        psS[:, 0:256], sq_sb[:, qcol:qcol + 128],
                        sk_ext[:, ch * 128: ch * 128 + 256],
                        start=True, stop=True),
                        wait=[("b_rpe2", krm), ("b_fold", foldm),
                              ("b_zev", s.cnt.get("b_zev", 0)),
                              ("b_swa", s.cnt.get("b_swa", 0))],
                        inc=["b_sc"])
                    scms[u] = s.cnt["b_sc"]

                def emit_B(u):
                    h, ch = divmod(u, 8)
                    qcol = h * TOK + ch * 128
                    psP = [p5, p6][u % 2]
                    psS = [p4, p3][u % 2]
                    psO = [p0, p1][u % 2]
                    PT = PTb[u % 2]
                    eT = ePT[u % 2]
                    s.op("dve", lambda e, psP=psP, PT=PT: e.tensor_tensor(
                        PT[:], psP[:, 0:128], tri_sb[:], op=ALU.mult),
                        wait=[("b_P", pms[u]),
                              ("b_oacc", max(0, s.cnt.get("b_oacc", 0) - 1))],
                        inc=["b_Pe"])
                    pem = s.cnt["b_Pe"]
                    s.op("act", lambda e, psS=psS: e.activation(
                        ePf[:, 0:256], psS[:, 0:256], AF.Exp),
                        wait=[("b_sc", scms[u]),
                              ("b_swe", max(0, s.cnt.get("b_swe", 0) - 3))],
                        inc=["b_swa"])
                    swam = s.cnt["b_swa"]
                    msk = swm0_sb if ch == 0 else swm_sb
                    s.op("dve", lambda e, msk=msk: e.tensor_tensor(
                        ePm[:, 0:256], ePf[:, 0:256], msk[:], op=ALU.mult),
                        wait=[("b_swa", swam)] + CST, inc=["b_swe"])
                    s.op("dve", lambda e: e.reduce_sum(
                        zsw[:, 0:1], ePm[:, 0:256], axis=AX),
                        wait=[("b_swe", None)], inc=["b_swe"])
                    s.op("dve", lambda e: e.reciprocal(zsw[:, 1:2], zsw[:, 0:1]),
                         wait=[("b_swe", None)], inc=["b_swe"])
                    s.op("dve", lambda e: e.tensor_scalar(
                        ePb[:, 0:256], ePm[:, 0:256], zsw[:, 1:2], 0.5,
                        op0=ALU.mult, op1=ALU.mult),
                        wait=[("b_swe", None), ("b_tr", s.cnt.get("b_tr", 0))],
                        inc=["b_swe"])
                    swem = s.cnt["b_swe"]
                    for ji in range(2):
                        s.op("pe", lambda e, ji=ji: e.transpose(
                            p7[:, ji * 128:(ji + 1) * 128],
                            ePb[:, ji * 128:(ji + 1) * 128], id_bf[:]),
                            wait=[("b_swe", swem),
                                  ("b_tev", s.cnt.get("b_tev", 0))]
                            if ji == 0 else [],
                            inc=(["b_tr"] if ji == 1 else []))
                    s.op("dve", lambda e, eT=eT: e.tensor_copy(
                        eT[:, 0:256], p7[:, 0:256]),
                        wait=[("b_tr", s.cnt["b_tr"]),
                              ("b_oacc", max(0, s.cnt.get("b_oacc", 0) - 1))],
                        inc=["b_tev"])
                    tevu = s.cnt["b_tev"]
                    s.op("pe", lambda e, ch=ch, qcol=qcol, psO=psO: e.matmul(
                        psO[:, 0:128], S0st[:, ch * 128:(ch + 1) * 128],
                        qtl[:, qcol:qcol + 128], start=True, stop=False),
                        wait=[("b_fold", foldm), ("b_qt", qtms[h]),
                              ("b_qev", s.cnt.get("b_qev", 0)),
                              ("b_oev", max(0, s.cnt.get("b_oev", 0) - 1))])
                    s.op("pe", lambda e, ch=ch, psO=psO, PT=PT: e.matmul(
                        psO[:, 0:128], v_tok[:, ch * 128:(ch + 1) * 128],
                        PT[:], start=False, stop=False),
                        wait=[("b_Pe", pem)])
                    s.op("pe", lambda e, ch=ch, psO=psO, eT=eT: e.matmul(
                        psO[:, 0:128], v_win[:, ch * 128:(ch + 1) * 128],
                        eT[:, 0:128], start=False, stop=False),
                        wait=[("b_tev", tevu)])
                    s.op("pe", lambda e, ch=ch, psO=psO, eT=eT: e.matmul(
                        psO[:, 0:128], v_win[:, (ch + 1) * 128:(ch + 2) * 128],
                        eT[:, 128:256], start=False, stop=True),
                        inc=["b_oacc"])
                    oam = s.cnt["b_oacc"]
                    s.op("act", lambda e, qcol=qcol, psO=psO: e.activation(
                        o_sb[:, qcol:qcol + 128], psO[:, 0:128], AF.Copy),
                        wait=[("b_oacc", oam), ("b_ost", 16 * gidx)],
                        inc=["b_oev"])

                if NU:
                    emit_A(0)
                    for u in range(NU):
                        if u + 1 < NU:
                            emit_A(u + 1)
                        emit_B(u)
                s.op("sp", lambda e, gidx=gidx: e.dma_start(
                    out=o_dram.ap()[gidx], in_=o_sb[:]),
                    wait=[("b_oev", None)], inc=["b_ost"], dma=True)
            s.barrier()

        def phase_wo():
          with contextlib.ExitStack() as pes:
            T = pes.enter_context
            o_all = T(nc.sbuf_tensor("w_oall", [128, DCN * TOK], BF16))
            wob = [T(nc.sbuf_tensor(f"w_wo{i}", [128, DCN * 256], BF16))
                   for i in range(2)]
            xb = [T(nc.sbuf_tensor(f"w_xb{i}", [128, TCN * 256], BF16))
                  for i in range(2)]
            xo = [T(nc.sbuf_tensor(f"w_xo{i}", [128, TCN * 256], BF16))
                  for i in range(2)]
            ps_w = [T(nc.psum_tensor(f"w_ps{i}", [128, 256], F32))
                    for i in range(2)]
            for gidx in range(8):
                s.op("pool", lambda e, gidx=gidx: e.dma_start(
                    out=o_all[:, gidx * 4096:(gidx + 1) * 4096],
                    in_=o_dram.ap()[gidx]),
                    wait=[("b_ost", 16 * min(gidx + 1, KBGRP))],
                    inc=["w_ldo"], dma=True)
            ldo = s.cnt["w_ldo"]
            xsv = x_spill.ap().rearrange("p (c f) -> p c f", c=TCN)
            x2v = x2_dram.ap().rearrange("p (c f) -> p c f", c=TCN)
            for rb in range(16):
                buf = wob[rb % 2]
                bv = buf.ap().rearrange("p (c o) -> p c o", c=DCN)
                s.op("pool", lambda e, buf=buf, rb=rb: e.dma_start(
                    out=buf[:], in_=wo[rb]),
                    wait=([("w_pe", 8 * (rb - 1))] if rb >= 2 else []),
                    inc=["w_ldw"], dma=True)
                ldw = s.cnt["w_ldw"]
                xbi = xb[rb % 2]
                s.op("sp", lambda e, xbi=xbi, rb=rb: e.dma_start(
                    out=xbi.ap().rearrange("p (c f) -> p c f", c=TCN),
                    in_=xsv[:, :, rb * 256:(rb + 1) * 256]),
                    wait=[("st_sp", 16)] + ([("w_ev", 8 * (rb - 1))]
                                            if rb >= 2 else []),
                    inc=["w_ldx"], dma=True)
                ldx = s.cnt["w_ldx"]
                xoi = xo[rb % 2]
                for tc in range(TCN):
                    ps = ps_w[tc % 2]
                    for dc in range(DCN):
                        s.op("pe", lambda e, bv=bv, dc=dc, tc=tc, ps=ps: e.matmul(
                            ps[:], o_all[:, dc * TOK + tc * 128: dc * TOK + (tc + 1) * 128],
                            bv[:, dc, :],
                            start=(dc == 0), stop=(dc == DCN - 1)),
                            wait=([("w_ldw", ldw), ("w_ldo", ldo),
                                   ("w_ev", max(0, s.cnt.get("w_ev", 0) - 1))]
                                  if dc == 0 else []),
                            inc=(["w_pe"] if dc == DCN - 1 else []))
                    wpm = s.cnt["w_pe"]
                    s.op("dve", lambda e, xbi=xbi, xoi=xoi, tc=tc, ps=ps:
                         e.tensor_tensor(
                        xoi[:, tc * 256:(tc + 1) * 256],
                        xbi[:, tc * 256:(tc + 1) * 256], ps[:], op=ALU.add),
                        wait=[("w_pe", wpm), ("w_ldx", ldx),
                              ("st_x2", max(0, 16 * (rb - 1)))],
                        inc=["w_ev"])
                s.op("sp", lambda e, xoi=xoi, rb=rb: e.dma_start(
                    out=x2v[:, :, rb * 256:(rb + 1) * 256],
                    in_=xoi.ap().rearrange("p (c f) -> p c f", c=TCN)),
                    wait=[("w_ev", None)], inc=["st_x2"], dma=True)
            s.barrier()

        run_b = DEBUG_STAGE >= 7 and N_SUBLAYERS >= 3
        x_final = x_sb
        if run_b:
            with contextlib.ExitStack() as pes:
                hid = [pes.enter_context(
                    nc.sbuf_tensor(f"hidB{i}", [128, DM], BF16)) for i in range(2)]
                sqs = [pes.enter_context(
                    nc.sbuf_tensor(f"sqB{i}", [128, DM], F32)) for i in range(2)]
                ps_tr = [pes.enter_context(
                    nc.psum_tensor(f"psTrB{i}", [128, 512], BF16)) for i in range(2)]
                phase_ln(x_sb, hid, ps_tr, sqs)
            s.op("sp", lambda e: e.dma_start(out=x_spill[:, :], in_=x_sb[:]),
                 inc=["st_sp"], dma=True)
            for e_ in Sched.ENG:
                s.op(e_, None, wait=[("st_sp", 16)])
            s1.close()

            phase_attn_b_groups()
            phase_wo()

            with contextlib.ExitStack() as pes:
                x2_sb = pes.enter_context(
                    nc.sbuf_tensor("x2_sb", [128, TCN * DM], BF16))
                s.op("sp", lambda e: e.dma_start(out=x2_sb[:], in_=x2_dram[:, :]),
                     wait=[("st_x2", 16 * 16)], inc=["ld_x2"], dma=True)
                for e_ in Sched.ENG:
                    s.op(e_, None, wait=[("ld_x2", 16)])
                if DEBUG_STAGE >= 8 and N_SUBLAYERS >= 4:
                    phase_ffn(1, x2_sb)
                s.op("pool", lambda e: e.dma_start(
                    out=out.rearrange("(c p) f -> p c f", p=128),
                    in_=x2_sb.ap().rearrange("p (c f) -> p c f", c=TCN)),
                    inc=["st_out"], dma=True)
                if s.cnt.get("st_dbg", 0) == 0:
                    s.op("pool", lambda e: e.dma_start(
                        out=dbg, in_=x2_sb[:, 0:TCN * 128]),
                        inc=["st_dbg"], dma=True)
                    s.op("pool", lambda e: e.dma_start(
                        out=dbgb[:, :], in_=x2_sb[:, 0:128]),
                        inc=["st_dbg"], dma=True)
                s.op("sp", None, wait=[("st_out", None), ("st_dbg", None)])
        else:
            s.op("pool", lambda e: e.dma_start(
                out=out.rearrange("(c p) f -> p c f", p=128),
                in_=x_sb.ap().rearrange("p (c f) -> p c f", c=TCN)),
                inc=["st_out"], dma=True)
            if s.cnt.get("st_dbg", 0) == 0:
                s.op("pool", lambda e: e.dma_start(out=dbg, in_=x_sb[:, 0:TCN * 128]),
                     inc=["st_dbg"], dma=True)
                s.op("pool", lambda e: e.dma_start(
                    out=dbgb[:, :], in_=x_sb[:, 0:128]),
                    inc=["st_dbg"], dma=True)
            s.op("sp", None, wait=[("st_out", None), ("st_dbg", None)])
            s1.close()

        with nc.Block() as block:
            block.tensor(lambda e: [f(e) for f in s.q["pe"]])
            block.vector(lambda e: [f(e) for f in s.q["dve"]])
            block.scalar(lambda e: [f(e) for f in s.q["act"]])
            block.gpsimd(lambda e: [f(e) for f in s.q["pool"]])
            block.sync(lambda e: [f(e) for f in s.q["sp"]])

    return nc


def _host_consts(core):
    ident = np.eye(128, dtype=np.float32)
    rot = np.zeros((128, 128), dtype=np.float32)
    for dp in range(64):
        rot[dp + 64, dp] = -1.0
        rot[dp, dp + 64] = 1.0
    pos = (core % 2) * 1024 + np.arange(TOK, dtype=np.float64)
    inv = 1.0 / (10000.0 ** (2.0 * np.arange(64, dtype=np.float64) / 128.0))
    fr = pos[None, :] * inv[:, None]
    emb = np.concatenate([fr, fr], axis=0)
    cos = np.cos(emb).astype(np.float32)
    sin = np.sin(emb).astype(np.float32)
    scl = 1.0 / np.sqrt(128.0)
    tri = np.triu(np.ones((128, 128), dtype=np.float32))
    i = np.arange(128)[:, None]
    j = np.arange(256)[None, :]
    kpos = j - 64
    m = (kpos <= i) & (kpos >= i - 64)
    sw = m.astype(np.float32)
    sw0 = sw.copy()
    if core % 2 == 0:
        sw0[:64, :64] = 0.0
    gate = np.full((128, 1), 1.0 if core % 2 == 1 else 0.0, dtype=np.float32)
    d0 = np.ones((128, TOK), dtype=np.float32)
    d0[:, ::128] = 0.0
    return dict(
        identf=ident, rotm=rot,
        cosq=(cos * scl).astype(np.float32), sinq=(sin * scl).astype(np.float32),
        cosk=cos, sink=sin,
        trimask=tri, swmask=sw, swmask0=sw0, gate=gate, d0mask=d0,
    )


def _colblocks(W, width):
    IN, OUT = W.shape
    nb = OUT // width
    return np.ascontiguousarray(
        W.reshape(IN // 128, 128, nb, width).transpose(2, 1, 0, 3)
        .reshape(nb, 128, (IN // 128) * width))


def kernel(L0_attn_a_x, L0_attn_a_w1, L0_attn_a_w2, L0_attn_a_b2,
           L0_attn_a_sparse_mask, L0_attn_b_Wq, L0_attn_b_Wk, L0_attn_b_Wv,
           L0_attn_b_Wo, L0_ffn_base_W, L0_ffn_router_W, L0_ffn_lora_A,
           L0_ffn_lora_B):
    x = np.ascontiguousarray(np.asarray(L0_attn_a_x).reshape(NCORE * TOK, DM))
    w1 = _colblocks(np.asarray(L0_attn_a_w1), 256)
    w2t_full = np.asarray(L0_attn_a_w2).T
    mask = np.ascontiguousarray(L0_attn_a_sparse_mask)
    b2 = np.ascontiguousarray(np.asarray(L0_attn_a_b2).reshape(1, DM))
    base_wt = _colblocks(np.asarray(L0_ffn_base_W).T, 256)
    loraa_t = _colblocks(
        np.asarray(L0_ffn_lora_A).transpose(1, 0, 2).reshape(DM, 128), 128)[0]
    lorab_f = np.ascontiguousarray(np.asarray(L0_ffn_lora_B).reshape(128, DM))
    rtp = np.zeros((DM, 128), dtype=np.float32)
    rtp[:, :8] = np.asarray(L0_ffn_router_W).T
    routertp = _colblocks(rtp, 128)[0]
    wq = _colblocks(np.asarray(L0_attn_b_Wq), 128)
    wk = _colblocks(np.asarray(L0_attn_b_Wk), 128)
    wv = _colblocks(np.asarray(L0_attn_b_Wv), 128)
    wo = _colblocks(np.asarray(L0_attn_b_Wo), 256)

    nc = build_program()
    in_maps = []
    for c in range(NCORE):
        m = dict(
            x=np.ascontiguousarray(x[c * TOK:(c + 1) * TOK]),
            w1=w1,
            w2t=_colblocks(
                np.ascontiguousarray(w2t_full[:, c * 512:(c + 1) * 512]), 512)[0],
            maskc=np.ascontiguousarray(mask[:, c * 512:(c + 1) * 512]),
            b2=b2, base_wt=base_wt, loraa_t=loraa_t, lorab_f=lorab_f,
            routertp=routertp, wq=wq, wk=wk, wv=wv, wo=wo,
        )
        m.update(_host_consts(c))
        in_maps.append(m)
    res = run_bass_kernel_spmd(nc, in_maps, core_ids=list(range(NCORE)))
    global LAST_RES
    LAST_RES = res
    outs = [res.results[c]["out"] for c in range(NCORE)]
    y = np.concatenate(outs, axis=0).reshape(4, 2048, DM).astype(np.float32)
    return y


# version 05
# speedup vs baseline: 1.1723x; 1.0265x over previous
import contextlib
import numpy as np
import concourse.bass as bass
import concourse.mybir as mybir
from concourse.bass_utils import run_bass_kernel_spmd

F32 = mybir.dt.float32
BF16 = mybir.dt.bfloat16
AF = mybir.ActivationFunctionType
ALU = mybir.AluOpType
AX = mybir.AxisListType.X

NCORE = 8
TOK = 1024
TCN = TOK // 128
DM = 4096
DCN = DM // 128
EPS = 1e-5

N_SUBLAYERS = 4
import os
DEBUG_STAGE = int(os.environ.get("KSTAGE", "99"))
KBSUB = int(os.environ.get("KBSUB", "3"))
KBGRP = int(os.environ.get("KBGRP", "8"))
KBK = int(os.environ.get("KBK", "8"))


class Sched:

    ENG = ("pe", "dve", "act", "pool", "sp")

    def __init__(self, nc, es):
        self.nc = nc
        self.es = es
        self.q = {e: [] for e in self.ENG}
        self.sems = {}
        self.cnt = {}
        self.last_wait = {e: {} for e in self.ENG}

    def sem(self, name):
        if name not in self.sems:
            self.sems[name] = self.es.enter_context(self.nc.semaphore(name))
            self.cnt[name] = 0
        return self.sems[name]

    def op(self, eng, fn, wait=(), inc=(), dma=False):
        wl = []
        for w in wait:
            name, val = w if isinstance(w, tuple) else (w, self.cnt[w])
            if val is None:
                val = self.cnt.get(name, 0)
            self.sem(name)
            if val > 0 and val > self.last_wait[eng].get(name, -1):
                self.last_wait[eng][name] = val
                wl.append((self.sems[name], val))
        il = []
        for i in inc:
            if isinstance(i, tuple):
                name, n = i
            else:
                name, n = i, (16 if dma else 1)
            self.sem(name)
            self.cnt[name] += n
            il.append((self.sems[name], n))

        def play(e, fn=fn, wl=wl, il=il):
            for s, v in wl:
                e.wait_ge(s, v)
            inst = fn(e) if fn is not None else None
            if il:
                assert inst is not None
                for s, n in il:
                    inst.then_inc(s, n)

        self.q[eng].append(play)

    def barrier(self):
        def dn(x):
            x.drain()
            return x.nop()
        for e in self.ENG:
            self.op(e, dn, inc=[("bar_" + e, 1)])
        snap = [(("bar_" + e), self.cnt["bar_" + e]) for e in self.ENG]
        for e in self.ENG:
            self.op(e, None, wait=snap)


def build_program():
    nc = bass.Bass("TRN2", target_bir_lowering=False, debug=False)

    def inp(name, shape, dt=F32):
        return nc.dram_tensor(name, shape, dt, kind="ExternalInput").ap()

    x_in = inp("x", [TOK, DM])
    w1 = inp("w1", [16, 128, DCN * 256])
    w2t = inp("w2t", [128, DCN * 512])
    maskc = inp("maskc", [DM, 512])
    b2 = inp("b2", [1, DM])
    base_wt = inp("base_wt", [16, 128, DCN * 256])
    loraa_t = inp("loraa_t", [128, DCN * 128])
    lorab_f = inp("lorab_f", [128, DM])
    routertp = inp("routertp", [128, DCN * 128])
    wq = inp("wq", [32, 128, DCN * 128])
    wk = inp("wk", [8, 128, DCN * 128])
    wv = inp("wv", [8, 128, DCN * 128])
    wo = inp("wo", [16, 128, DCN * 256])
    identf = inp("identf", [128, 128])
    rotm = inp("rotm", [128, 128])
    cosq = inp("cosq", [128, TOK])
    sinq = inp("sinq", [128, TOK])
    cosk = inp("cosk", [128, TOK])
    sink = inp("sink", [128, TOK])
    trimask = inp("trimask", [128, 128])
    swmask = inp("swmask", [128, 256])
    swmask0 = inp("swmask0", [128, 256])
    gate = inp("gate", [128, 1])
    d0mask = inp("d0mask", [128, TOK])

    out = nc.dram_tensor("out", [TOK, DM], F32, kind="ExternalOutput").ap()
    dbg = nc.dram_tensor("dbg", [128, TCN * 128], F32, kind="ExternalOutput").ap()
    dbgb = nc.dram_tensor("dbgb", [128, 128], F32, kind="ExternalOutput").ap()

    attn_loc = nc.dram_tensor("attn_loc", [128, DCN * 512], BF16)
    attn_all = nc.dram_tensor("attn_all", [NCORE * 128, DCN * 512], BF16)
    x_spill = nc.dram_tensor("x_spill", [128, TCN * DM], BF16)
    x2_dram = nc.dram_tensor("x2_dram", [128, TCN * DM], BF16)
    o_dram = nc.dram_tensor("o_dram", [8, 128, 4 * TOK], BF16)
    exch_loc = nc.dram_tensor("exch_loc", [8, 128, 256], F32)
    exch_pair = nc.dram_tensor("exch_pair", [8, 256, 256], F32)
    zdram = nc.dram_tensor("zdram", [8, 5, 1024], F32)

    es_root = contextlib.ExitStack()
    with es_root as es:
        s = Sched(nc, es)

        hT = es.enter_context(nc.sbuf_tensor("hT", [128, DCN * TOK], BF16, side="right"))
        id_bf = es.enter_context(nc.sbuf_tensor("id_bf", [128, 128], BF16, side="right"))
        ones_bf = es.enter_context(nc.sbuf_tensor("ones_bf", [128, 1], BF16, side="right"))
        ones_row = es.enter_context(nc.sbuf_tensor("ones_row", [1, 128], F32, side="right"))
        half_row = es.enter_context(nc.sbuf_tensor("half_row", [1, 128], F32, side="right"))
        stat = es.enter_context(nc.sbuf_tensor("stat", [128, 48], F32, side="right"))
        eps_sb = es.enter_context(nc.sbuf_tensor("eps_sb", [128, 1], F32, side="right"))
        lnh_sb = es.enter_context(nc.sbuf_tensor("lnh_sb", [128, 1], F32, side="right"))

        s1 = contextlib.ExitStack()
        x_sb = s1.enter_context(nc.sbuf_tensor("x_sb", [128, TCN * DM], BF16))

        s.op("pool", lambda e: e.dma_start(
            out=x_sb.ap().rearrange("p (c f) -> p c f", c=TCN),
            in_=x_in.rearrange("(c p) f -> p c f", p=128)),
            inc=["ld_x"], dma=True)
        s.op("pool", lambda e: e.dma_start(out=id_bf[:], in_=identf),
             inc=["ld_const"], dma=True)
        s.op("dve", lambda e: e.memset(ones_bf[:], 1.0), inc=["const_c"])
        s.op("dve", lambda e: e.memset(eps_sb[:], EPS), inc=["const_c"])
        s.op("dve", lambda e: e.memset(lnh_sb[:], -0.6931471805599453), inc=["const_c"])
        s.op("dve", lambda e: e.memset(ones_row[:], 1.0), inc=["const_c"])
        s.op("dve", lambda e: e.memset(half_row[:], 0.5), inc=["const_c"])

        ln_id = [0]

        def phase_ln(xt, hidden_scratch, ps_tr, sq_scratch):
            tag = "ln"
            ln_id[0] += 1
            b_rd = s.cnt.get("rd_" + tag, 0)
            b_act = s.cnt.get("act_" + tag, 0)
            b_pe = s.cnt.get("pe_" + tag, 0)
            for tc in range(TCN):
                xc = xt[:, tc * DM:(tc + 1) * DM]
                sq_i = sq_scratch[tc % 2]
                ht_i = hidden_scratch[tc % 2]
                c0 = 8 * tc
                sc0 = (tc % 2) * 6
                s.op("act", lambda e, xc=xc, sq=sq_i: e.activation(
                    sq[:], xc, AF.Square),
                    wait=[("rd_" + tag, b_rd + max(0, tc - 1))],
                    inc=["act_" + tag])
                s.op("dve", lambda e, xc=xc, sc0=sc0: e.reduce_sum(
                    stat[:, sc0:sc0 + 1], xc, axis=AX),
                    wait=[("dve_" + tag, max(0, s.cnt.get("dve_" + tag, 0) - 6))],
                    inc=["dve_" + tag])
                s.op("dve", lambda e, sq=sq_i, sc0=sc0: e.reduce_sum(
                    stat[:, sc0 + 1:sc0 + 2], sq[:], axis=AX),
                    wait=[("act_" + tag, b_act + 3 * tc + 1)], inc=["rd_" + tag])
                s.op("dve", lambda e, sc0=sc0: e.tensor_scalar(
                    stat[:, sc0 + 2:sc0 + 3], stat[:, sc0:sc0 + 1],
                    1.0 / DM, None, op0=ALU.mult),
                    wait=[("rd_" + tag, None)], inc=["dve_" + tag])
                s.op("dve", lambda e, sc0=sc0: e.tensor_scalar(
                    stat[:, sc0 + 4:sc0 + 5], stat[:, sc0 + 1:sc0 + 2],
                    1.0 / DM, None, op0=ALU.mult), inc=["dve_" + tag])
                s.op("dve", lambda e, sc0=sc0: e.tensor_tensor(
                    stat[:, sc0 + 5:sc0 + 6], stat[:, sc0 + 2:sc0 + 3],
                    stat[:, sc0 + 2:sc0 + 3], op=ALU.mult),
                    wait=[("dve_" + tag, None)], inc=["dve_" + tag])
                s.op("dve", lambda e, sc0=sc0: e.tensor_tensor(
                    stat[:, sc0 + 4:sc0 + 5], stat[:, sc0 + 4:sc0 + 5],
                    stat[:, sc0 + 5:sc0 + 6], op=ALU.subtract),
                    wait=[("dve_" + tag, None)], inc=["dve_" + tag])
                s.op("act", lambda e, sc0=sc0: e.activation(
                    stat[:, sc0 + 5:sc0 + 6], stat[:, sc0 + 4:sc0 + 5],
                    AF.Ln, bias=eps_sb[:, 0:1]),
                    wait=[("dve_" + tag, None), ("const_c", None)],
                    inc=["act_" + tag])
                s.op("act", lambda e, sc0=sc0: e.activation(
                    stat[:, sc0 + 3:sc0 + 4], stat[:, sc0 + 5:sc0 + 6],
                    AF.Exp, scale=-0.5),
                    inc=["act_" + tag])
                s.op("dve", None, wait=[("act_" + tag, None)])
                s.op("dve", lambda e, xc=xc, ht=ht_i, sc0=sc0: e.tensor_scalar(
                    ht[:], xc, stat[:, sc0 + 2:sc0 + 3], stat[:, sc0 + 3:sc0 + 4],
                    op0=ALU.subtract, op1=ALU.mult),
                    wait=[("dve_" + tag, None),
                          ("pe_" + tag, b_pe + max(0, 8 * (tc - 1)))],
                    inc=["dve_" + tag])
                hm = s.cnt["dve_" + tag]
                for dg in range(DCN // 4):
                    ps = ps_tr[dg % 2]
                    for j in range(4):
                        dc = dg * 4 + j
                        s.op("pe", lambda e, ht=ht_i, dc=dc, j=j, ps=ps: e.transpose(
                            ps[:, j * 128:(j + 1) * 128],
                            ht[:, dc * 128:(dc + 1) * 128], id_bf[:]),
                            wait=[("ld_const", 16), ("dve_" + tag, hm),
                                  ("ev_" + tag, max(0, s.cnt.get("ev_" + tag, 0) - 1))],
                            inc=(["pe_" + tag] if j == 3 else []))
                    pm = s.cnt["pe_" + tag]
                    dst = hT.ap().rearrange("p (d t) -> p d t", d=DCN)[
                        :, dg * 4:(dg + 1) * 4, tc * 128:(tc + 1) * 128]
                    src_ = ps.ap().rearrange("p (a b) -> p a b", a=4)
                    eng = "dve" if (dg % 2 == 0) else "act"
                    if eng == "dve":
                        s.op("dve", lambda e, dst=dst, src_=src_: e.tensor_copy(dst, src_),
                             wait=[("pe_" + tag, pm)], inc=["ev_" + tag])
                    else:
                        s.op("act", lambda e, dst=dst, src_=src_: e.activation(
                            dst, src_, AF.Copy),
                            wait=[("pe_" + tag, pm)], inc=["ev_" + tag])
            s.barrier()

        if DEBUG_STAGE >= 1:
          with contextlib.ExitStack() as pes:
            hid = [pes.enter_context(nc.sbuf_tensor(f"hidA{i}", [128, DM], BF16))
                   for i in range(2)]
            sqs = [pes.enter_context(nc.sbuf_tensor(f"sqA{i}", [128, DM], F32))
                   for i in range(2)]
            ps_tr = [pes.enter_context(nc.psum_tensor(f"psTrA{i}", [128, 512], BF16))
                     for i in range(2)]
            s.op("dve", None, wait=[("ld_x", 16)])
            s.op("act", None, wait=[("ld_x", 16)])
            phase_ln(x_sb, hid, ps_tr, sqs)

        if DEBUG_STAGE >= 2:
          with contextlib.ExitStack() as pes:
            w2t_sb = pes.enter_context(nc.sbuf_tensor("w2t_sb", [128, DCN * 512], BF16))
            w1b = [pes.enter_context(nc.sbuf_tensor(f"w1b{i}", [128, DCN * 256], BF16))
                   for i in range(2)]
            mkb = [pes.enter_context(nc.sbuf_tensor(f"mkb{i}", [128, 512], F32))
                   for i in range(2)]
            atile = [pes.enter_context(nc.sbuf_tensor(f"atile{i}", [128, 512], BF16))
                     for i in range(2)]
            ps_a = [pes.enter_context(nc.psum_tensor(f"psA{i}", [128, 512], F32))
                    for i in range(2)]

            s.op("pool", lambda e: e.dma_start(out=w2t_sb[:], in_=w2t),
                 inc=["ld_w2t"], dma=True)

            for scb in range(16):
                buf = w1b[scb % 2]
                s.op("pool", lambda e, buf=buf, scb=scb: e.dma_start(
                    out=buf[:], in_=w1[scb]),
                    wait=([("pe_aw", 2 * (scb - 1))] if scb >= 2 else []),
                    inc=["ld_w1"], dma=True)
                ldm = s.cnt["ld_w1"]
                for sc2 in range(2):
                    sc = scb * 2 + sc2
                    ps = ps_a[sc % 2]
                    mb = mkb[sc % 2]
                    at = atile[sc % 2]
                    s.op("sp", lambda e, mb=mb, sc=sc: e.dma_start(
                        out=mb[:], in_=maskc[sc * 128:(sc + 1) * 128, :]),
                        wait=[("ev_attn", max(0, sc - 1))],
                        inc=["ld_mk"], dma=True)
                    mkm = s.cnt["ld_mk"]
                    bufv = buf.ap().rearrange("p (c t) -> p c t", c=DCN)
                    for dc in range(DCN):
                        s.op("pe", lambda e, bufv=bufv, dc=dc, sc2=sc2, ps=ps: e.matmul(
                            ps[:], bufv[:, dc, sc2 * 128:(sc2 + 1) * 128],
                            w2t_sb[:, dc * 512:(dc + 1) * 512],
                            start=(dc == 0), stop=(dc == DCN - 1)),
                            wait=([("ld_w1", ldm), ("ld_w2t", 16),
                                   ("ev_attn", max(0, s.cnt.get("ev_attn", 0) - 1))]
                                  if dc == 0 else []),
                            inc=(["pe_aw"] if dc == DCN - 1 else []))
                    pem = s.cnt["pe_aw"]
                    s.op("dve", lambda e, ps=ps, mb=mb, at=at: e.tensor_tensor(
                        at[:], ps[:], mb[:], op=ALU.mult),
                        wait=[("pe_aw", pem), ("ld_mk", mkm),
                              ("st_attn", max(0, 16 * (sc - 1)))],
                        inc=["ev_attn"])
                    evm = s.cnt["ev_attn"]
                    s.op("sp", lambda e, at=at, sc=sc: e.dma_start(
                        out=attn_loc.ap()[:, sc * 512:(sc + 1) * 512], in_=at[:]),
                        wait=[("ev_attn", evm)], inc=["st_attn"], dma=True)
            s.op("pool", lambda e: e.collective_compute(
                "AllGather", ALU.bypass,
                replica_groups=[list(range(NCORE))],
                ins=[attn_loc.ap()], outs=[attn_all.ap()]),
                wait=[("st_attn", 16 * 32)], inc=[("cc", 1)])
            s.barrier()

        if DEBUG_STAGE >= 3:
          with contextlib.ExitStack() as pes:
            abuf = [pes.enter_context(nc.sbuf_tensor(f"ab{i}", [128, DCN * 512], BF16))
                    for i in range(2)]
            ps_y = [pes.enter_context(nc.psum_tensor(f"psY{i}", [128, 512], F32))
                    for i in range(2)]
            b2b = [pes.enter_context(nc.sbuf_tensor(f"b2b{i}", [128, 512], F32))
                   for i in range(2)]
            tmp1 = [pes.enter_context(nc.sbuf_tensor(f"tmpA{i}", [128, 512], F32))
                    for i in range(2)]
            tmp2 = [pes.enter_context(nc.sbuf_tensor(f"tmpB{i}", [128, 512], F32))
                    for i in range(2)]
            for r in range(NCORE):
                buf = abuf[r % 2]
                s.op("sp", lambda e, buf=buf, r=r: e.dma_start(
                    out=buf[:], in_=attn_all.ap()[r * 128:(r + 1) * 128, :]),
                    wait=[("cc", 1)] + ([("pe_y", (r - 1) * TCN)]
                                        if r >= 2 else []),
                    inc=["ld_at"], dma=True)
                ldm = s.cnt["ld_at"]
                bb = b2b[r % 2]
                s.op("sp", lambda e, bb=bb, r=r: e.dma_start(
                    out=bb[:],
                    in_=b2[0:1, r * 512:(r + 1) * 512].partition_broadcast(128)),
                    wait=[("ep_y", max(0, (r - 1) * TCN))],
                    inc=["b2r"], dma=True)
                b2m = s.cnt["b2r"]
                bufv = buf.ap().rearrange("p (c t) -> p c t", c=DCN)
                for tc in range(TCN):
                    ps = ps_y[tc % 2]
                    for dc in range(DCN):
                        s.op("pe", lambda e, bufv=bufv, dc=dc, tc=tc, ps=ps: e.matmul(
                            ps[:], hT[:, dc * TOK + tc * 128: dc * TOK + (tc + 1) * 128],
                            bufv[:, dc, :],
                            start=(dc == 0), stop=(dc == DCN - 1)),
                            wait=([("ld_at", ldm),
                                   ("ep_y", max(0, s.cnt.get("ep_y", 0) - 1))]
                                  if dc == 0 else []),
                            inc=(["pe_y"] if dc == DCN - 1 else []))
                    pem = s.cnt["pe_y"]
                    t1, t2 = tmp1[tc % 2], tmp2[tc % 2]
                    s.op("dve", lambda e, ps=ps, t1=t1, bb=bb: e.tensor_tensor(
                        t1[:], ps[:], bb[:], op=ALU.add),
                        wait=[("pe_y", pem), ("b2r", b2m),
                              ("ge_y", max(0, s.cnt.get("ge_y", 0) - 1))],
                        inc=["ep_y"])
                    epm = s.cnt["ep_y"]
                    s.op("act", lambda e, t1=t1, t2=t2: e.activation(
                        t2[:], t1[:], AF.Gelu),
                        wait=[("ep_y", epm),
                              ("rs_y", max(0, s.cnt.get("rs_y", 0) - 1))],
                        inc=["ge_y"])
                    gem = s.cnt["ge_y"]
                    xsl = x_sb[:, tc * DM + r * 512: tc * DM + (r + 1) * 512]
                    s.op("dve", lambda e, xsl=xsl, t2=t2: e.tensor_tensor(
                        xsl, xsl, t2[:], op=ALU.add),
                        wait=[("ge_y", gem)], inc=["rs_y"])
            s.barrier()

        def phase_ffn(fi, xt):
            t = "f_"
            tn = f"f{fi}_"
            b_rt = s.cnt.get(t + "rt", 0)
            b_la = s.cnt.get(t + "la", 0)
            b_lb = s.cnt.get(t + "lb", 0)
            b_lgev = s.cnt.get("lgev" + t, 0)
            b_evmt = s.cnt.get("ev_mt" + t, 0)
            b_peb = s.cnt.get("pe_b" + t, 0)
            with contextlib.ExitStack() as pes:
                hid = [pes.enter_context(
                    nc.sbuf_tensor(f"hid{tn}{i}", [128, DM], BF16)) for i in range(2)]
                sqs = [pes.enter_context(
                    nc.sbuf_tensor(f"sq{tn}{i}", [128, DM], F32)) for i in range(2)]
                ps_tr = [pes.enter_context(
                    nc.psum_tensor(f"psTr{tn}{i}", [128, 512], BF16)) for i in range(2)]
                phase_ln(xt, hid, ps_tr, sqs)
            if DEBUG_STAGE == 4 and fi == 0:
                s.op("pool", lambda e: e.dma_start(
                    out=dbg, in_=hT[:, 0:TCN * 128]),
                    inc=["st_dbg"], dma=True)
                s.op("pool", lambda e: e.dma_start(
                    out=dbgb[:, 0:48], in_=stat[:]),
                    inc=["st_dbg"], dma=True)
            with contextlib.ExitStack() as pes:
                la_sb = pes.enter_context(
                    nc.sbuf_tensor(tn + "la", [128, DCN * 128], BF16))
                lb_sb = pes.enter_context(
                    nc.sbuf_tensor(tn + "lb", [128, DM], BF16))
                rt_sb = pes.enter_context(
                    nc.sbuf_tensor(tn + "rt", [128, DCN * 128], BF16))
                midT = pes.enter_context(
                    nc.sbuf_tensor(tn + "midT", [128, TCN * 128], BF16))
                lg = pes.enter_context(nc.sbuf_tensor(tn + "lg", [128, TCN * 8], F32))
                gt = pes.enter_context(nc.sbuf_tensor(tn + "gt", [128, 8], F32))
                e1b = pes.enter_context(nc.sbuf_tensor(tn + "e1", [128, 8], F32))
                e2b = pes.enter_context(nc.sbuf_tensor(tn + "e2", [128, 8], F32))
                tmb = pes.enter_context(nc.sbuf_tensor(tn + "tm", [128, 8], F32))
                wde = pes.enter_context(nc.sbuf_tensor(tn + "wd", [128, TCN * 8], F32))
                mg = [pes.enter_context(
                    nc.sbuf_tensor(f"{tn}mg{i}", [128, 128], BF16)) for i in range(2)]
                bb = [pes.enter_context(
                    nc.sbuf_tensor(f"{tn}bb{i}", [128, DCN * 256], BF16))
                    for i in range(2)]
                ps_log = pes.enter_context(nc.psum_tensor(tn + "psL", [128, 64], F32))
                ps_mid = [pes.enter_context(
                    nc.psum_tensor(f"{tn}psM{i}", [128, 128], F32)) for i in range(2)]
                ps_mT = [pes.enter_context(
                    nc.psum_tensor(f"{tn}psMT{i}", [128, 128], BF16)) for i in range(2)]
                ps_b = [pes.enter_context(
                    nc.psum_tensor(f"{tn}psB{i}", [128, 256], F32)) for i in range(2)]

                LA, LB, RT, G = t + "la", t + "lb", t + "rt", t + "gg"
                s.op("pool", lambda e: e.dma_start(out=la_sb[:], in_=loraa_t),
                     inc=[LA], dma=True)
                s.op("pool", lambda e: e.dma_start(out=rt_sb[:], in_=routertp),
                     inc=[RT], dma=True)
                s.op("pool", lambda e: e.dma_start(out=lb_sb[:], in_=lorab_f),
                     inc=[LB], dma=True)

                for tc in range(TCN if DEBUG_STAGE >= 5 else 0):
                    pslg = ps_log[:, tc * 8:(tc + 1) * 8]
                    for dc in range(DCN):
                        s.op("pe", lambda e, dc=dc, tc=tc, pslg=pslg: e.matmul(
                            pslg, hT[:, dc * TOK + tc * 128: dc * TOK + (tc + 1) * 128],
                            rt_sb[:, dc * 128: dc * 128 + 8],
                            start=(dc == 0), stop=(dc == DCN - 1)),
                            wait=([(RT, b_rt + 16), ("lgev" + t, b_lgev + tc)]
                                  if dc == 0 else []),
                            inc=(["pe_lg" + t] if dc == DCN - 1 else []))
                    lgm = s.cnt["pe_lg" + t]
                    psm = ps_mid[tc % 2]
                    for dc in range(DCN):
                        s.op("pe", lambda e, dc=dc, tc=tc, psm=psm: e.matmul(
                            psm[:], hT[:, dc * TOK + tc * 128: dc * TOK + (tc + 1) * 128],
                            la_sb[:, dc * 128:(dc + 1) * 128],
                            start=(dc == 0), stop=(dc == DCN - 1)),
                            wait=([(LA, b_la + 16),
                                   ("em_" + t, max(0, s.cnt.get("em_" + t, 0) - 1))]
                                  if dc == 0 else []),
                            inc=(["pe_md" + t] if dc == DCN - 1 else []))
                    mdm = s.cnt["pe_md" + t]

                    def g(eng, fn):
                        s.op(eng, fn, wait=[(G, s.cnt.get(G, 0))], inc=[G])
                    lgc = lg[:, tc * 8:(tc + 1) * 8]
                    s.op("dve", lambda e, pslg=pslg, lgc=lgc: e.tensor_copy(lgc, pslg),
                         wait=[("pe_lg" + t, lgm), (G, s.cnt.get(G, 0))],
                         inc=["lgev" + t])
                    s.op("dve", lambda e, lgc=lgc: e.reduce_max(
                        gt[:, 0:1], lgc, axis=AX),
                        wait=[("lgev" + t, None)], inc=[G])
                    g("dve", lambda e, lgc=lgc: e.tensor_scalar(
                        e1b[:], lgc, gt[:, 0:1], None, op0=ALU.is_equal))
                    g("dve", lambda e, lgc=lgc: e.scalar_tensor_tensor(
                        tmb[:], e1b[:], -1e30, lgc, op0=ALU.mult, op1=ALU.add))
                    g("dve", lambda e: e.reduce_max(gt[:, 1:2], tmb[:], axis=AX))
                    g("dve", lambda e: e.tensor_scalar(
                        e2b[:], tmb[:], gt[:, 1:2], None, op0=ALU.is_equal))
                    g("dve", lambda e: e.tensor_tensor(
                        gt[:, 4:5], gt[:, 1:2], gt[:, 0:1], op=ALU.subtract))
                    g("act", lambda e: e.activation(gt[:, 3:4], gt[:, 4:5], AF.Sigmoid))
                    g("dve", lambda e: e.tensor_scalar(
                        gt[:, 2:3], gt[:, 3:4], -2.0, 2.0, op0=ALU.mult, op1=ALU.add))
                    g("dve", lambda e: e.tensor_scalar(
                        gt[:, 3:4], gt[:, 3:4], 2.0, None, op0=ALU.mult))
                    g("dve", lambda e: e.tensor_scalar(
                        e1b[:], e1b[:], gt[:, 2:3], None, op0=ALU.mult))
                    g("dve", lambda e: e.tensor_scalar(
                        e2b[:], e2b[:], gt[:, 3:4], None, op0=ALU.mult))
                    wdc = wde[:, tc * 8:(tc + 1) * 8]
                    g("dve", lambda e: e.tensor_tensor(
                        e1b[:], e1b[:], e2b[:], op=ALU.add))
                    g("dve", lambda e, wdc=wdc: e.tensor_copy(wdc, e1b[:]))
                    gl = s.cnt[G]

                    mgt = mg[tc % 2]
                    wrep = wde.ap()[:, tc * 8:(tc + 1) * 8].unsqueeze(2) \
                        .broadcast_to([128, 8, 16])
                    s.op("dve", lambda e, psm=psm, mgt=mgt, wrep=wrep: e.tensor_tensor(
                        mgt.ap().rearrange("p (e r) -> p e r", e=8),
                        psm.ap().rearrange("p (e r) -> p e r", e=8),
                        wrep, op=ALU.mult),
                        wait=[("pe_md" + t, mdm), (G, gl),
                              ("pe_mt" + t, max(0, s.cnt.get("pe_mt" + t, 0) - 1))],
                        inc=["em_" + t])
                    emm = s.cnt["em_" + t]
                    psl = ps_mT[tc % 2][:]
                    s.op("pe", lambda e, mgt=mgt, psl=psl: e.transpose(
                        psl, mgt[:], id_bf[:]),
                        wait=[("em_" + t, emm),
                              ("ev_mt" + t, max(0, s.cnt.get("ev_mt" + t, 0) - 1))],
                        inc=["pe_mt" + t])
                    ptm = s.cnt["pe_mt" + t]
                    s.op("act", lambda e, tc=tc, psl=psl: e.activation(
                        midT[:, tc * 128:(tc + 1) * 128], psl, AF.Copy),
                        wait=[("pe_mt" + t, ptm)], inc=["ev_mt" + t])

                if DEBUG_STAGE == 5 and fi == 0:
                    s.op("pool", lambda e: e.dma_start(out=dbg, in_=midT[:]),
                         wait=[("ev_mt" + t, TCN)], inc=["st_dbg"], dma=True)
                    s.op("pool", lambda e: e.dma_start(
                        out=dbgb[:, 0:64], in_=lg[:]),
                        wait=[("lgev" + t, TCN)], inc=["st_dbg"], dma=True)
                    s.op("pool", lambda e: e.dma_start(
                        out=dbgb[:, 64:128], in_=wde[:]),
                        wait=[(G, None)], inc=["st_dbg"], dma=True)

                for rb in range(16 if DEBUG_STAGE >= 6 else 0):
                    buf = bb[rb % 2]
                    bv = buf.ap().rearrange("p (c o) -> p c o", c=DCN)
                    s.op("pool", lambda e, buf=buf, rb=rb: e.dma_start(
                        out=buf[:], in_=base_wt[rb]),
                        wait=([("pe_b" + t, b_peb + 8 * (rb - 1))]
                              if rb >= 2 else []),
                        inc=["ld_b" + t], dma=True)
                    ldm = s.cnt["ld_b" + t]
                    use_inc = os.environ.get("KNOINC", "0") != "1"
                    for tc in range(TCN):
                        ps = ps_b[tc % 2]
                        for dc in range(DCN):
                            s.op("pe", lambda e, bv=bv, dc=dc, tc=tc, ps=ps: e.matmul(
                                ps[:], hT[:, dc * TOK + tc * 128: dc * TOK + (tc + 1) * 128],
                                bv[:, dc, :],
                                start=(dc == 0),
                                stop=(False if use_inc else (dc == DCN - 1))),
                                wait=([("ld_b" + t, ldm), (LB, b_lb + 16),
                                       ("ev_mt" + t,
                                        (b_evmt + TCN) if DEBUG_STAGE >= 5 else 0),
                                       ("rs_b" + t,
                                        max(0, s.cnt.get("rs_b" + t, 0) - 1))]
                                      if dc == 0 else []))
                        if use_inc:
                            s.op("pe", lambda e, tc=tc, rb=rb, ps=ps: e.matmul(
                                ps[:], midT[:, tc * 128:(tc + 1) * 128],
                                lb_sb[:, rb * 256:(rb + 1) * 256],
                                start=False, stop=True),
                                inc=["pe_b" + t])
                        else:
                            s.op("pe", lambda e: (e.drain(), e.nop())[1],
                                 inc=["pe_b" + t])
                        pbm = s.cnt["pe_b" + t]
                        xsl = xt[:, tc * DM + rb * 256: tc * DM + (rb + 1) * 256]
                        s.op("dve", lambda e, xsl=xsl, ps=ps: e.tensor_tensor(
                            xsl, xsl, ps[:], op=ALU.add),
                            wait=[("pe_b" + t, pbm)], inc=["rs_b" + t])
                s.barrier()

        if DEBUG_STAGE >= 4 and N_SUBLAYERS >= 2:
            phase_ffn(0, x_sb)

        def phase_attn_b_groups():
          with contextlib.ExitStack() as pes:
            T = pes.enter_context
            wqb = [T(nc.sbuf_tensor(f"b_wq{i}", [128, DCN * 128], BF16))
                   for i in range(2)]
            wkb = T(nc.sbuf_tensor("b_wk", [128, DCN * 128], BF16))
            wvb = T(nc.sbuf_tensor("b_wv", [128, DCN * 128], BF16))
            cq = T(nc.sbuf_tensor("b_cq", [128, TOK], BF16))
            sqt = T(nc.sbuf_tensor("b_sqt", [128, TOK], BF16))
            ck = T(nc.sbuf_tensor("b_ck", [128, TOK], BF16))
            skt = T(nc.sbuf_tensor("b_skt", [128, TOK], BF16))
            rot_bf = T(nc.sbuf_tensor("b_rot", [128, 128], BF16))
            swm_sb = T(nc.sbuf_tensor("b_swm", [128, 256], F32))
            swm0_sb = T(nc.sbuf_tensor("b_swm0", [128, 256], F32))
            tri_sb = T(nc.sbuf_tensor("b_tri", [128, 128], F32))
            gate_sb = T(nc.sbuf_tensor("b_gate", [128, 1], F32))
            ones128 = T(nc.sbuf_tensor("b_ones", [128, 128], F32))
            qx = T(nc.sbuf_tensor("b_qx", [128, 4 * TOK], BF16))
            qraw = T(nc.sbuf_tensor("b_qraw", [128, 4 * TOK], BF16))
            sq_sb = T(nc.sbuf_tensor("b_sq", [128, 4 * TOK], BF16))
            qtl = qx
            kraw = T(nc.sbuf_tensor("b_kraw", [128, TOK], BF16))
            kx = T(nc.sbuf_tensor("b_kx", [128, TOK], BF16))
            ktl = T(nc.sbuf_tensor("b_ktl", [128, TOK], BF16))
            khat = T(nc.sbuf_tensor("b_khat", [128, TOK], BF16))
            khtok = T(nc.sbuf_tensor("b_khtok", [128, TOK], BF16))
            sk_ext = T(nc.sbuf_tensor("b_skext", [128, 1152], BF16))
            v_ext = T(nc.sbuf_tensor("b_vext", [128, 1152], BF16))
            v_tok = T(nc.sbuf_tensor("b_vtok", [128, 8 * 128], BF16))
            v_win = T(nc.sbuf_tensor("b_vwin", [128, 9 * 128], BF16))
            g_sb = T(nc.sbuf_tensor("b_g", [128, TOK], F32))
            a_sb = T(nc.sbuf_tensor("b_a", [128, TOK], F32))
            eA = T(nc.sbuf_tensor("b_eA", [128, TOK], F32))
            eNA = T(nc.sbuf_tensor("b_eNA", [128, TOK], F32))
            tmpA = T(nc.sbuf_tensor("b_tmpA", [128, 512], F32))
            tmpB = T(nc.sbuf_tensor("b_tmpB", [128, 512], F32))
            tmpC = T(nc.sbuf_tensor("b_tmpC", [128, TOK], F32))
            zrow = T(nc.sbuf_tensor("b_zrow", [1, 1024], F32))
            zrow2 = T(nc.sbuf_tensor("b_zrow2", [1, 1024], F32))
            S_state = T(nc.sbuf_tensor("b_S", [128, 128], F32))
            S0st = T(nc.sbuf_tensor("b_S0", [128, 8 * 128], BF16))
            sprev = T(nc.sbuf_tensor("b_sprev", [128, 128], F32))
            exch_sb = T(nc.sbuf_tensor("b_exch", [128, 256], F32))
            aoff = T(nc.sbuf_tensor("b_aoff", [128, 16], F32))
            o_sb = T(nc.sbuf_tensor("b_o", [128, 4 * TOK], BF16))
            ePf = T(nc.sbuf_tensor("b_ePf", [128, 256], F32))
            ePm = T(nc.sbuf_tensor("b_ePm", [128, 256], F32))
            ePb = T(nc.sbuf_tensor("b_ePb", [128, 256], BF16))
            zsw = T(nc.sbuf_tensor("b_zsw", [128, 4], F32))
            PTb = [T(nc.sbuf_tensor(f"b_PT{i}", [128, 128], BF16)) for i in range(2)]
            ePT = [T(nc.sbuf_tensor(f"b_ePT{i}", [128, 256], BF16)) for i in range(2)]
            p0 = T(nc.psum_tensor("bp0", [128, 512], F32))
            p1 = T(nc.psum_tensor("bp1", [128, 512], F32))
            p2 = T(nc.psum_tensor("bp2", [128, 512], F32))
            p3 = T(nc.psum_tensor("bp3", [128, 512], F32))
            p4 = T(nc.psum_tensor("bp4", [128, 512], F32))
            p5 = T(nc.psum_tensor("bp5", [128, 512], F32))
            p6 = T(nc.psum_tensor("bp6", [128, 512], F32))
            p7 = T(nc.psum_tensor("bp7", [128, 512], BF16))

            for nm, dst, src in (("cq", cq, cosq), ("sqt", sqt, sinq),
                                 ("ck", ck, cosk), ("skt", skt, sink),
                                 ("rotb", rot_bf, rotm)):
                s.op("pool", lambda e, dst=dst, src=src: e.dma_start(
                    out=dst[:], in_=src), inc=["b_cst"], dma=True)
            for dst, src in ((swm_sb, swmask), (swm0_sb, swmask0),
                             (tri_sb, trimask), (gate_sb, gate)):
                s.op("sp", lambda e, dst=dst, src=src: e.dma_start(
                    out=dst[:], in_=src), inc=["b_cst"], dma=True)
            s.op("dve", lambda e: e.memset(ones128[:], 1.0), inc=["b_cst1"])
            CST = [("b_cst", 9 * 16), ("b_cst1", 1), ("const_c", None)]


            for gidx in range(KBGRP):
                tg = "b"
                oev_pg = s.cnt.get("b_oev", 0)
                swe_pg = s.cnt.get("b_swe", 0)
                s.op("pool", lambda e, gidx=gidx: e.dma_start(
                    out=wkb[:], in_=wk[gidx]),
                    wait=[("b_kp", s.cnt.get("b_kp", 0))],
                    inc=["b_ldk"], dma=True)
                s.op("pool", lambda e, gidx=gidx: e.dma_start(
                    out=wvb[:], in_=wv[gidx]),
                    wait=[("b_vp", s.cnt.get("b_vp", 0))],
                    inc=["b_ldv"], dma=True)
                ldk = s.cnt["b_ldk"]
                ldv = s.cnt["b_ldv"]
                s.op("dve", lambda e: e.memset(sk_ext[:, 1088:1152], 0.0),
                     wait=[("b_sc", s.cnt.get("b_sc", 0)),
                           ("b_tr", s.cnt.get("b_tr", 0)),
                           ("b_ex", s.cnt.get("b_ex", 0))], inc=["b_z0"])
                s.op("dve", lambda e: e.memset(v_ext[:, 1088:1152], 0.0),
                     inc=["b_z0"])
                for tt in range(2 if KBK >= 1 else 0):
                    ps = [p0, p1][tt]
                    for dc in range(DCN):
                        s.op("pe", lambda e, dc=dc, tt=tt, ps=ps: e.matmul(
                            ps[:], wkb[:, dc * 128:(dc + 1) * 128],
                            hT[:, dc * TOK + tt * 512: dc * TOK + tt * 512 + 512],
                            start=(dc == 0), stop=(dc == DCN - 1)),
                            wait=([("b_ldk", ldk), ("b_oev", oev_pg)]
                                  if dc == 0 else []),
                            inc=(["b_kp"] if dc == DCN - 1 else []))
                    kpm = s.cnt["b_kp"]
                    if os.environ.get("KNOEV", "0") == "1":
                        continue
                    s.op("dve", lambda e, tt=tt, ps=ps: e.tensor_copy(
                        kraw[:, tt * 512:(tt + 1) * 512], ps[:]),
                        wait=[("b_kp", kpm), ("b_g", s.cnt.get("b_g", 0))],
                        inc=["b_kev"])
                    s.op("act", lambda e, tt=tt: e.activation(
                        kx[:, tt * 512:(tt + 1) * 512],
                        kraw[:, tt * 512:(tt + 1) * 512], AF.Exp),
                        wait=[("b_kev", s.cnt.get("b_kev", 0)),
                              ("b_kt", s.cnt.get("b_kt", 0))],
                        inc=["b_kev"])
                for tt in range(2 if KBK >= 1 else 0):
                    ps = [p0, p1][tt]
                    for dc in range(DCN):
                        s.op("pe", lambda e, dc=dc, tt=tt, ps=ps: e.matmul(
                            ps[:], wvb[:, dc * 128:(dc + 1) * 128],
                            hT[:, dc * TOK + tt * 512: dc * TOK + tt * 512 + 512],
                            start=(dc == 0), stop=(dc == DCN - 1)),
                            wait=([("b_ldv", ldv), ("b_kev", s.cnt.get("b_kev", 0))]
                                  if dc == 0 else []),
                            inc=(["b_vp"] if dc == DCN - 1 else []))
                    vpm = s.cnt["b_vp"]
                    if os.environ.get("KNOEV", "0") == "1":
                        continue
                    s.op("dve", lambda e, tt=tt, ps=ps: e.tensor_copy(
                        v_ext[:, 64 + tt * 512: 64 + (tt + 1) * 512], ps[:]),
                        wait=[("b_vp", vpm), ("b_tr", s.cnt.get("b_tr", 0)),
                              ("b_ex", s.cnt.get("b_ex", 0))],
                        inc=["b_vev"])
                vev = s.cnt.get("b_vev", 0)
                qtms = [0, 0, 0, 0]
                rpems = [0, 0, 0, 0]
                for h in range(4 if KBSUB >= 2 else 0):
                    wqi = wqb[h % 2]
                    s.op("pool", lambda e, h=h, gidx=gidx, wqi=wqi: e.dma_start(
                        out=wqi[:], in_=wq[gidx * 4 + h]),
                        wait=[("b_qp", max(0, s.cnt.get("b_qp", 0) - 2))],
                        inc=["b_ldq"], dma=True)
                    ldq = s.cnt["b_ldq"]
                    for tt in range(2):
                        ps = [p0, p1][(h * 2 + tt) % 2]
                        for dc in range(DCN):
                            s.op("pe", lambda e, dc=dc, tt=tt, ps=ps, wqi=wqi:
                                 e.matmul(
                                ps[:], wqi[:, dc * 128:(dc + 1) * 128],
                                hT[:, dc * TOK + tt * 512: dc * TOK + tt * 512 + 512],
                                start=(dc == 0), stop=(dc == DCN - 1)),
                                wait=([("b_ldq", ldq),
                                       ("b_qt", s.cnt.get("b_qt", 0)),
                                       ("b_vev", vev),
                                       ("b_qev", max(0, s.cnt.get("b_qev", 0) - 2))]
                                      if dc == 0 else []),
                                inc=(["b_qp"] if dc == DCN - 1 else []))
                        qpm = s.cnt["b_qp"]
                        col = h * TOK + tt * 512
                        s.op("dve", lambda e, col=col, ps=ps: e.tensor_copy(
                            qraw[:, col:col + 512], ps[:]),
                            wait=[("b_qp", qpm)], inc=["b_qev"])
                        s.op("act", lambda e, col=col: e.activation(
                            qx[:, col:col + 512],
                            qraw[:, col:col + 512], AF.Exp),
                            wait=[("b_qev", s.cnt.get("b_qev", 0))],
                            inc=["b_qev"])
                    for tt in range(2):
                        col = h * TOK + tt * 512
                        s.op("pe", lambda e, col=col: e.matmul(
                            p2[:], rot_bf[:], qraw[:, col:col + 512],
                            start=True, stop=True),
                            wait=[("b_qev", s.cnt.get("b_qev", 0)),
                                  ("b_kt", s.cnt.get("b_kt", 0)),
                                  ("b_qt", s.cnt.get("b_qt", 0)),
                                  ("b_rpe", s.cnt.get("b_rpe", 0))],
                            inc=["b_rot"])
                        rm = s.cnt["b_rot"]
                        s.op("dve", lambda e, col=col, tt=tt: e.tensor_tensor(
                            tmpA[:, 0:512], qraw[:, col:col + 512],
                            cq[:, tt * 512:(tt + 1) * 512], op=ALU.mult),
                            wait=[("b_rpe2", s.cnt.get("b_rpe2", 0))],
                            inc=["b_rpe2"])
                        s.op("dve", lambda e, col=col, tt=tt: e.tensor_tensor(
                            tmpB[:, 0:512], p2[:, 0:512],
                            sqt[:, tt * 512:(tt + 1) * 512], op=ALU.mult),
                            wait=[("b_rot", rm)], inc=["b_rpe"])
                        s.op("dve", lambda e, col=col: e.tensor_tensor(
                            sq_sb[:, col:col + 512],
                            tmpA[:, 0:512], tmpB[:, 0:512], op=ALU.add),
                            wait=[("b_rpe", None), ("b_rpe2", None)],
                            inc=["b_rpe2"])
                    rpems[h] = s.cnt.get("b_rpe2", 0)

                for tt in range(2 if KBK >= 2 else 0):
                    s.op("pe", lambda e, tt=tt: e.matmul(
                        p2[:], rot_bf[:],
                        kraw[:, tt * 512:(tt + 1) * 512], start=True, stop=True),
                        wait=[("b_kev", s.cnt.get("b_kev", 0)),
                              ("b_rpe", s.cnt.get("b_rpe", 0))] + CST,
                        inc=["b_rot"])
                    rm = s.cnt["b_rot"]
                    s.op("pool", lambda e, tt=tt: e.tensor_tensor(
                        tmpA[:, 0:512], kraw[:, tt * 512:(tt + 1) * 512],
                        ck[:, tt * 512:(tt + 1) * 512], op=ALU.mult),
                        wait=[("b_rpe2", s.cnt.get("b_rpe2", 0)),
                              ("b_kev", s.cnt.get("b_kev", 0))], inc=["b_rpe2"])
                    s.op("dve", lambda e, tt=tt: e.tensor_tensor(
                        tmpB[:, 0:512], p2[:, 0:512],
                        skt[:, tt * 512:(tt + 1) * 512], op=ALU.mult),
                        wait=[("b_rot", rm)], inc=["b_rpe"])
                    s.op("dve", lambda e, tt=tt: e.tensor_tensor(
                        sk_ext[:, 64 + tt * 512: 64 + (tt + 1) * 512],
                        tmpA[:, 0:512], tmpB[:, 0:512], op=ALU.add),
                        wait=[("b_rpe", None), ("b_rpe2", None),
                              ("b_sc", s.cnt.get("b_sc", 0)),
                              ("b_ex", s.cnt.get("b_ex", 0))], inc=["b_rpe2"])
                krm = s.cnt.get("b_rpe2", 0)
                GS = "b_g"

                def gs(eng, fn):
                    if KBK >= 3:
                        s.op(eng, fn, wait=[(GS, s.cnt.get(GS, 0))], inc=[GS])
                if KBK >= 3:
                  s.op("act", lambda e: e.activation(tmpC[:], kraw[:], AF.Abs),
                     wait=[("b_kev", s.cnt.get("b_kev", 0)),
                           (GS, s.cnt.get(GS, 0)),
                           ("b_qt", s.cnt.get("b_qt", 0)),
                           ("b_chn", s.cnt.get("b_chn", 0)),
                           ("b_fold", s.cnt.get("b_fold", 0)),
                           ("b_kh", s.cnt.get("b_kh", 0))], inc=[GS])
                gs("act", lambda e: e.activation(a_sb[:], tmpC[:], AF.Exp, scale=-1.0))
                gs("act", lambda e: e.activation(tmpC[:], a_sb[:], AF.Ln, bias=1.0))
                gs("dve", lambda e: e.tensor_scalar(
                    a_sb[:], kraw[:], 0.0, None, op0=ALU.min))
                gs("dve", lambda e: e.tensor_tensor(
                    g_sb[:], a_sb[:], tmpC[:], op=ALU.subtract))
                for ch in range(8):
                    gs("dve", lambda e, ch=ch: e.tensor_tensor_scan(
                        a_sb[:, ch * 128:(ch + 1) * 128], ones128[:],
                        g_sb[:, ch * 128:(ch + 1) * 128], 0.0,
                        op0=ALU.mult, op1=ALU.add))
                gs("dve", lambda e: e.tensor_scalar(
                    a_sb[:], a_sb[:], -1280.0, None, op0=ALU.max))
                gs("act", lambda e: e.activation(eA[:], a_sb[:], AF.Exp,
                                                 scale=1.0 / 16.0))
                gs("act", lambda e: e.activation(eNA[:], a_sb[:], AF.Exp,
                                                 scale=-1.0 / 16.0))
                gs("dve", lambda e: e.memset(aoff[:, 0:1], 0.0))
                for ch in range(1, 8):
                    gs("dve", lambda e, ch=ch: e.tensor_tensor(
                        aoff[:, ch:ch + 1], aoff[:, ch - 1:ch],
                        a_sb[:, ch * 128 - 1:ch * 128], op=ALU.add))
                gs("act", lambda e: e.activation(aoff[:, 8:16], aoff[:, 0:8],
                                                 AF.Exp, scale=1.0 / 16.0))
                gsm = s.cnt.get(GS, 0)
                for tt in range(2 if KBK >= 4 else 0):
                    s.op("pe", lambda e, tt=tt: e.matmul(
                        p3[0:1, 0:512],
                        ones_bf[:], kx[:, tt * 512:(tt + 1) * 512],
                        start=True, stop=True),
                        wait=[("b_kev", s.cnt.get("b_kev", 0)),
                              ("b_zev", s.cnt.get("b_zev", 0)),
                              ("b_swa", s.cnt.get("b_swa", 0))],
                        inc=["b_zk"])
                    zkm = s.cnt["b_zk"]
                    s.op("dve", lambda e, tt=tt: e.tensor_copy(
                        zrow[0:1, tt * 512:(tt + 1) * 512], p3[0:1, 0:512]),
                        wait=[("b_zk", zkm)], inc=["b_zev"])
                if KBK >= 4:
                    s.op("act", lambda e: e.activation(
                        zrow2[0:1, :], zrow[0:1, :], AF.Ln),
                        wait=[("b_zev", s.cnt.get("b_zev", 0))], inc=["b_zev"])
                    s.op("act", lambda e: e.activation(
                        zrow2[0:1, :], zrow2[0:1, :], AF.Exp, scale=-1.0),
                        inc=["b_zev"])
                if KBK >= 5:
                    s.op("dve", lambda e: e.tensor_tensor(
                        tmpC[:], kx[:], eNA[:], op=ALU.mult),
                        wait=[(GS, gsm)], inc=["b_kt"])
                    for tt in range(2):
                        s.op("pe", lambda e, tt=tt: e.matmul(
                            p2[:], ones_row[:],
                            zrow2[0:1, tt * 512:(tt + 1) * 512],
                            start=True, stop=True),
                            wait=[("b_zev", s.cnt.get("b_zev", 0)),
                                  ("b_rpe", s.cnt.get("b_rpe", 0)),
                                  ("b_kt", s.cnt.get("b_kt", 0))],
                            inc=["b_rot"])
                        s.op("dve", lambda e, tt=tt: e.tensor_tensor(
                            ktl[:, tt * 512:(tt + 1) * 512],
                            tmpC[:, tt * 512:(tt + 1) * 512], p2[:, 0:512],
                            op=ALU.mult),
                            wait=[("b_rot", s.cnt.get("b_rot", 0)),
                                  ("b_P", s.cnt.get("b_P", 0))], inc=["b_kt"])
                ktm = s.cnt.get("b_kt", 0)
                for ch in range(8 if KBK >= 5 else 0):
                    s.op("dve", lambda e, ch=ch: e.tensor_scalar(
                        khat[:, ch * 128:(ch + 1) * 128],
                        ktl[:, ch * 128:(ch + 1) * 128],
                        eA[:, ch * 128 + 127: ch * 128 + 128], None, op0=ALU.mult),
                        wait=[("b_kt", ktm), ("b_tr", s.cnt.get("b_tr", 0))]
                        if ch == 0 else [("b_kt", ktm)],
                        inc=["b_kh"])
                khm = s.cnt.get("b_kh", 0)
                trjobs = []
                for ch in range(8):
                    trjobs.append((khat.ap()[:, ch * 128:(ch + 1) * 128],
                                   khtok.ap()[:, ch * 128:(ch + 1) * 128],
                                   ("b_kh", khm)))
                for ch in range(8):
                    trjobs.append((v_ext.ap()[:, 64 + ch * 128: 64 + (ch + 1) * 128],
                                   v_tok.ap()[:, ch * 128:(ch + 1) * 128],
                                   ("b_vev", vev)))
                for j in range(1, 9):
                    trjobs.append((v_ext.ap()[:, j * 128:(j + 1) * 128],
                                   v_win.ap()[:, j * 128:(j + 1) * 128],
                                   ("b_vev", vev)))
                for j0 in range(0, len(trjobs) if KBK >= 6 else 0, 4):
                    grp = trjobs[j0:j0 + 4]
                    for ji, (src, dst, wt) in enumerate(grp):
                        s.op("pe", lambda e, src=src, ji=ji: e.transpose(
                            p7[:, ji * 128:(ji + 1) * 128], src, id_bf[:]),
                            wait=[wt, ("b_tev", s.cnt.get("b_tev", 0))]
                            if ji == 0 else [wt],
                            inc=(["b_tr"] if ji == len(grp) - 1 else []))
                    trm = s.cnt["b_tr"]
                    for ji, (src, dst, wt) in enumerate(grp):
                        s.op("dve", lambda e, dst=dst, ji=ji: e.tensor_copy(
                            dst, p7[:, ji * 128:(ji + 1) * 128]),
                            wait=[("b_tr", trm)],
                            inc=(["b_tev"] if ji == len(grp) - 1 else []))
                tevm = s.cnt.get("b_tev", 0)
                if KBK >= 7:
                    s.op("dve", lambda e: e.memset(S_state[:], 0.0),
                         wait=[("b_chn", s.cnt.get("b_chn", 0)),
                               ("b_oacc", s.cnt.get("b_oacc", 0))], inc=["b_chn"])
                for ch in range(8 if KBK >= 7 else 0):
                    psD = [p5, p6][ch // 4]
                    sl = (ch % 4) * 128
                    s.op("pe", lambda e, ch=ch, psD=psD, sl=sl: e.matmul(
                        psD[:, sl:sl + 128], khtok[:, ch * 128:(ch + 1) * 128],
                        v_tok[:, ch * 128:(ch + 1) * 128],
                        start=True, stop=True),
                        wait=([("b_tev", tevm), ("b_Pe", s.cnt.get("b_Pe", 0)),
                               ("b_chn", s.cnt.get("b_chn", 0))]
                              if ch == 0 else []),
                        inc=(["b_dS"] if ch == 7 else []))
                dsm = s.cnt.get("b_dS", 0)
                for ch in range(8 if KBK >= 7 else 0):
                    psD = [p5, p6][ch // 4]
                    sl = (ch % 4) * 128
                    s.op("dve", lambda e, ch=ch: e.tensor_copy(
                        S0st[:, ch * 128:(ch + 1) * 128], S_state[:]),
                        wait=[("b_dS", dsm), ("b_chn", s.cnt.get("b_chn", 0)),
                              ("b_fold", s.cnt.get("b_fold", 0))], inc=["b_chn"])
                    s.op("dve", lambda e, ch=ch, psD=psD, sl=sl:
                         e.scalar_tensor_tensor(
                        S_state[:], S_state[:],
                        eA[:, ch * 128 + 127: ch * 128 + 128],
                        psD[:, sl:sl + 128], op0=ALU.mult, op1=ALU.add),
                        wait=[("b_chn", s.cnt.get("b_chn", 0))],
                        inc=["b_chn"])
                chm = s.cnt.get("b_chn", 0)
                if KBK >= 8:
                  s.op("pool", lambda e, gidx=gidx: e.dma_start(
                    out=exch_loc.ap()[gidx, :, 0:128], in_=S_state[:]),
                    wait=[("b_chn", chm)], inc=["b_ex"], dma=True)
                  s.op("pool", lambda e, gidx=gidx: e.dma_start(
                    out=exch_loc.ap()[gidx, :, 128:192],
                    in_=sk_ext[:, 1024:1088]),
                    wait=[("b_rpe2", None)], inc=["b_ex"], dma=True)
                  s.op("pool", lambda e, gidx=gidx: e.dma_start(
                    out=exch_loc.ap()[gidx, :, 192:256],
                    in_=v_ext[:, 1024:1088]),
                    wait=[("b_vev", vev)], inc=["b_ex"], dma=True)
                  s.op("pool", lambda e, gidx=gidx: e.collective_compute(
                    "AllGather", ALU.bypass,
                    replica_groups=[[0, 1], [2, 3], [4, 5], [6, 7]],
                    ins=[exch_loc.ap()[gidx]], outs=[exch_pair.ap()[gidx]]),
                    wait=[("b_ex", None)], inc=[("ccb", 1)])

                for h in range(4 if KBSUB >= 2 else 0):
                    for tt in range(2):
                        col = h * TOK + tt * 512
                        s.op("pe", lambda e, col=col, tt=tt: e.matmul(
                            p3[0:1, 0:512],
                            ones_bf[:], qx[:, col:col + 512],
                            start=True, stop=True),
                            wait=[("b_qev", s.cnt.get("b_qev", 0)),
                                  ("b_zev", s.cnt.get("b_zev", 0)),
                                  ("b_swa", s.cnt.get("b_swa", 0))],
                            inc=["b_zk"])
                        zkm = s.cnt["b_zk"]
                        s.op("dve", lambda e, tt=tt: e.tensor_copy(
                            zrow[0:1, tt * 512:(tt + 1) * 512], p3[0:1, 0:512]),
                            wait=[("b_zk", zkm)], inc=["b_zev"])
                    s.op("act", lambda e: e.activation(
                        zrow2[0:1, :], zrow[0:1, :], AF.Ln),
                        wait=[("b_zev", s.cnt.get("b_zev", 0))], inc=["b_zev"])
                    s.op("act", lambda e: e.activation(
                        zrow2[0:1, :], zrow2[0:1, :], AF.Exp, scale=-1.0,
                        bias=lnh_sb[0:1, 0:1]),
                        inc=["b_zev"])
                    s.op("dve", lambda e, h=h: e.tensor_tensor(
                        tmpC[:], qx[:, h * TOK:(h + 1) * TOK], eA[:], op=ALU.mult),
                        wait=[(GS, gsm), ("b_kt", ktm),
                              ("b_qt", s.cnt.get("b_qt", 0))],
                        inc=["b_qt"])
                    for tt in range(2):
                        s.op("pe", lambda e, tt=tt: e.matmul(
                            p2[:], ones_row[:],
                            zrow2[0:1, tt * 512:(tt + 1) * 512],
                            start=True, stop=True),
                            wait=[("b_zev", s.cnt.get("b_zev", 0)),
                                  ("b_rpe", s.cnt.get("b_rpe", 0)),
                                  ("b_qt", s.cnt.get("b_qt", 0))],
                            inc=["b_rot"])
                        s.op("dve", lambda e, h=h, tt=tt: e.tensor_tensor(
                            qtl[:, h * TOK + tt * 512: h * TOK + (tt + 1) * 512],
                            tmpC[:, tt * 512:(tt + 1) * 512], p2[:, 0:512],
                            op=ALU.mult),
                            wait=[("b_rot", s.cnt.get("b_rot", 0)),
                                  ("b_zk", s.cnt.get("b_zk", 0)),
                                  ("b_P", s.cnt.get("b_P", 0))], inc=["b_qt"])
                    qtms[h] = s.cnt.get("b_qt", 0)
                qtm = s.cnt.get("b_qt", 0)

                if KBK >= 8:
                  s.op("sp", lambda e, gidx=gidx: e.dma_start(
                    out=exch_sb[:], in_=exch_pair.ap()[gidx][0:128, :]),
                    wait=[("ccb", gidx + 1),
                          ("b_fold", s.cnt.get("b_fold", 0))],
                    inc=["b_exl"], dma=True)
                exm = s.cnt.get("b_exl", 0)
                s.op("dve", lambda e: e.tensor_scalar(
                    sprev[:], exch_sb[:, 0:128], gate_sb[:, 0:1], None,
                    op0=ALU.mult),
                    wait=[("b_exl", exm)] + CST, inc=["b_fold"])
                s.op("dve", lambda e: e.tensor_scalar(
                    sk_ext[:, 0:64], exch_sb[:, 128:192], gate_sb[:, 0:1], None,
                    op0=ALU.mult),
                    wait=[("b_exl", exm), ("b_sc", s.cnt.get("b_sc", 0))],
                    inc=["b_fold"])
                s.op("dve", lambda e: e.tensor_scalar(
                    v_ext[:, 0:64], exch_sb[:, 192:256], gate_sb[:, 0:1], None,
                    op0=ALU.mult),
                    wait=[("b_exl", exm)], inc=["b_fold"])
                fm = s.cnt["b_fold"]
                s.op("pe", lambda e: e.transpose(
                    p7[:, 0:128], v_ext[:, 0:128], id_bf[:]),
                    wait=[("b_fold", fm), ("b_tev", tevm)], inc=["b_tr"])
                s.op("dve", lambda e: e.tensor_copy(v_win[:, 0:128], p7[:, 0:128]),
                     wait=[("b_tr", s.cnt["b_tr"])], inc=["b_tev"])
                for ch in range(8):
                    s.op("dve", lambda e, ch=ch: e.scalar_tensor_tensor(
                        S0st[:, ch * 128:(ch + 1) * 128], sprev[:],
                        aoff[:, 8 + ch: 9 + ch],
                        S0st[:, ch * 128:(ch + 1) * 128],
                        op0=ALU.mult, op1=ALU.add),
                        wait=[("b_fold", fm), ("b_chn", chm),
                              (GS, gsm)], inc=["b_fold"])
                foldm = s.cnt["b_fold"]

                NU = 32 if KBSUB >= 3 else 0
                pms = {}
                scms = {}

                def emit_A(u):
                    h, ch = divmod(u, 8)
                    qcol = h * TOK + ch * 128
                    psP = [p5, p6][u % 2]
                    psS = [p4, p3][u % 2]
                    s.op("pe", lambda e, qcol=qcol, ch=ch, psP=psP: e.matmul(
                        psP[:, 0:128], ktl[:, ch * 128:(ch + 1) * 128],
                        qtl[:, qcol:qcol + 128], start=True, stop=True),
                        wait=[("b_qt", qtms[h]), ("b_kt", ktm),
                              ("b_chn", chm),
                              ("b_Pe", s.cnt.get("b_Pe", 0))],
                        inc=["b_P"])
                    pms[u] = s.cnt["b_P"]
                    s.op("pe", lambda e, qcol=qcol, ch=ch, psS=psS: e.matmul(
                        psS[:, 0:256], sq_sb[:, qcol:qcol + 128],
                        sk_ext[:, ch * 128: ch * 128 + 256],
                        start=True, stop=True),
                        wait=[("b_rpe2", krm), ("b_fold", foldm),
                              ("b_zev", s.cnt.get("b_zev", 0)),
                              ("b_swa", s.cnt.get("b_swa", 0))],
                        inc=["b_sc"])
                    scms[u] = s.cnt["b_sc"]

                def emit_B(u):
                    h, ch = divmod(u, 8)
                    qcol = h * TOK + ch * 128
                    psP = [p5, p6][u % 2]
                    psS = [p4, p3][u % 2]
                    psO = [p0, p1][u % 2]
                    PT = PTb[u % 2]
                    eT = ePT[u % 2]
                    s.op("dve", lambda e, psP=psP, PT=PT: e.tensor_tensor(
                        PT[:], psP[:, 0:128], tri_sb[:], op=ALU.mult),
                        wait=[("b_P", pms[u]),
                              ("b_oacc", max(0, s.cnt.get("b_oacc", 0) - 1))],
                        inc=["b_Pe"])
                    pem = s.cnt["b_Pe"]
                    s.op("act", lambda e, psS=psS: e.activation(
                        ePf[:, 0:256], psS[:, 0:256], AF.Exp),
                        wait=[("b_sc", scms[u]),
                              ("b_swe", max(0, s.cnt.get("b_swe", 0) - 3))],
                        inc=["b_swa"])
                    swam = s.cnt["b_swa"]
                    msk = swm0_sb if ch == 0 else swm_sb
                    s.op("dve", lambda e, msk=msk: e.tensor_tensor(
                        ePm[:, 0:256], ePf[:, 0:256], msk[:], op=ALU.mult),
                        wait=[("b_swa", swam)] + CST, inc=["b_swe"])
                    s.op("dve", lambda e: e.reduce_sum(
                        zsw[:, 0:1], ePm[:, 0:256], axis=AX),
                        wait=[("b_swe", None)], inc=["b_swe"])
                    s.op("act", lambda e: e.activation(
                        zsw[:, 1:2], zsw[:, 0:1], AF.Ln),
                        wait=[("b_swe", s.cnt.get("b_swe", 0))], inc=["b_swz"])
                    s.op("act", lambda e: e.activation(
                        zsw[:, 2:3], zsw[:, 1:2], AF.Exp, scale=-1.0,
                        bias=lnh_sb[:, 0:1]),
                        inc=["b_swz"])
                    s.op("dve", lambda e: e.tensor_scalar(
                        ePb[:, 0:256], ePm[:, 0:256], zsw[:, 2:3], None,
                        op0=ALU.mult),
                        wait=[("b_swz", s.cnt.get("b_swz", 0)),
                              ("b_tr", s.cnt.get("b_tr", 0))],
                        inc=["b_swe"])
                    swem = s.cnt["b_swe"]
                    for ji in range(2):
                        s.op("pe", lambda e, ji=ji: e.transpose(
                            p7[:, ji * 128:(ji + 1) * 128],
                            ePb[:, ji * 128:(ji + 1) * 128], id_bf[:]),
                            wait=[("b_swe", swem),
                                  ("b_tev", s.cnt.get("b_tev", 0))]
                            if ji == 0 else [],
                            inc=(["b_tr"] if ji == 1 else []))
                    s.op("dve", lambda e, eT=eT: e.tensor_copy(
                        eT[:, 0:256], p7[:, 0:256]),
                        wait=[("b_tr", s.cnt["b_tr"]),
                              ("b_oacc", max(0, s.cnt.get("b_oacc", 0) - 1))],
                        inc=["b_tev"])
                    tevu = s.cnt["b_tev"]
                    s.op("pe", lambda e, ch=ch, qcol=qcol, psO=psO: e.matmul(
                        psO[:, 0:128], S0st[:, ch * 128:(ch + 1) * 128],
                        qtl[:, qcol:qcol + 128], start=True, stop=False),
                        wait=[("b_fold", foldm), ("b_qt", qtms[h]),
                              ("b_qev", s.cnt.get("b_qev", 0)),
                              ("b_oev", max(0, s.cnt.get("b_oev", 0) - 1))])
                    s.op("pe", lambda e, ch=ch, psO=psO, PT=PT: e.matmul(
                        psO[:, 0:128], v_tok[:, ch * 128:(ch + 1) * 128],
                        PT[:], start=False, stop=False),
                        wait=[("b_Pe", pem)])
                    s.op("pe", lambda e, ch=ch, psO=psO, eT=eT: e.matmul(
                        psO[:, 0:128], v_win[:, ch * 128:(ch + 1) * 128],
                        eT[:, 0:128], start=False, stop=False),
                        wait=[("b_tev", tevu)])
                    s.op("pe", lambda e, ch=ch, psO=psO, eT=eT: e.matmul(
                        psO[:, 0:128], v_win[:, (ch + 1) * 128:(ch + 2) * 128],
                        eT[:, 128:256], start=False, stop=True),
                        inc=["b_oacc"])
                    oam = s.cnt["b_oacc"]
                    s.op("act", lambda e, qcol=qcol, psO=psO: e.activation(
                        o_sb[:, qcol:qcol + 128], psO[:, 0:128], AF.Copy),
                        wait=[("b_oacc", oam), ("b_ost", 16 * gidx)],
                        inc=["b_oev"])

                if NU:
                    emit_A(0)
                    for u in range(NU):
                        if u + 1 < NU:
                            emit_A(u + 1)
                        emit_B(u)
                s.op("sp", lambda e, gidx=gidx: e.dma_start(
                    out=o_dram.ap()[gidx], in_=o_sb[:]),
                    wait=[("b_oev", None)], inc=["b_ost"], dma=True)
            s.barrier()

        def phase_wo():
          with contextlib.ExitStack() as pes:
            T = pes.enter_context
            o_all = T(nc.sbuf_tensor("w_oall", [128, DCN * TOK], BF16))
            wob = [T(nc.sbuf_tensor(f"w_wo{i}", [128, DCN * 256], BF16))
                   for i in range(2)]
            xb = [T(nc.sbuf_tensor(f"w_xb{i}", [128, TCN * 256], BF16))
                  for i in range(2)]
            xo = [T(nc.sbuf_tensor(f"w_xo{i}", [128, TCN * 256], BF16))
                  for i in range(2)]
            ps_w = [T(nc.psum_tensor(f"w_ps{i}", [128, 256], F32))
                    for i in range(2)]
            for gidx in range(8):
                s.op("pool", lambda e, gidx=gidx: e.dma_start(
                    out=o_all[:, gidx * 4096:(gidx + 1) * 4096],
                    in_=o_dram.ap()[gidx]),
                    wait=[("b_ost", 16 * min(gidx + 1, KBGRP))],
                    inc=["w_ldo"], dma=True)
            ldo = s.cnt["w_ldo"]
            xsv = x_spill.ap().rearrange("p (c f) -> p c f", c=TCN)
            x2v = x2_dram.ap().rearrange("p (c f) -> p c f", c=TCN)
            for rb in range(16):
                buf = wob[rb % 2]
                bv = buf.ap().rearrange("p (c o) -> p c o", c=DCN)
                s.op("pool", lambda e, buf=buf, rb=rb: e.dma_start(
                    out=buf[:], in_=wo[rb]),
                    wait=([("w_pe", 8 * (rb - 1))] if rb >= 2 else []),
                    inc=["w_ldw"], dma=True)
                ldw = s.cnt["w_ldw"]
                xbi = xb[rb % 2]
                s.op("sp", lambda e, xbi=xbi, rb=rb: e.dma_start(
                    out=xbi.ap().rearrange("p (c f) -> p c f", c=TCN),
                    in_=xsv[:, :, rb * 256:(rb + 1) * 256]),
                    wait=[("st_sp", 16)] + ([("w_ev", 8 * (rb - 1))]
                                            if rb >= 2 else []),
                    inc=["w_ldx"], dma=True)
                ldx = s.cnt["w_ldx"]
                xoi = xo[rb % 2]
                for tc in range(TCN):
                    ps = ps_w[tc % 2]
                    for dc in range(DCN):
                        s.op("pe", lambda e, bv=bv, dc=dc, tc=tc, ps=ps: e.matmul(
                            ps[:], o_all[:, dc * TOK + tc * 128: dc * TOK + (tc + 1) * 128],
                            bv[:, dc, :],
                            start=(dc == 0), stop=(dc == DCN - 1)),
                            wait=([("w_ldw", ldw), ("w_ldo", ldo),
                                   ("w_ev", max(0, s.cnt.get("w_ev", 0) - 1))]
                                  if dc == 0 else []),
                            inc=(["w_pe"] if dc == DCN - 1 else []))
                    wpm = s.cnt["w_pe"]
                    s.op("dve", lambda e, xbi=xbi, xoi=xoi, tc=tc, ps=ps:
                         e.tensor_tensor(
                        xoi[:, tc * 256:(tc + 1) * 256],
                        xbi[:, tc * 256:(tc + 1) * 256], ps[:], op=ALU.add),
                        wait=[("w_pe", wpm), ("w_ldx", ldx),
                              ("st_x2", max(0, 16 * (rb - 1)))],
                        inc=["w_ev"])
                s.op("sp", lambda e, xoi=xoi, rb=rb: e.dma_start(
                    out=x2v[:, :, rb * 256:(rb + 1) * 256],
                    in_=xoi.ap().rearrange("p (c f) -> p c f", c=TCN)),
                    wait=[("w_ev", None)], inc=["st_x2"], dma=True)
            s.barrier()

        run_b = DEBUG_STAGE >= 7 and N_SUBLAYERS >= 3
        x_final = x_sb
        if run_b:
            with contextlib.ExitStack() as pes:
                hid = [pes.enter_context(
                    nc.sbuf_tensor(f"hidB{i}", [128, DM], BF16)) for i in range(2)]
                sqs = [pes.enter_context(
                    nc.sbuf_tensor(f"sqB{i}", [128, DM], F32)) for i in range(2)]
                ps_tr = [pes.enter_context(
                    nc.psum_tensor(f"psTrB{i}", [128, 512], BF16)) for i in range(2)]
                phase_ln(x_sb, hid, ps_tr, sqs)
            s.op("sp", lambda e: e.dma_start(out=x_spill[:, :], in_=x_sb[:]),
                 inc=["st_sp"], dma=True)
            for e_ in Sched.ENG:
                s.op(e_, None, wait=[("st_sp", 16)])
            s1.close()

            phase_attn_b_groups()
            phase_wo()

            with contextlib.ExitStack() as pes:
                x2_sb = pes.enter_context(
                    nc.sbuf_tensor("x2_sb", [128, TCN * DM], BF16))
                s.op("sp", lambda e: e.dma_start(out=x2_sb[:], in_=x2_dram[:, :]),
                     wait=[("st_x2", 16 * 16)], inc=["ld_x2"], dma=True)
                for e_ in Sched.ENG:
                    s.op(e_, None, wait=[("ld_x2", 16)])
                if DEBUG_STAGE >= 8 and N_SUBLAYERS >= 4:
                    phase_ffn(1, x2_sb)
                s.op("pool", lambda e: e.dma_start(
                    out=out.rearrange("(c p) f -> p c f", p=128),
                    in_=x2_sb.ap().rearrange("p (c f) -> p c f", c=TCN)),
                    inc=["st_out"], dma=True)
                if s.cnt.get("st_dbg", 0) == 0:
                    s.op("pool", lambda e: e.dma_start(
                        out=dbg, in_=x2_sb[:, 0:TCN * 128]),
                        inc=["st_dbg"], dma=True)
                    s.op("pool", lambda e: e.dma_start(
                        out=dbgb[:, :], in_=x2_sb[:, 0:128]),
                        inc=["st_dbg"], dma=True)
                s.op("sp", None, wait=[("st_out", None), ("st_dbg", None)])
        else:
            s.op("pool", lambda e: e.dma_start(
                out=out.rearrange("(c p) f -> p c f", p=128),
                in_=x_sb.ap().rearrange("p (c f) -> p c f", c=TCN)),
                inc=["st_out"], dma=True)
            if s.cnt.get("st_dbg", 0) == 0:
                s.op("pool", lambda e: e.dma_start(out=dbg, in_=x_sb[:, 0:TCN * 128]),
                     inc=["st_dbg"], dma=True)
                s.op("pool", lambda e: e.dma_start(
                    out=dbgb[:, :], in_=x_sb[:, 0:128]),
                    inc=["st_dbg"], dma=True)
            s.op("sp", None, wait=[("st_out", None), ("st_dbg", None)])
            s1.close()

        with nc.Block() as block:
            block.tensor(lambda e: [f(e) for f in s.q["pe"]])
            block.vector(lambda e: [f(e) for f in s.q["dve"]])
            block.scalar(lambda e: [f(e) for f in s.q["act"]])
            block.gpsimd(lambda e: [f(e) for f in s.q["pool"]])
            block.sync(lambda e: [f(e) for f in s.q["sp"]])

    return nc


def _host_consts(core):
    ident = np.eye(128, dtype=np.float32)
    rot = np.zeros((128, 128), dtype=np.float32)
    for dp in range(64):
        rot[dp + 64, dp] = -1.0
        rot[dp, dp + 64] = 1.0
    pos = (core % 2) * 1024 + np.arange(TOK, dtype=np.float64)
    inv = 1.0 / (10000.0 ** (2.0 * np.arange(64, dtype=np.float64) / 128.0))
    fr = pos[None, :] * inv[:, None]
    emb = np.concatenate([fr, fr], axis=0)
    cos = np.cos(emb).astype(np.float32)
    sin = np.sin(emb).astype(np.float32)
    scl = 1.0 / np.sqrt(128.0)
    tri = np.triu(np.ones((128, 128), dtype=np.float32))
    i = np.arange(128)[:, None]
    j = np.arange(256)[None, :]
    kpos = j - 64
    m = (kpos <= i) & (kpos >= i - 64)
    sw = m.astype(np.float32)
    sw0 = sw.copy()
    if core % 2 == 0:
        sw0[:64, :64] = 0.0
    gate = np.full((128, 1), 1.0 if core % 2 == 1 else 0.0, dtype=np.float32)
    d0 = np.ones((128, TOK), dtype=np.float32)
    d0[:, ::128] = 0.0
    return dict(
        identf=ident, rotm=rot,
        cosq=(cos * scl).astype(np.float32), sinq=(sin * scl).astype(np.float32),
        cosk=cos, sink=sin,
        trimask=tri, swmask=sw, swmask0=sw0, gate=gate, d0mask=d0,
    )


def _colblocks(W, width):
    IN, OUT = W.shape
    nb = OUT // width
    return np.ascontiguousarray(
        W.reshape(IN // 128, 128, nb, width).transpose(2, 1, 0, 3)
        .reshape(nb, 128, (IN // 128) * width))


def kernel(L0_attn_a_x, L0_attn_a_w1, L0_attn_a_w2, L0_attn_a_b2,
           L0_attn_a_sparse_mask, L0_attn_b_Wq, L0_attn_b_Wk, L0_attn_b_Wv,
           L0_attn_b_Wo, L0_ffn_base_W, L0_ffn_router_W, L0_ffn_lora_A,
           L0_ffn_lora_B):
    x = np.ascontiguousarray(np.asarray(L0_attn_a_x).reshape(NCORE * TOK, DM))
    w1 = _colblocks(np.asarray(L0_attn_a_w1), 256)
    w2t_full = np.asarray(L0_attn_a_w2).T
    mask = np.ascontiguousarray(L0_attn_a_sparse_mask)
    b2 = np.ascontiguousarray(np.asarray(L0_attn_a_b2).reshape(1, DM))
    base_wt = _colblocks(np.asarray(L0_ffn_base_W).T, 256)
    loraa_t = _colblocks(
        np.asarray(L0_ffn_lora_A).transpose(1, 0, 2).reshape(DM, 128), 128)[0]
    lorab_f = np.ascontiguousarray(np.asarray(L0_ffn_lora_B).reshape(128, DM))
    rtp = np.zeros((DM, 128), dtype=np.float32)
    rtp[:, :8] = np.asarray(L0_ffn_router_W).T
    routertp = _colblocks(rtp, 128)[0]
    wq = _colblocks(np.asarray(L0_attn_b_Wq), 128)
    wk = _colblocks(np.asarray(L0_attn_b_Wk), 128)
    wv = _colblocks(np.asarray(L0_attn_b_Wv), 128)
    wo = _colblocks(np.asarray(L0_attn_b_Wo), 256)

    nc = build_program()
    in_maps = []
    for c in range(NCORE):
        m = dict(
            x=np.ascontiguousarray(x[c * TOK:(c + 1) * TOK]),
            w1=w1,
            w2t=_colblocks(
                np.ascontiguousarray(w2t_full[:, c * 512:(c + 1) * 512]), 512)[0],
            maskc=np.ascontiguousarray(mask[:, c * 512:(c + 1) * 512]),
            b2=b2, base_wt=base_wt, loraa_t=loraa_t, lorab_f=lorab_f,
            routertp=routertp, wq=wq, wk=wk, wv=wv, wo=wo,
        )
        m.update(_host_consts(c))
        in_maps.append(m)
    res = run_bass_kernel_spmd(nc, in_maps, core_ids=list(range(NCORE)))
    global LAST_RES
    LAST_RES = res
    outs = [res.results[c]["out"] for c in range(NCORE)]
    y = np.concatenate(outs, axis=0).reshape(4, 2048, DM).astype(np.float32)
    return y
